# version 08
# baseline (speedup 1.0000x reference)
import contextlib
import numpy as np
import concourse.bass as bass
import concourse.mybir as mybir
from concourse.bass_utils import run_bass_kernel_spmd

F32 = mybir.dt.float32
BF16 = mybir.dt.bfloat16
I32 = mybir.dt.int32
AF = mybir.ActivationFunctionType
ALU = mybir.AluOpType
AX = mybir.AxisListType

D = 1024
T = 8192
HN = 16
C = 128
NCH = T // C
DC = D // 128
FF = 4096
II = 2816
IC = II // 128
EXPB = 22.0
E2B = float(np.exp(2 * EXPB))
NT5 = T // 512



def _finish(nc, blockctx, stack, E, semh, cnt):
    def fin(sync):
        sync.wait_ge(semh["st"], cnt["st"])
        sync.wait_ge(semh["ld"], cnt["ld"])
    E("sync", fin)
    blockctx.__exit__(None, None, None)
    stack.close()
    return nc


def build():
    nc = bass.Bass("TRN2", target_bir_lowering=False, detect_race_conditions=False)

    def din(name, shape, dtype=F32):
        return nc.dram_tensor(name, shape, dtype, kind="ExternalInput").ap()

    x_h = din("x", [T, D])
    cvec_h = din("cvec", [128, 8])
    clip_h = din("clip2", [1, 2])
    wmod_h = din("wmod", [D, 6 * D])
    winw_h = din("norm_in_w", [128, 1024])
    convw_h = din("conv_w", [128, 32])
    convb_h = din("conv_b", [128, 8])
    qw_h = din("q_w", [D, D]); kw_h = din("k_w", [D, D]); vw_h = din("v_w", [D, D])
    aw_h = din("alpha_w", [D, D]); ab_h = din("alpha_b", [128, 1024])
    bw_h = din("beta_w", [D, HN]); bb_h = din("beta_b", [128, HN])
    hnw_h = din("head_norm_w", [64, HN])
    u1w_h = din("u1_w", [D, 64]); u1b_h = din("u1_b", [64, 1])
    u2w_h = din("u2_w", [64, D]); u2b_h = din("u2_b", [128, 8])
    ffnw_h = din("ff_norm_w", [128, 1024])
    fw1_h = din("ff_w1", [D, FF]); fw3_h = din("ff_w3", [D, FF]); fw2_h = din("ff_w2", [FF, D])
    outw_h = din("out_w", [D, D])
    wgT_h = din("wgT", [D, II]); wuT_h = din("wuT", [D, II]); wdT_h = din("wdT", [II, D])
    ident_h = din("ident", [128, 128])
    csrel_h = din("csrel", [128, 128])
    cscols_h = din("cscols", [128, 3])
    mxu_h = din("msk_xu", [128, 128])
    mxl_h = din("msk_xl", [128, 128])
    mm2_h = din("msk_m2", [128, 128])
    hsel_h = din("hsel", [128, 128])
    hselT_h = din("hselT", [16, 8 * 128])
    ones1_h = din("ones1", [1, 128])
    ones128_h = din("ones128", [128, 128])
    o64c_h = din("ones64c", [64, 1])
    cpool_h = din("cpool", [128, 8])

    out_h = nc.dram_tensor("out", [T, D], F32, kind="ExternalOutput").ap()

    import os as _os

    def scr(name, shape, dtype=F32):
        if name in _os.environ.get("KSPO", "").split(","):
            return nc.dram_tensor(name, shape, dtype, kind="ExternalOutput").ap()
        return nc.dram_tensor(name, shape, dtype).ap()

    sp_h1 = scr("sp_h1", [128, NCH * 1024])
    sp_hc = scr("sp_hc", [128, NCH * 1024])
    sp_sg = scr("sp_sg", [64, T])
    sp_on = scr("sp_on", [64, NCH * 2048])
    sp_qf = scr("sp_qf", [128, NCH * 1024])
    sp_kf = scr("sp_kf", [128, NCH * 1024])
    sp_vt = scr("sp_vt", [128, NCH * 1024])
    sp_sp = scr("sp_sp", [128, NCH * 1024])
    sp_bt = scr("sp_bt", [128, NCH * 16])
    sp_y1 = scr("sp_y1", [128, NCH * 1024])
    sp_y2 = scr("sp_y2", [128, NCH * 1024])
    sp_fp = [scr(f"sp_fp{g}", [128, NCH * 1024]) for g in range(4)]
    sp_xm = scr("sp_xm", [T, D])
    sp_gu = scr("sp_gu", [128, IC * T])
    sp_wgq = scr("sp_wgq", [128, DC * II], BF16)
    sp_wuq = scr("sp_wuq", [128, DC * II], BF16)
    sp_wdq = scr("sp_wdq", [128, IC * D], BF16)
    ccb = [scr(f"ccb{i}", [1, 1]) for i in range(4)]
    sp_row = scr("sp_row", [1, 4096])
    DEBUG_SCAN = _os.environ.get("KDBG", "0") == "1"
    KSTOP = int(_os.environ.get("KSTOP", "99"))
    KS0 = int(_os.environ.get("KS0", "99"))
    sp_dbg = scr("sp_dbg", [128, 16384]) if DEBUG_SCAN else None
    sp_mod = scr("sp_mod", [1, 48 * 128])
    sp_bc = scr("sp_bc", [128, 9 * 1024])

    _rownb = {}
    _ESZ = {F32: 4, I32: 4, BF16: 2}

    def sbt(name, shape, dtype=F32):
        _rownb[name] = shape[1] * _ESZ[dtype]
        return nc.alloc_sbuf_tensor(name, shape, dtype).ap()

    ident = sbt("identS", [128, 128])
    csrel = sbt("csrelS", [128, 128])
    cscols = sbt("cscolsS", [128, 3])
    mxu = sbt("mxuS", [128, 128]); mxl = sbt("mxlS", [128, 128]); mm2 = sbt("mm2S", [128, 128])
    hsel = sbt("hselS", [128, 128]); hselT = sbt("hselTS", [16, 8 * 128])
    ones1 = sbt("ones1S", [1, 128]); o64c = sbt("o64cS", [64, 1])
    ones128 = sbt("ones128S", [128, 128])
    zb = sbt("zbS", [128, 1024])
    bb_tile = sbt("bbtileS", [128, HN])
    cpool = sbt("cpoolS", [128, 8])
    modc = sbt("modcS", [128, 48])
    cb_cols = sbt("cbcolsS", [128, 8]); u2b_cols = sbt("u2bcolsS", [128, 8])
    u1b_col = sbt("u1bcolS", [64, 1])
    hnw_cols = sbt("hnwcolsS", [64, 16])
    cw_cols = sbt("cwcolsS", [128, 32])
    swg_col = sbt("swgcolS", [128, IC]); swu_col = sbt("swucolS", [128, IC]); swd_col = sbt("swdcolS", [128, 8])
    mean2 = sbt("mean2S", [128, NCH]); rstd2 = sbt("rstd2S", [128, NCH])
    hmax = sbt("hmaxS", [128, 1]); gumax = sbt("gumaxS", [128, 1])
    sig_clip = sbt("sigclipS", [1, 2])
    rsx_col = sbt("rsxcolS", [128, 1]); rsd_col = sbt("rsdcolS", [128, 1])
    gsc_g = sbt("gscgS", [128, IC]); gsc_u = sbt("gscuS", [128, IC]); dsc = sbt("dscS", [128, 8])
    bc0 = sbt("bc0S", [128, 1024]); bc1 = sbt("bc1S", [128, 1024]); bc2 = sbt("bc2S", [128, 1024])
    tiny = sbt("tinyS", [128, 16])

    AR_COLS = 45056
    AR = nc.alloc_sbuf_tensor("arena", [128, AR_COLS], F32).ap()
    ARB = AR.bitcast(BF16)
    _rownb["arena"] = AR_COLS * 4

    psb = [nc.alloc_psum_tensor(f"psb{i}", [128, 512], F32).ap() for i in range(8)]
    for i in range(8):
        _rownb[f"psb{i}"] = 512 * 4
    _APCLS = AR.__class__

    stack = contextlib.ExitStack()
    semh = {}
    cnt = {}
    for s in ("ld", "st", "pe", "dve", "act", "gdma", "cc"):
        semh[s] = stack.enter_context(nc.semaphore("sem_" + s))
        cnt[s] = 0

    blockctx = nc.Block()
    block = blockctx.__enter__()

    _HZL = 1024
    _wr = {"vector": [], "scalar": []}
    _cum = {"vector": 0, "scalar": 0}
    _TRACK = ("tensor_tensor", "tensor_scalar", "scalar_tensor_tensor", "tensor_copy",
              "tensor_reduce", "reciprocal", "memset", "activation", "iota")

    def _rng(ap):
        name = ap.name
        rb = _rownb.get(name)
        sh = ap.shape
        if rb is None or len(sh) != 2:
            return (name, 0, 128, 0, 1 << 30, False)
        esz = _ESZ[ap.dtype]
        rl = rb // esz
        off = ap.offset
        p0 = off // rl
        b0 = (off % rl) * esz
        return (name, p0, p0 + sh[0], b0, b0 + sh[1] * esz, True)

    class _EWrap:
        def __init__(self, t, eng):
            self._t = t
            self._eng = eng

        def nop(self):
            return self.drain()

        def drain(self):
            if self._eng in _wr:
                _wr[self._eng].clear()
            return self._t.drain()

        def __getattr__(self, n):
            f = getattr(self._t, n)
            eng = self._eng
            if eng not in _wr or n not in _TRACK:
                return f

            def call(*a, **k):
                rs, ws = [], []
                if n == "memset":
                    ws.append(_rng(a[0] if a else k["out"]))
                else:
                    for kw, v in k.items():
                        if isinstance(v, _APCLS):
                            (ws if kw == "out" else rs).append(_rng(v))
                    for v in a:
                        if isinstance(v, _APCLS):
                            rs.append(_rng(v))
                wl = _wr[eng]
                cum = _cum[eng]
                wl[:] = [w for w in wl if cum - w[6] < (_HZL if w[5] else 16384)]
                hz = False
                for (nm, p0, p1, b0, b1, ex) in rs:
                    for (wn, wp0, wp1, wb0, wb1, wex, wce) in wl:
                        if nm != wn or p1 <= wp0 or wp1 <= p0 or b1 <= wb0 or wb1 <= b0:
                            continue
                        if not (ex and wex) or (cum - wce) + (wb1 - b0) < _HZL:
                            hz = True
                            break
                    if hz:
                        break
                if hz:
                    self.drain()
                width = 32
                for r in rs + ws:
                    width = max(width, (r[4] - r[3]) if r[5] else 4096)
                _cum[eng] = cum = cum + width
                for w in ws:
                    wl.append(w + (cum,))
                return f(*a, **k)
            return call

    def E(eng, fn):
        box = [None]

        def wrap(e):
            box[0] = fn(_EWrap(e, eng) if eng in ("tensor", "vector", "scalar") else e)
        getattr(block, eng)(wrap)
        return box[0]

    def inc(instr, s, k=1):
        cnt[s] += k
        instr.then_inc(semh[s], k)
        return cnt[s]

    def dinc(s, k=16):
        cnt[s] += k
        return cnt[s]

    bank_last = [None] * 8
    bank_rr = [0]

    def take_bank(eng, nb=1):
        out = []
        for _ in range(nb):
            b = bank_rr[0] % 8
            bank_rr[0] += 1
            if bank_last[b] is not None:
                s, v = bank_last[b]
                eng.wait_ge(semh[s], v)
                bank_last[b] = None
            out.append(b)
        return out

    def consumed(b, s, v):
        bank_last[b] = (s, v)

    A0, A1, A2, A3, A5 = 0, 22528, 25344, 28160, 30976
    VR0, MR0 = 35800, 38880
    mrows = AR[0:1, MR0:MR0 + 6144]
    A4 = A1
    A6 = 33792

    def s0_loads(sync):
        for dst, src in [(ident, ident_h), (csrel, csrel_h), (cscols, cscols_h),
                         (mxu, mxu_h), (mxl, mxl_h), (mm2, mm2_h), (hsel, hsel_h),
                         (hselT, hselT_h), (ones1, ones1_h), (o64c, o64c_h), (cpool, cpool_h)]:
            sync.dma_start(out=dst[:, :], in_=src[:, :]).then_inc(semh["ld"], 16)
            dinc("ld")
        for dst, srcw in [(cb_cols, convb_h), (u2b_cols, u2b_h), (u1b_col, u1b_h),
                          (bb_tile, bb_h), (hnw_cols, hnw_h), (cw_cols, convw_h),
                          (sig_clip, clip_h), (ones128, ones128_h)]:
            sync.dma_start(out=dst[:, :], in_=srcw[:, :]).then_inc(semh["ld"], 16)
            dinc("ld")
        sync.dma_start(out=tiny[:, 0:8], in_=cvec_h[:, :]).then_inc(semh["ld"], 16)
        for _ in range(1):
            dinc("ld")
    E("sync", s0_loads)
    LD0 = cnt["ld"]

    def s0_silc_a(scalar):
        scalar.wait_ge(semh["ld"], LD0)
        scalar.activation(out=tiny[:, 8:16], in_=tiny[:, 0:8], func=AF.Exp, scale=-1.0)
        scalar.activation(out=sig_clip[:, :], in_=sig_clip[:, :], func=AF.Exp, scale=-1.0)
        return inc(scalar.nop(), "act")
    m = E("scalar", s0_silc_a)

    def s0_silc_v(vector, m=m):
        vector.wait_ge(semh["act"], m)
        vector.tensor_scalar(out=tiny[:, 8:16], in0=tiny[:, 8:16], scalar1=1.0, scalar2=None,
                             op0=ALU.add, op1=ALU.bypass)
        vector.reciprocal(out=tiny[:, 8:16], in_=tiny[:, 8:16])
        vector.tensor_tensor(out=tiny[:, 8:16], in0=tiny[:, 0:8], in1=tiny[:, 8:16], op=ALU.mult)
        vector.tensor_scalar(out=sig_clip[:, :], in0=sig_clip[:, :], scalar1=1.0, scalar2=None,
                             op0=ALU.add, op1=ALU.bypass)
        vector.reciprocal(out=sig_clip[:, :], in_=sig_clip[:, :])
        vector.tensor_scalar(out=u2b_cols[:, :], in0=u2b_cols[:, :], scalar1=-1.0, scalar2=None,
                             op0=ALU.mult, op1=ALU.bypass)
        return inc(vector.nop(), "dve")
    M_SILC = E("vector", s0_silc_v)

    if KS0 < 2:
        return _finish(nc, blockctx, stack, E, semh, cnt)

    pe_wm = {}
    for kc in range(DC):
        def s0_wm_ld(sync, kc=kc):
            if kc >= 2:
                sync.wait_ge(semh["dve"], pe_wm[kc - 2])
            sync.dma_start(out=AR[:, A0 + (kc % 2) * 6144:A0 + (kc % 2) * 6144 + 6144],
                           in_=wmod_h[kc * 128:(kc + 1) * 128, :]).then_inc(semh["ld"], 16)
            return dinc("ld")
        mld = E("sync", s0_wm_ld)

        def s0_wm_mm(tensor, kc=kc, mld=mld):
            tensor.wait_ge(semh["ld"], mld)
            if kc == 0:
                tensor.wait_ge(semh["dve"], M_SILC)
            b = take_bank(tensor, 1)[0]
            for mc in range(48):
                base = A0 + (kc % 2) * 6144 + mc * 128
                tensor.matmul(psb[b][:, mc:mc + 1], AR[:, base:base + 128],
                              tiny[:, 8 + kc:9 + kc], start=True, stop=True)
            return b, inc(tensor.nop(), "pe")
        bwm, m_wmk = E("tensor", s0_wm_mm)
        pe_wm[kc] = m_wmk

        def s0_wm_acc(vector, kc=kc, bwm=bwm, m_wmk=m_wmk):
            vector.wait_ge(semh["pe"], m_wmk)
            if kc == 0:
                vector.tensor_copy(out=modc[:, :], in_=psb[bwm][:, 0:48])
            else:
                vector.tensor_tensor(out=modc[:, :], in0=modc[:, :], in1=psb[bwm][:, 0:48], op=ALU.add)
            v = inc(vector.nop(), "dve")
            consumed(bwm, "dve", v)
            return v
        pe_wm[kc] = E("vector", s0_wm_acc)

    if KS0 < 3:
        return _finish(nc, blockctx, stack, E, semh, cnt)

    def s0_modc(vector):
        vector.wait_ge(semh["dve"], pe_wm[DC - 1])
        return inc(vector.nop(), "dve")
    m = E("vector", s0_modc)

    def s0_gabs(scalar, m=m):
        scalar.wait_ge(semh["dve"], m)
        scalar.activation(out=AR[:, A1:A1 + 8], in_=modc[:, 16:24], func=AF.Abs)
        scalar.activation(out=AR[:, A1 + 16:A1 + 24], in_=modc[:, 40:48], func=AF.Abs)
        return inc(scalar.nop(), "act")
    m = E("scalar", s0_gabs)

    def s0_gab(vector, m=m):
        vector.wait_ge(semh["act"], m)
        for base, aoff in ((16, 0), (40, 16)):
            vector.tensor_scalar(out=AR[:, A1 + aoff:A1 + aoff + 8], in0=AR[:, A1 + aoff:A1 + aoff + 8],
                                 scalar1=0.25, scalar2=None, op0=ALU.max, op1=ALU.bypass)
            vector.tensor_scalar(out=AR[:, A1 + 8 + aoff:A1 + 16 + aoff], in0=modc[:, base:base + 8],
                                 scalar1=0.0, scalar2=2.0, op0=ALU.is_ge, op1=ALU.mult)
            vector.tensor_scalar(out=AR[:, A1 + 8 + aoff:A1 + 16 + aoff], in0=AR[:, A1 + 8 + aoff:A1 + 16 + aoff],
                                 scalar1=-1.0, scalar2=None, op0=ALU.add, op1=ALU.bypass)
            vector.tensor_tensor(out=modc[:, base:base + 8], in0=AR[:, A1 + aoff:A1 + aoff + 8],
                                 in1=AR[:, A1 + 8 + aoff:A1 + 16 + aoff], op=ALU.mult)
        return inc(vector.nop(), "dve")
    m = E("vector", s0_gab)

    def s0_modb(tensor, m=m):
        tensor.wait_ge(semh["dve"], m)
        b = take_bank(tensor, 1)[0]
        tensor.transpose(psb[b][0:48, 0:128], modc[:, :], ident[:, :])
        return b, inc(tensor.nop(), "pe")
    b, m = E("tensor", s0_modb)

    def s0_modre(vector, b=b, m=m):
        vector.wait_ge(semh["pe"], m)
        vector.tensor_copy(out=AR[0:48, MR0:MR0 + 128], in_=psb[b][0:48, 0:128])
        v = inc(vector.nop(), "dve")
        consumed(b, "dve", v)
        return v
    m = E("vector", s0_modre)

    def s0_modd(sync, m=m):
        sync.wait_ge(semh["dve"], m)
        sync.dma_start(out=sp_mod[0:1, :].rearrange("o (r q) -> (o r) q", q=128),
                       in_=AR[0:48, MR0:MR0 + 128]).then_inc(semh["st"], 16)
        dinc("st")
        sync.wait_ge(semh["st"], cnt["st"])
        sync.dma_start(out=mrows[:, :], in_=sp_mod[:, :]).then_inc(semh["ld"], 16)
        return dinc("ld")
    m = E("sync", s0_modd)

    def s0_modr(vector, m=m):
        vector.wait_ge(semh["ld"], m)
        return inc(vector.nop(), "dve")
    M_MODR = E("vector", s0_modr)

    def zb_init(vector):
        vector.memset(zb[:, :], 0.0)
        return inc(vector.nop(), "dve")
    M_ZB = E("vector", zb_init)
    zb_last = [0]

    def emit_bcast(dst, roff, addone, dep_s="dve", dep_v=None):
        dep = dep_v if dep_v is not None else M_MODR

        def cstep(vector):
            vector.wait_ge(semh[dep_s], dep)
            vector.wait_ge(semh["dve"], M_ZB)
            if zb_last[0]:
                vector.wait_ge(semh["pe"], zb_last[0])
            vector.tensor_copy(out=zb[0:1, 0:1024], in_=mrows[0:1, roff * 128:(roff + 8) * 128])
            return inc(vector.nop(), "dve")
        mc0 = E("vector", cstep)

        def tstep(tensor):
            tensor.wait_ge(semh["dve"], mc0)
            bs = take_bank(tensor, 2)
            for j in range(8):
                tensor.matmul(psb[bs[j // 4]][:, (j % 4) * 128:(j % 4) * 128 + 128],
                              ones128[:, :], zb[:, j * 128:(j + 1) * 128], start=True, stop=True)
            return bs, inc(tensor.nop(), "pe")
        bs, m2 = E("tensor", tstep)
        zb_last[0] = m2

        def vstep(vector):
            vector.wait_ge(semh["pe"], m2)
            for h in range(2):
                if addone:
                    vector.tensor_scalar(out=dst[:, h * 512:(h + 1) * 512], in0=psb[bs[h]][:, 0:512],
                                         scalar1=1.0, scalar2=None, op0=ALU.add, op1=ALU.bypass)
                else:
                    vector.tensor_copy(out=dst[:, h * 512:(h + 1) * 512], in_=psb[bs[h]][:, 0:512])
            v = inc(vector.nop(), "dve")
            for bb2 in bs:
                consumed(bb2, "dve", v)
            return v
        return E("vector", vstep)

    def emit_wquant(src_h, nchunks, ncols, scol, spill):
        def ld(sync):
            for kc in range(nchunks):
                sync.dma_start(out=AR[:, A0 + kc * ncols:A0 + (kc + 1) * ncols],
                               in_=src_h[kc * 128:(kc + 1) * 128, :]).then_inc(semh["ld"], 16)
                dinc("ld")
            return cnt["ld"]
        mld = E("sync", ld)

        def folda(scalar):
            scalar.wait_ge(semh["ld"], mld)
            scalar.activation(out=AR[:, A1:A1 + ncols], in_=AR[:, A0:A0 + ncols], func=AF.Abs)
            return inc(scalar.nop(), "act")
        mfa = E("scalar", folda)

        def fold(vector):
            first = True
            for kc in range(1, nchunks):
                pass
            return None
        mprev = mfa
        for kc in range(1, nchunks):
            def fa(scalar, kc=kc, mprev=mprev):
                if kc == 1:
                    scalar.wait_ge(semh["ld"], mld)
                else:
                    scalar.wait_ge(semh["dve"], mprev)
                scalar.activation(out=AR[:, A3:A3 + ncols], in_=AR[:, A0 + kc * ncols:A0 + (kc + 1) * ncols],
                                  func=AF.Abs)
                return inc(scalar.nop(), "act")
            ma2 = E("scalar", fa)

            def fv(vector, ma2=ma2):
                vector.wait_ge(semh["act"], ma2)
                vector.tensor_tensor(out=AR[:, A1:A1 + ncols], in0=AR[:, A1:A1 + ncols],
                                     in1=AR[:, A3:A3 + ncols], op=ALU.max)
                return inc(vector.nop(), "dve")
            mprev = E("vector", fv)
        mf = mprev

        nq = ncols // 128
        done_q = 0
        while done_q < nq:
            take = min(8, nq - done_q)

            def trq(tensor, done_q=done_q, take=take):
                tensor.wait_ge(semh["dve"], mf)
                bs = take_bank(tensor, (take + 3) // 4)
                for q in range(take):
                    tensor.transpose(psb[bs[q // 4]][:, (q % 4) * 128:(q % 4) * 128 + 128],
                                     AR[:, A1 + (done_q + q) * 128:A1 + (done_q + q + 1) * 128],
                                     ident[:, :])
                return bs, inc(tensor.nop(), "pe")
            bs, mt = E("tensor", trq)

            def rdq(vector, bs=bs, mt=mt, done_q=done_q, take=take):
                vector.wait_ge(semh["pe"], mt)
                for q in range(take):
                    vector.tensor_reduce(out=scol[:, done_q + q:done_q + q + 1],
                                         in_=psb[bs[q // 4]][:, (q % 4) * 128:(q % 4) * 128 + 128],
                                         axis=AX.X, op=ALU.max)
                v = inc(vector.nop(), "dve")
                for b2 in bs:
                    consumed(b2, "dve", v)
                return v
            E("vector", rdq)
            done_q += take

        def scfin(vector):
            vector.tensor_scalar(out=scol[:, 0:nq], in0=scol[:, 0:nq], scalar1=1.0 / 127.0,
                                 scalar2=None, op0=ALU.mult, op1=ALU.bypass)
            vector.reciprocal(out=AR[:, A3:A3 + nq], in_=scol[:, 0:nq])
            return inc(vector.nop(), "dve")
        mf2 = E("vector", scfin)

        def scoltr(tensor):
            tensor.wait_ge(semh["dve"], mf2)
            b = take_bank(tensor, 1)[0]
            tensor.transpose(psb[b][0:nq, 0:128], AR[:, A3:A3 + nq], ident[:, :])
            return b, inc(tensor.nop(), "pe")
        b, mtr = E("tensor", scoltr)

        def scolev(vector, b=b, mtr=mtr):
            vector.wait_ge(semh["pe"], mtr)
            vector.tensor_copy(out=AR[0:nq, A5:A5 + 128], in_=psb[b][0:nq, 0:128])
            v = inc(vector.nop(), "dve")
            consumed(b, "dve", v)
            return v
        m_scol0 = E("vector", scolev)

        def scolld(sync):
            sync.wait_ge(semh["dve"], m_scol0)
            sync.dma_start(out=sp_row[0:1, 0:ncols].rearrange("o (r q) -> (o r) q", q=128),
                           in_=AR[0:nq, A5:A5 + 128]).then_inc(semh["st"], 16)
            dinc("st")
            sync.wait_ge(semh["st"], cnt["st"])
            sync.dma_start(out=AR[0:1, A1 + ncols:A1 + 2 * ncols],
                           in_=sp_row[0:1, 0:ncols]).then_inc(semh["ld"], 16)
            return dinc("ld")
        m_scol = E("sync", scolld)

        nq2 = ncols // 128
        done = 0
        while done < nq2:
            take = min(8, nq2 - done)

            def bcp(vector, done=done, take=take):
                vector.wait_ge(semh["ld"], m_scol)
                vector.wait_ge(semh["dve"], M_ZB)
                if zb_last[0]:
                    vector.wait_ge(semh["pe"], zb_last[0])
                vector.tensor_copy(out=zb[0:1, 0:take * 128],
                                   in_=AR[0:1, A1 + ncols + done * 128:A1 + ncols + (done + take) * 128])
                return inc(vector.nop(), "dve")
            mcp = E("vector", bcp)

            def bw(tensor, done=done, take=take, mcp=mcp):
                tensor.wait_ge(semh["dve"], mcp)
                bs = take_bank(tensor, (take + 3) // 4)
                for q in range(take):
                    tensor.matmul(psb[bs[q // 4]][:, (q % 4) * 128:(q % 4) * 128 + 128],
                                  ones128[:, :], zb[:, q * 128:(q + 1) * 128], start=True, stop=True)
                return bs, inc(tensor.nop(), "pe")
            bs, mb = E("tensor", bw)
            zb_last[0] = mb

            def bv(vector, bs=bs, mb=mb, done=done, take=take):
                vector.wait_ge(semh["pe"], mb)
                for q in range(take):
                    qq = done + q
                    vector.tensor_copy(out=AR[:, A2 + qq * 128:A2 + (qq + 1) * 128],
                                       in_=psb[bs[q // 4]][:, (q % 4) * 128:(q % 4) * 128 + 128])
                v = inc(vector.nop(), "dve")
                for bb2 in bs:
                    consumed(bb2, "dve", v)
                return v
            E("vector", bv)
            done += take

        def quant(vector):
            for kc in range(nchunks):
                w0 = AR[:, A0 + kc * ncols:A0 + (kc + 1) * ncols]
                vector.tensor_tensor(out=AR[:, A3:A3 + ncols], in0=w0, in1=AR[:, A2:A2 + ncols], op=ALU.mult)
                icst = AR[:, A4:A4 + ncols].bitcast(I32)
                vector.tensor_copy(out=icst, in_=AR[:, A3:A3 + ncols])
                vector.tensor_copy(out=AR[:, A5:A5 + ncols], in_=icst)
                vector.tensor_tensor(out=AR[:, A1:A1 + ncols], in0=AR[:, A3:A3 + ncols],
                                     in1=AR[:, A5:A5 + ncols], op=ALU.subtract)
                vector.scalar_tensor_tensor(out=AR[:, A3:A3 + ncols], in0=AR[:, A1:A1 + ncols],
                                            scalar=0.5, in1=AR[:, A5:A5 + ncols], op0=ALU.is_gt, op1=ALU.add)
                vector.scalar_tensor_tensor(out=AR[:, A5:A5 + ncols], in0=AR[:, A1:A1 + ncols],
                                            scalar=-0.5, in1=AR[:, A3:A3 + ncols], op0=ALU.is_ge, op1=ALU.add)
                vector.tensor_scalar(out=AR[:, A3:A3 + ncols], in0=AR[:, A5:A5 + ncols],
                                     scalar1=-1.0, scalar2=127.0, op0=ALU.add, op1=ALU.min)
                vector.tensor_scalar(out=ARB[:, 2 * (A0 + kc * ncols):2 * (A0 + kc * ncols) + ncols],
                                     in0=AR[:, A3:A3 + ncols],
                                     scalar1=-128.0, scalar2=None, op0=ALU.max, op1=ALU.bypass)
            return inc(vector.nop(), "dve")
        mq = E("vector", quant)

        def sp(sync):
            sync.wait_ge(semh["dve"], mq)
            for kc in range(nchunks):
                sync.dma_start(out=spill[:, kc * ncols:(kc + 1) * ncols],
                               in_=ARB[:, 2 * (A0 + kc * ncols):2 * (A0 + kc * ncols) + ncols]
                               ).then_inc(semh["st"], 16)
                dinc("st")
            return cnt["st"]
        return E("sync", sp)

    if KS0 < 4:
        return _finish(nc, blockctx, stack, E, semh, cnt)

    _bc_specs = [
        0,
        8,
        24,
        32,
        16,
        40,
    ]
    KS04 = int(_os.environ.get("KS04", "99"))
    for bi, roff in enumerate(_bc_specs):
        if bi >= KS04:
            break
        mbt = emit_bcast(bc0, roff, addone=(bi in (1, 3)))
        if KS04 == bi + 100:
            pass

        if int(_os.environ.get("KSNOSPILL", "0")) == 1:
            continue

        def bsp(sync, bi=bi, mbt=mbt):
            sync.wait_ge(semh["dve"], mbt)
            sync.dma_start(out=sp_bc[:, bi * 1024:(bi + 1) * 1024], in_=bc0[:, :]).then_inc(semh["st"], 16)
            return dinc("st")
        msz = E("sync", bsp)

        def bspw(vector, msz=msz):
            vector.wait_ge(semh["st"], msz)
            return inc(vector.nop(), "dve")
        E("vector", bspw)

    def emit_bcld(dst, bi, dep_s="dve", dep_v=None):
        hsrc = {6: winw_h, 7: ffnw_h, 8: ab_h}
        def bld(sync):
            if dep_v is not None:
                sync.wait_ge(semh[dep_s], dep_v)
            if bi in hsrc:
                sync.dma_start(out=dst[:, :], in_=hsrc[bi][:, :]).then_inc(semh["ld"], 16)
            else:
                sync.dma_start(out=dst[:, :], in_=sp_bc[:, bi * 1024:(bi + 1) * 1024]).then_inc(semh["ld"], 16)
            return dinc("ld")
        m2 = E("sync", bld)

        def bldw(vector, m2=m2):
            vector.wait_ge(semh["ld"], m2)
            return inc(vector.nop(), "dve")
        return E("vector", bldw)

    if KS0 < 5:
        return _finish(nc, blockctx, stack, E, semh, cnt)

    emit_wquant(wgT_h, DC, II, swg_col, sp_wgq)
    emit_wquant(wuT_h, DC, II, swu_col, sp_wuq)
    m_wq = emit_wquant(wdT_h, IC, D, swd_col, sp_wdq)

    if KSTOP < 1:
        return _finish(nc, blockctx, stack, E, semh, cnt)

    emit_bcld(bc0, 1, dep_s="st", dep_v=m_wq)
    emit_bcld(bc1, 0)
    M_BC = emit_bcld(bc2, 6)

    W0, XT0, SQ0, H10, HN0, CL0 = 0, 512, 1536, 2560, 3584, 4608
    H1F0, HNE0, HCP0, ES0, US0, SG0 = 4640, 5664, 6720, 7744, 8768, 9088

    def s1_wld(sync):
        sync.wait_ge(semh["st"], m_wq)
        for j in range(DC):
            sync.dma_start(out=AR[:, W0 + j * 64:W0 + (j + 1) * 64],
                           in_=u1w_h[j * 128:(j + 1) * 128, :]).then_inc(semh["ld"], 16)
            dinc("ld")
        return cnt["ld"]
    M_U1W = E("sync", s1_wld)

    s1_done = {}
    for ti in range(NCH):
        def ld(sync, ti=ti):
            if ti > 0:
                sync.wait_ge(semh["dve"], s1_done[ti - 1])
            else:
                sync.wait_ge(semh["ld"], M_U1W)
                sync.wait_ge(semh["dve"], M_BC)
            sync.dma_start(out=AR[:, XT0:XT0 + 1024],
                           in_=x_h[ti * 128:(ti + 1) * 128, :]).then_inc(semh["ld"], 16)
            return dinc("ld")
        mld = E("sync", ld)

        def v1(vector, mld=mld):
            vector.wait_ge(semh["ld"], mld)
            x = AR[:, XT0:XT0 + 1024]
            vector.tensor_reduce(out=AR[:, CL0:CL0 + 1], in_=x, axis=AX.X, op=ALU.add)
            vector.tensor_tensor(out=AR[:, SQ0:SQ0 + 1024], in0=x, in1=x, op=ALU.mult)
            vector.tensor_reduce(out=AR[:, CL0 + 1:CL0 + 2], in_=AR[:, SQ0:SQ0 + 1024],
                                 axis=AX.X, op=ALU.add)
            vector.tensor_scalar(out=AR[:, CL0:CL0 + 1], in0=AR[:, CL0:CL0 + 1],
                                 scalar1=1.0 / 1024.0, scalar2=None, op0=ALU.mult, op1=ALU.bypass)
            vector.tensor_scalar(out=AR[:, CL0 + 1:CL0 + 2], in0=AR[:, CL0 + 1:CL0 + 2],
                                 scalar1=1.0 / 1024.0, scalar2=None, op0=ALU.mult, op1=ALU.bypass)
            vector.tensor_tensor(out=AR[:, CL0 + 2:CL0 + 3], in0=AR[:, CL0:CL0 + 1],
                                 in1=AR[:, CL0:CL0 + 1], op=ALU.mult)
            vector.tensor_tensor(out=AR[:, CL0 + 1:CL0 + 2], in0=AR[:, CL0 + 1:CL0 + 2],
                                 in1=AR[:, CL0 + 2:CL0 + 3], op=ALU.subtract)
            return inc(vector.nop(), "dve")
        m1 = E("vector", v1)

        def a1(scalar, m1=m1):
            scalar.wait_ge(semh["dve"], m1)
            scalar.activation(out=AR[:, CL0 + 3:CL0 + 4], in_=AR[:, CL0 + 1:CL0 + 2],
                              func=AF.Ln, bias=cpool[:, 1:2], scale=1.0)
            scalar.activation(out=AR[:, CL0 + 3:CL0 + 4], in_=AR[:, CL0 + 3:CL0 + 4],
                              func=AF.Exp, scale=-0.5)
            return inc(scalar.nop(), "act")
        m2 = E("scalar", a1)

        def v2(vector, m2=m2):
            vector.wait_ge(semh["act"], m2)
            x = AR[:, XT0:XT0 + 1024]
            vector.tensor_scalar(out=AR[:, H10:H10 + 1024], in0=x,
                                 scalar1=AR[:, CL0:CL0 + 1], scalar2=AR[:, CL0 + 3:CL0 + 4],
                                 op0=ALU.subtract, op1=ALU.mult)
            vector.tensor_tensor(out=AR[:, H10:H10 + 1024], in0=AR[:, H10:H10 + 1024],
                                 in1=bc0[:, :], op=ALU.mult)
            vector.tensor_tensor(out=AR[:, H10:H10 + 1024], in0=AR[:, H10:H10 + 1024],
                                 in1=bc1[:, :], op=ALU.add)
            vector.tensor_tensor(out=AR[:, SQ0:SQ0 + 1024], in0=AR[:, H10:H10 + 1024],
                                 in1=AR[:, H10:H10 + 1024], op=ALU.mult)
            vector.tensor_reduce(out=AR[:, CL0 + 4:CL0 + 5], in_=AR[:, SQ0:SQ0 + 1024],
                                 axis=AX.X, op=ALU.add)
            return inc(vector.nop(), "dve")
        m3 = E("vector", v2)

        def a2(scalar, m3=m3):
            scalar.wait_ge(semh["dve"], m3)
            scalar.activation(out=AR[:, CL0 + 5:CL0 + 6], in_=AR[:, CL0 + 4:CL0 + 5],
                              func=AF.Ln, bias=cpool[:, 0:1], scale=1.0 / 1024.0)
            scalar.activation(out=AR[:, CL0 + 5:CL0 + 6], in_=AR[:, CL0 + 5:CL0 + 6],
                              func=AF.Exp, scale=-0.5)
            return inc(scalar.nop(), "act")
        m4 = E("scalar", a2)

        def v3(vector, m4=m4):
            vector.wait_ge(semh["act"], m4)
            vector.tensor_scalar(out=AR[:, HN0:HN0 + 1024], in0=AR[:, H10:H10 + 1024],
                                 scalar1=AR[:, CL0 + 5:CL0 + 6], scalar2=None, op0=ALU.mult, op1=ALU.bypass)
            vector.tensor_tensor(out=AR[:, HN0:HN0 + 1024], in0=AR[:, HN0:HN0 + 1024],
                                 in1=bc2[:, :], op=ALU.mult)
            return inc(vector.nop(), "dve")
        m5 = E("vector", v3)

        def t1(tensor, m5=m5):
            tensor.wait_ge(semh["dve"], m5)
            bs = take_bank(tensor, 4)
            for j in range(DC):
                tensor.transpose(psb[bs[j // 2]][:, (j % 2) * 128:(j % 2) * 128 + 128],
                                 AR[:, H10 + j * 128:H10 + (j + 1) * 128], ident[:, :])
            return bs, inc(tensor.nop(), "pe")
        bs, m6 = E("tensor", t1)

        def v4(vector, bs=bs, m6=m6):
            vector.wait_ge(semh["pe"], m6)
            for j in range(DC):
                vector.tensor_copy(out=AR[:, H1F0 + j * 128:H1F0 + (j + 1) * 128],
                                   in_=psb[bs[j // 2]][:, (j % 2) * 128:(j % 2) * 128 + 128])
            v = inc(vector.nop(), "dve")
            for b2 in bs:
                consumed(b2, "dve", v)
            return v
        m7 = E("vector", v4)

        def t2(tensor, m5=m5):
            bs = take_bank(tensor, 4)
            for j in range(DC):
                tensor.transpose(psb[bs[j // 2]][:, (j % 2) * 128:(j % 2) * 128 + 128],
                                 AR[:, HN0 + j * 128:HN0 + (j + 1) * 128], ident[:, :])
            return bs, inc(tensor.nop(), "pe")
        bs2, m8 = E("tensor", t2)

        def v5(vector, ti=ti, bs=bs2, m8=m8):
            vector.wait_ge(semh["pe"], m8)
            hne3 = AR[:, HNE0:HNE0 + 1048].rearrange("p (j q) -> p j q", q=131)
            if ti > 0:
                vector.tensor_copy(out=hne3[:, :, 0:3], in_=hne3[:, :, 128:131])
            else:
                vector.memset(AR[:, HNE0:HNE0 + 1048], 0.0)
            for j in range(DC):
                vector.tensor_copy(out=AR[:, HNE0 + j * 131 + 3:HNE0 + j * 131 + 131],
                                   in_=psb[bs[j // 2]][:, (j % 2) * 128:(j % 2) * 128 + 128])
            v = inc(vector.nop(), "dve")
            for b2 in bs:
                consumed(b2, "dve", v)
            for j in range(DC):
                base = HNE0 + j * 131
                vector.tensor_scalar(out=AR[:, HCP0 + j * 128:HCP0 + (j + 1) * 128],
                                     in0=AR[:, base:base + 128], scalar1=cw_cols[:, j * 4:j * 4 + 1],
                                     scalar2=None, op0=ALU.mult, op1=ALU.bypass)
                for s in range(1, 4):
                    vector.scalar_tensor_tensor(out=AR[:, HCP0 + j * 128:HCP0 + (j + 1) * 128],
                                                in0=AR[:, base + s:base + s + 128],
                                                scalar=cw_cols[:, j * 4 + s:j * 4 + s + 1],
                                                in1=AR[:, HCP0 + j * 128:HCP0 + (j + 1) * 128],
                                                op0=ALU.mult, op1=ALU.add)
                vector.tensor_scalar(out=AR[:, HCP0 + j * 128:HCP0 + (j + 1) * 128],
                                     in0=AR[:, HCP0 + j * 128:HCP0 + (j + 1) * 128],
                                     scalar1=cb_cols[:, j:j + 1], scalar2=None, op0=ALU.add, op1=ALU.bypass)
            return inc(vector.nop(), "dve")
        m9 = E("vector", v5)

        def a3(scalar, m9=m9):
            scalar.wait_ge(semh["dve"], m9)
            scalar.activation(out=AR[:, ES0:ES0 + 1024], in_=AR[:, HCP0:HCP0 + 1024],
                              func=AF.Exp, scale=-1.0)
            return inc(scalar.nop(), "act")
        m10 = E("scalar", a3)

        def v6(vector, m10=m10):
            vector.wait_ge(semh["act"], m10)
            vector.tensor_scalar(out=AR[:, ES0:ES0 + 1024], in0=AR[:, ES0:ES0 + 1024],
                                 scalar1=1.0, scalar2=None, op0=ALU.add, op1=ALU.bypass)
            vector.reciprocal(out=AR[:, ES0:ES0 + 1024], in_=AR[:, ES0:ES0 + 1024])
            vector.tensor_tensor(out=AR[:, HCP0:HCP0 + 1024], in0=AR[:, HCP0:HCP0 + 1024],
                                 in1=AR[:, ES0:ES0 + 1024], op=ALU.mult)
            return inc(vector.nop(), "dve")
        m11 = E("vector", v6)

        def t3(tensor, m7=m7):
            tensor.wait_ge(semh["dve"], m7)
            b = take_bank(tensor, 1)[0]
            for j in range(DC):
                tensor.matmul(psb[b][0:64, 0:128], AR[:, W0 + j * 64:W0 + (j + 1) * 64],
                              AR[:, H1F0 + j * 128:H1F0 + (j + 1) * 128],
                              start=(j == 0), stop=(j == DC - 1))
            return b, inc(tensor.nop(), "pe")
        b3, m12 = E("tensor", t3)

        def v7(vector, b3=b3, m12=m12):
            vector.wait_ge(semh["pe"], m12)
            vector.tensor_scalar(out=AR[0:64, US0:US0 + 128], in0=psb[b3][0:64, 0:128],
                                 scalar1=u1b_col[:, 0:1], scalar2=None, op0=ALU.add, op1=ALU.bypass)
            v = inc(vector.nop(), "dve")
            consumed(b3, "dve", v)
            return v
        m13 = E("vector", v7)

        def a4(scalar, m13=m13):
            scalar.wait_ge(semh["dve"], m13)
            scalar.activation(out=AR[0:64, US0 + 128:US0 + 256], in_=AR[0:64, US0:US0 + 128],
                              func=AF.Exp, scale=-1.0)
            return inc(scalar.nop(), "act")
        m14 = E("scalar", a4)

        def v8(vector, m14=m14):
            vector.wait_ge(semh["act"], m14)
            vector.tensor_scalar(out=AR[0:64, US0 + 128:US0 + 256], in0=AR[0:64, US0 + 128:US0 + 256],
                                 scalar1=1.0, scalar2=None, op0=ALU.add, op1=ALU.bypass)
            vector.reciprocal(out=AR[0:64, US0 + 128:US0 + 256], in_=AR[0:64, US0 + 128:US0 + 256])
            vector.tensor_tensor(out=AR[0:64, SG0:SG0 + 128], in0=AR[0:64, US0:US0 + 128],
                                 in1=AR[0:64, US0 + 128:US0 + 256], op=ALU.mult)
            return inc(vector.nop(), "dve")
        m15 = E("vector", v8)

        def sp1(sync, ti=ti, m11=m11, m15=m15):
            sync.wait_ge(semh["dve"], m11)
            sync.dma_start(out=sp_h1[:, ti * 1024:(ti + 1) * 1024],
                           in_=AR[:, H1F0:H1F0 + 1024]).then_inc(semh["st"], 16)
            sync.dma_start(out=sp_hc[:, ti * 1024:(ti + 1) * 1024],
                           in_=AR[:, HCP0:HCP0 + 1024]).then_inc(semh["st"], 16)
            sync.wait_ge(semh["dve"], m15)
            sync.dma_start(out=sp_sg[:, ti * 128:(ti + 1) * 128],
                           in_=AR[0:64, SG0:SG0 + 128]).then_inc(semh["st"], 16)
            dinc("st", 48)
            return cnt["st"]
        msp = E("sync", sp1)

        def fin(vector, msp=msp):
            vector.wait_ge(semh["st"], msp)
            return inc(vector.nop(), "dve")
        s1_done[ti] = E("vector", fin)

    S1_DONE = s1_done[NCH - 1]

    if KSTOP < 2:
        return _finish(nc, blockctx, stack, E, semh, cnt)

    M_ABB = emit_bcld(bc2, 8, dep_s="dve", dep_v=S1_DONE)

    PW0, PW1 = 0, 8192
    PBW = 16384
    PHC = 16512
    PO0, PO1 = 17536, 18560
    PSQ = 19584
    PSS, PBS = 20608, 20736

    def qk_wld(sync):
        sync.wait_ge(semh["dve"], S1_DONE)
        for dst, srcw in ((PW0, qw_h), (PW1, kw_h)):
            for j in range(DC):
                sync.dma_start(out=AR[:, dst + j * 1024:dst + (j + 1) * 1024],
                               in_=srcw[j * 128:(j + 1) * 128, :]).then_inc(semh["ld"], 16)
                dinc("ld")
        return cnt["ld"]
    M_QKW = E("sync", qk_wld)

    qk_done = {}
    for c in range(NCH):
        def ld(sync, c=c):
            if c > 0:
                sync.wait_ge(semh["dve"], qk_done[c - 1])
            else:
                sync.wait_ge(semh["ld"], M_QKW)
                sync.wait_ge(semh["dve"], M_ABB)
            sync.dma_start(out=AR[:, PHC:PHC + 1024],
                           in_=sp_hc[:, c * 1024:(c + 1) * 1024]).then_inc(semh["ld"], 16)
            return dinc("ld")
        mld = E("sync", ld)

        def pj(tensor, mld=mld):
            tensor.wait_ge(semh["ld"], mld)
            bs = take_bank(tensor, 4)
            for wbase, boff in ((PW0, 0), (PW1, 2)):
                for mc in range(8):
                    b = bs[boff + mc // 4]
                    for kc in range(DC):
                        tensor.matmul(psb[b][:, (mc % 4) * 128:(mc % 4) * 128 + 128],
                                      AR[:, wbase + kc * 1024 + mc * 128:wbase + kc * 1024 + (mc + 1) * 128],
                                      AR[:, PHC + kc * 128:PHC + (kc + 1) * 128],
                                      start=(kc == 0), stop=(kc == DC - 1))
            return bs, inc(tensor.nop(), "pe")
        bs, m1 = E("tensor", pj)

        def pje(vector, bs=bs, m1=m1):
            vector.wait_ge(semh["pe"], m1)
            for h in range(2):
                vector.tensor_copy(out=AR[:, PO0 + h * 512:PO0 + (h + 1) * 512], in_=psb[bs[h]][:, 0:512])
                vector.tensor_copy(out=AR[:, PO1 + h * 512:PO1 + (h + 1) * 512], in_=psb[bs[2 + h]][:, 0:512])
            v = inc(vector.nop(), "dve")
            for b2 in bs:
                consumed(b2, "dve", v)
            return v
        mqk = E("vector", pje)

        def qknorm(base, dep):
            def sq(scalar):
                scalar.wait_ge(semh["dve"], dep)
                scalar.activation(out=AR[:, PSQ:PSQ + 1024], in_=AR[:, base:base + 1024], func=AF.Square)
                return inc(scalar.nop(), "act")
            ma = E("scalar", sq)

            def ssq(tensor, ma=ma):
                tensor.wait_ge(semh["act"], ma)
                bs = take_bank(tensor, 2)
                for j in range(DC):
                    b = bs[j // 4]
                    tensor.matmul(psb[b][0:64, (j % 4) * 128:(j % 4) * 128 + 128],
                                  ones128[0:64, 0:64], AR[0:64, PSQ + j * 128:PSQ + (j + 1) * 128],
                                  start=True, stop=True)
                    tensor.matmul(psb[b][64:128, (j % 4) * 128:(j % 4) * 128 + 128],
                                  ones128[64:128, 64:128], AR[64:128, PSQ + j * 128:PSQ + (j + 1) * 128],
                                  start=True, stop=True)
                return bs, inc(tensor.nop(), "pe")
            bs, mb = E("tensor", ssq)

            def rn_a(scalar, bs=bs, mb=mb):
                scalar.wait_ge(semh["pe"], mb)
                for h in range(2):
                    scalar.activation(out=AR[:, PSQ + h * 512:PSQ + (h + 1) * 512],
                                      in_=psb[bs[h]][:, 0:512], func=AF.Ln)
                for h in range(2):
                    scalar.activation(out=AR[:, PSQ + h * 512:PSQ + (h + 1) * 512],
                                      in_=AR[:, PSQ + h * 512:PSQ + (h + 1) * 512], func=AF.Exp, scale=0.5)
                v = inc(scalar.nop(), "act")
                for b2 in bs:
                    consumed(b2, "act", v)
                return v
            mc2 = E("scalar", rn_a)

            def app(vector, mc2=mc2):
                vector.wait_ge(semh["act"], mc2)
                vector.tensor_scalar(out=AR[:, PSQ:PSQ + 1024], in0=AR[:, PSQ:PSQ + 1024],
                                     scalar1=1e-6, scalar2=None, op0=ALU.add, op1=ALU.bypass)
                vector.reciprocal(out=AR[:, PSQ:PSQ + 1024], in_=AR[:, PSQ:PSQ + 1024])
                vector.tensor_tensor(out=AR[:, base:base + 1024], in0=AR[:, base:base + 1024],
                                     in1=AR[:, PSQ:PSQ + 1024], op=ALU.mult)
                return inc(vector.nop(), "dve")
            return E("vector", app)

        m_qn = qknorm(PO0, mqk)
        m_kn = qknorm(PO1, m_qn)

        def spq(sync, c=c, m_kn=m_kn):
            sync.wait_ge(semh["dve"], m_kn)
            sync.dma_start(out=sp_qf[:, c * 1024:(c + 1) * 1024],
                           in_=AR[:, PO0:PO0 + 1024]).then_inc(semh["st"], 16)
            sync.dma_start(out=sp_kf[:, c * 1024:(c + 1) * 1024],
                           in_=AR[:, PO1:PO1 + 1024]).then_inc(semh["st"], 16)
            dinc("st", 32)
            return cnt["st"]
        msp = E("sync", spq)

        def fin(vector, msp=msp):
            vector.wait_ge(semh["st"], msp)
            return inc(vector.nop(), "dve")
        qk_done[c] = E("vector", fin)

    QK_DONE = qk_done[NCH - 1]

    if KSTOP < 3:
        return _finish(nc, blockctx, stack, E, semh, cnt)

    def vz_wld(sync):
        sync.wait_ge(semh["dve"], QK_DONE)
        for dst, srcw in ((PW0, vw_h), (PW1, aw_h)):
            for j in range(DC):
                sync.dma_start(out=AR[:, dst + j * 1024:dst + (j + 1) * 1024],
                               in_=srcw[j * 128:(j + 1) * 128, :]).then_inc(semh["ld"], 16)
                dinc("ld")
        for j in range(DC):
            sync.dma_start(out=AR[:, PBW + j * 16:PBW + (j + 1) * 16],
                           in_=bw_h[j * 128:(j + 1) * 128, :]).then_inc(semh["ld"], 16)
            dinc("ld")
        return cnt["ld"]
    M_VZW = E("sync", vz_wld)

    vz_done = {}
    for c in range(NCH):
        def ld(sync, c=c):
            if c > 0:
                sync.wait_ge(semh["dve"], vz_done[c - 1])
            else:
                sync.wait_ge(semh["ld"], M_VZW)
            sync.dma_start(out=AR[:, PHC:PHC + 1024],
                           in_=sp_hc[:, c * 1024:(c + 1) * 1024]).then_inc(semh["ld"], 16)
            return dinc("ld")
        mld = E("sync", ld)

        def pj(tensor, mld=mld):
            tensor.wait_ge(semh["ld"], mld)
            bs = take_bank(tensor, 4)
            for wbase, boff in ((PW0, 0), (PW1, 2)):
                for half in range(2):
                    b = bs[boff + half]
                    for kc in range(DC):
                        tensor.matmul(psb[b][:, 0:512],
                                      AR[:, PHC + kc * 128:PHC + (kc + 1) * 128],
                                      AR[:, wbase + kc * 1024 + half * 512:wbase + kc * 1024 + (half + 1) * 512],
                                      start=(kc == 0), stop=(kc == DC - 1))
            return bs, inc(tensor.nop(), "pe")
        bs, m1 = E("tensor", pj)

        def bmm(tensor, mld=mld):
            b = take_bank(tensor, 1)[0]
            for kc in range(DC):
                tensor.matmul(psb[b][:, 0:16], AR[:, PHC + kc * 128:PHC + (kc + 1) * 128],
                              AR[:, PBW + kc * 16:PBW + (kc + 1) * 16],
                              start=(kc == 0), stop=(kc == DC - 1))
            return b, inc(tensor.nop(), "pe")
        b4, m4 = E("tensor", bmm)

        def pje(vector, bs=bs, m1=m1):
            vector.wait_ge(semh["pe"], m1)
            for half in range(2):
                vector.tensor_copy(out=AR[:, PO0 + half * 512:PO0 + (half + 1) * 512],
                                   in_=psb[bs[half]][:, 0:512])
            for half in range(2):
                vector.tensor_tensor(out=AR[:, PSQ + half * 512:PSQ + (half + 1) * 512],
                                     in0=psb[bs[2 + half]][:, 0:512],
                                     in1=bc2[:, half * 512:(half + 1) * 512], op=ALU.add)
            v = inc(vector.nop(), "dve")
            for b2 in bs:
                consumed(b2, "dve", v)
            return v
        m3 = E("vector", pje)

        def spx(scalar, m3=m3):
            scalar.wait_ge(semh["dve"], m3)
            scalar.activation(out=AR[:, PO1:PO1 + 1024], in_=AR[:, PSQ:PSQ + 1024],
                              func=AF.Exp, scale=-1.0)
            scalar.activation(out=AR[:, PO1:PO1 + 1024], in_=AR[:, PO1:PO1 + 1024],
                              func=AF.Ln, bias=cpool[:, 2:3], scale=1.0)
            return inc(scalar.nop(), "act")
        m_sp = E("scalar", spx)

        def bsig_v0(vector, b4=b4, m4=m4):
            vector.wait_ge(semh["pe"], m4)
            vector.tensor_tensor(out=AR[:, PBS:PBS + 16], in0=psb[b4][:, 0:16],
                                 in1=bb_tile[:, :], op=ALU.add)
            v = inc(vector.nop(), "dve")
            consumed(b4, "dve", v)
            return v
        m5 = E("vector", bsig_v0)

        def bsig_a(scalar, m5=m5):
            scalar.wait_ge(semh["dve"], m5)
            scalar.activation(out=AR[:, PBS:PBS + 16], in_=AR[:, PBS:PBS + 16],
                              func=AF.Exp, scale=-1.0)
            return inc(scalar.nop(), "act")
        m5a = E("scalar", bsig_a)

        def bsig_v(vector, m5a=m5a):
            vector.wait_ge(semh["act"], m5a)
            vector.tensor_scalar(out=AR[:, PBS:PBS + 16], in0=AR[:, PBS:PBS + 16],
                                 scalar1=1.0, scalar2=None, op0=ALU.add, op1=ALU.bypass)
            vector.reciprocal(out=AR[:, PBS:PBS + 16], in_=AR[:, PBS:PBS + 16])
            return inc(vector.nop(), "dve")
        m_bs = E("vector", bsig_v)

        def spv(sync, c=c, m_sp=m_sp, m_bs=m_bs):
            sync.wait_ge(semh["act"], m_sp)
            sync.wait_ge(semh["dve"], m_bs)
            sync.dma_start(out=sp_vt[:, c * 1024:(c + 1) * 1024],
                           in_=AR[:, PO0:PO0 + 1024]).then_inc(semh["st"], 16)
            sync.dma_start(out=sp_sp[:, c * 1024:(c + 1) * 1024],
                           in_=AR[:, PO1:PO1 + 1024]).then_inc(semh["st"], 16)
            sync.dma_start(out=sp_bt[:, c * 16:(c + 1) * 16],
                           in_=AR[:, PBS:PBS + 16]).then_inc(semh["st"], 16)
            dinc("st", 48)
            return cnt["st"]
        msp = E("sync", spv)

        def fin(vector, msp=msp):
            vector.wait_ge(semh["st"], msp)
            return inc(vector.nop(), "dve")
        vz_done[c] = E("vector", fin)

    VZ_DONE = vz_done[NCH - 1]

    if KSTOP < 4:
        return _finish(nc, blockctx, stack, E, semh, cnt)

    QF2, KF2, VT2, SPT2, LRT2, EP2 = 0, 1024, 2048, 3072, 4096, 5120
    EN2, SCR2, KMP2, KMN2, KTM2 = 6144, 7168, 8192, 9216, 10240
    GC2, BSB2 = 11264, 11296
    XA_, XB_, XTA_, XTB_ = 11552, 11680, 11808, 11936
    WA_, WB_, M2T_, SA_, SB_ = 12064, 12128, 12192, 12320, 12832
    OHP_, O2S_, RNR_ = 13344, 15392, 17440
    xa_t = AR[:, XA_:XA_ + 128]; xb_t = AR[:, XB_:XB_ + 128]
    xta_t = AR[:, XTA_:XTA_ + 128]; xtb_t = AR[:, XTB_:XTB_ + 128]
    wa_t = AR[:, WA_:WA_ + 64]; wb_t = AR[:, WB_:WB_ + 64]
    m2t_t = AR[:, M2T_:M2T_ + 128]
    s_a = AR[:, SA_:SA_ + 512]; s_b = AR[:, SB_:SB_ + 512]
    ohp_t = AR[0:64, OHP_:OHP_ + 2048]; o2s_t = AR[0:64, O2S_:O2S_ + 2048]
    rnrow_t = AR[0:1, RNR_:RNR_ + 2048]

    def sinit(vector):
        vector.memset(s_a[:, :], 0.0)
        return inc(vector.nop(), "dve")
    M_SINIT = E("vector", sinit)

    GCREF = AR[:, GC2:GC2 + 8]
    GCFULL = AR[:, GC2 + 8:GC2 + 16]
    GCR = AR[:, GC2 + 16:GC2 + 24]

    sa2_done = {}
    for c in range(NCH):
        Scur = s_a if (c % 2 == 0) else s_b
        Snew = s_b if (c % 2 == 0) else s_a

        def ld(sync, c=c):
            if c > 0:
                sync.wait_ge(semh["dve"], sa2_done[c - 1])
            else:
                sync.wait_ge(semh["dve"], VZ_DONE)
            sync.dma_start(out=AR[:, QF2:QF2 + 1024],
                           in_=sp_qf[:, c * 1024:(c + 1) * 1024]).then_inc(semh["ld"], 16)
            sync.dma_start(out=AR[:, KF2:KF2 + 1024],
                           in_=sp_kf[:, c * 1024:(c + 1) * 1024]).then_inc(semh["ld"], 16)
            sync.dma_start(out=AR[:, VT2:VT2 + 1024],
                           in_=sp_vt[:, c * 1024:(c + 1) * 1024]).then_inc(semh["ld"], 16)
            sync.dma_start(out=AR[:, SPT2:SPT2 + 1024],
                           in_=sp_sp[:, c * 1024:(c + 1) * 1024]).then_inc(semh["ld"], 16)
            sync.dma_start(out=AR[:, BSB2:BSB2 + 16],
                           in_=sp_bt[:, c * 16:(c + 1) * 16]).then_inc(semh["ld"], 16)
            dinc("ld", 80)
            return cnt["ld"]
        mld = E("sync", ld)

        def lrmm(tensor, mld=mld, c=c):
            tensor.wait_ge(semh["ld"], mld)
            if c == 0:
                tensor.wait_ge(semh["dve"], M_SINIT)
            bs = take_bank(tensor, 2)
            for half in range(2):
                tensor.matmul(psb[bs[half]][:, 0:512], csrel[:, :],
                              AR[:, SPT2 + half * 512:SPT2 + (half + 1) * 512], start=True, stop=True)
            b3 = take_bank(tensor, 1)[0]
            for q in range(3):
                for j in range(DC):
                    tensor.matmul(psb[b3][:, q * 8 + j:q * 8 + j + 1],
                                  AR[:, SPT2 + j * 128:SPT2 + (j + 1) * 128],
                                  cscols[:, q:q + 1], start=True, stop=True)
            return bs, b3, inc(tensor.nop(), "pe")
        bs, b3, m6 = E("tensor", lrmm)

        def lre(vector, bs=bs, m6=m6):
            vector.wait_ge(semh["pe"], m6)
            for half in range(2):
                vector.tensor_copy(out=AR[:, LRT2 + half * 512:LRT2 + (half + 1) * 512],
                                   in_=psb[bs[half]][:, 0:512])
            v = inc(vector.nop(), "dve")
            for b2 in bs:
                consumed(b2, "dve", v)
            return inc(vector.nop(), "dve")
        m7 = E("vector", lre)

        def gce(scalar, b3=b3, m6=m6):
            scalar.wait_ge(semh["pe"], m6)
            scalar.activation(out=GCREF, in_=psb[b3][:, 0:8], func=AF.Exp, bias=cpool[:, 4:5], scale=1.0)
            scalar.activation(out=GCFULL, in_=psb[b3][:, 8:16], func=AF.Exp, scale=1.0)
            scalar.activation(out=GCR, in_=psb[b3][:, 16:24], func=AF.Exp, bias=cpool[:, 4:5], scale=1.0)
            v = inc(scalar.nop(), "act")
            consumed(b3, "act", v)
            return v
        m_gc = E("scalar", gce)

        def ltr(tensor, m7=m7):
            tensor.wait_ge(semh["dve"], m7)
            bs = take_bank(tensor, 2)
            for j in range(DC):
                tensor.transpose(psb[bs[j // 4]][:, (j % 4) * 128:(j % 4) * 128 + 128],
                                 AR[:, LRT2 + j * 128:LRT2 + (j + 1) * 128], ident[:, :])
            return bs, inc(tensor.nop(), "pe")
        bs, m8 = E("tensor", ltr)

        def Ee(scalar, bs=bs, m8=m8):
            scalar.wait_ge(semh["pe"], m8)
            for h in range(2):
                scalar.activation(out=AR[:, EP2 + h * 512:EP2 + (h + 1) * 512], in_=psb[bs[h]][:, 0:512],
                                  func=AF.Exp, bias=cpool[:, 3:4], scale=1.0)
            for h in range(2):
                scalar.activation(out=AR[:, EN2 + h * 512:EN2 + (h + 1) * 512], in_=psb[bs[h]][:, 0:512],
                                  func=AF.Exp, bias=cpool[:, 3:4], scale=-1.0)
            v = inc(scalar.nop(), "act")
            for b2 in bs:
                consumed(b2, "act", v)
            return v
        m_E = E("scalar", Ee)

        def scaled(vector, m_E=m_E, m_gc=m_gc):
            vector.wait_ge(semh["act"], m_E)
            vector.wait_ge(semh["act"], m_gc)
            vector.tensor_tensor(out=AR[:, SCR2:SCR2 + 1024], in0=AR[:, KF2:KF2 + 1024],
                                 in1=AR[:, EP2:EP2 + 1024], op=ALU.mult)
            vector.tensor_tensor(out=AR[:, KMP2:KMP2 + 1024], in0=AR[:, KF2:KF2 + 1024],
                                 in1=AR[:, EN2:EN2 + 1024], op=ALU.mult)
            for j in range(DC):
                vector.tensor_scalar(out=AR[:, KF2 + j * 128:KF2 + (j + 1) * 128],
                                     in0=AR[:, SCR2 + j * 128:SCR2 + (j + 1) * 128],
                                     scalar1=GCREF[:, j:j + 1], scalar2=None, op0=ALU.mult, op1=ALU.bypass)
            return inc(vector.nop(), "dve")
        m9 = E("vector", scaled)

        def kmfin(vector, m9=m9):
            vector.wait_ge(semh["dve"], m9)
            vector.tensor_scalar(out=AR[:, KMN2:KMN2 + 1024], in0=AR[:, KMP2:KMP2 + 1024],
                                 scalar1=-1.0, scalar2=None, op0=ALU.mult, op1=ALU.bypass)
            vector.tensor_tensor(out=AR[:, QF2:QF2 + 1024], in0=AR[:, QF2:QF2 + 1024],
                                 in1=AR[:, EP2:EP2 + 1024], op=ALU.mult)
            for j in range(DC):
                vector.tensor_scalar(out=AR[:, EN2 + j * 128:EN2 + (j + 1) * 128],
                                     in0=AR[:, QF2 + j * 128:QF2 + (j + 1) * 128],
                                     scalar1=GCREF[:, j:j + 1], scalar2=None, op0=ALU.mult, op1=ALU.bypass)
            return inc(vector.nop(), "dve")
        m11 = E("vector", kmfin)

        def ktr(tensor, m11=m11):
            tensor.wait_ge(semh["dve"], m11)
            bs = take_bank(tensor, 2)
            for j in range(DC):
                tensor.transpose(psb[bs[j // 4]][:, (j % 4) * 128:(j % 4) * 128 + 128],
                                 AR[:, KMP2 + j * 128:KMP2 + (j + 1) * 128], ident[:, :])
            return bs, inc(tensor.nop(), "pe")
        bs, m12 = E("tensor", ktr)

        def kte(vector, bs=bs, m12=m12):
            vector.wait_ge(semh["pe"], m12)
            for h in range(2):
                vector.tensor_copy(out=AR[:, KTM2 + h * 512:KTM2 + (h + 1) * 512],
                                   in_=psb[bs[h]][:, 0:512])
            v = inc(vector.nop(), "dve")
            for b2 in bs:
                consumed(b2, "dve", v)
            return v
        m_prev = E("vector", kte)

        if DEBUG_SCAN and c == 0:
            def dbg1(sync, m_prev=m_prev):
                sync.wait_ge(semh["dve"], m_prev)
                for off, base in [(0, LRT2), (1024, EP2), (2048, EN2), (3072, SCR2),
                                  (4096, KMP2), (5120, QF2), (6144, KTM2), (7168, KF2)]:
                    sync.dma_start(out=sp_dbg[:, off:off + 1024],
                                   in_=AR[:, base:base + 1024]).then_inc(semh["st"], 16)
                    dinc("st")
                sync.dma_start(out=sp_dbg[:, 8192:8192 + 32], in_=AR[:, GC2:GC2 + 32]).then_inc(semh["st"], 16)
                dinc("st")
                return cnt["st"]
            mdb = E("sync", dbg1)

            def dbg1w(vector, mdb=mdb):
                vector.wait_ge(semh["st"], mdb)
                return inc(vector.nop(), "dve")
            m_prev = E("vector", dbg1w)

        for hh in range(HN):
            j = hh // 2
            lo = (hh % 2) * 64
            kmn_h = AR[lo:lo + 64, KMN2 + j * 128:KMN2 + (j + 1) * 128]
            kmp_h = AR[lo:lo + 64, KMP2 + j * 128:KMP2 + (j + 1) * 128]
            khp_h = AR[lo:lo + 64, SCR2 + j * 128:SCR2 + (j + 1) * 128]
            kg_h = AR[lo:lo + 64, KF2 + j * 128:KF2 + (j + 1) * 128]
            qh_h = AR[lo:lo + 64, QF2 + j * 128:QF2 + (j + 1) * 128]
            qg_h = AR[lo:lo + 64, EN2 + j * 128:EN2 + (j + 1) * 128]
            scur_h = Scur[lo:lo + 64, j * 64:(j + 1) * 64]
            snew_h = Snew[lo:lo + 64, j * 64:(j + 1) * 64]
            vt_h = AR[:, VT2 + hh * 64:VT2 + (hh + 1) * 64]
            ktm_h = AR[:, KTM2 + hh * 64:KTM2 + (hh + 1) * 64]

            def hmm(tensor, kmn_h=kmn_h, khp_h=khp_h, kmp_h=kmp_h, qh_h=qh_h,
                    kg_h=kg_h, scur_h=scur_h, m_prev=m_prev):
                tensor.wait_ge(semh["dve"], m_prev)
                bs = take_bank(tensor, 4)
                tensor.matmul(psb[bs[0]][:, 0:128], kmn_h, khp_h, start=True, stop=True)
                tensor.matmul(psb[bs[2]][:, 0:128], kmp_h, qh_h, start=True, stop=True)
                tensor.matmul(psb[bs[3]][:, 0:64], kg_h, scur_h, start=True, stop=True)
                return bs, inc(tensor.nop(), "pe")
            bs, ma = E("tensor", hmm)

            bcol = AR[:, BSB2 + hh:BSB2 + hh + 1]

            def hmsk(vector, bs=bs, ma=ma, vt_h=vt_h, bcol=bcol):
                vector.wait_ge(semh["pe"], ma)
                vector.tensor_tensor(out=xa_t[:, :], in0=psb[bs[0]][:, 0:128], in1=mxu[:, :], op=ALU.mult)
                vector.tensor_scalar(out=xa_t[:, :], in0=xa_t[:, :], scalar1=bcol, scalar2=None,
                                     op0=ALU.mult, op1=ALU.bypass)
                vector.tensor_tensor(out=m2t_t[:, :], in0=psb[bs[2]][:, 0:128], in1=mm2[:, :], op=ALU.mult)
                vector.tensor_scalar(out=m2t_t[:, :], in0=m2t_t[:, :], scalar1=bcol, scalar2=None,
                                     op0=ALU.mult, op1=ALU.bypass)
                vector.tensor_tensor(out=wa_t[:, :], in0=vt_h, in1=psb[bs[3]][:, 0:64], op=ALU.subtract)
                v = inc(vector.nop(), "dve")
                for b2 in bs:
                    consumed(b2, "dve", v)
                return v
            mh1 = E("vector", hmsk)

            def xtr0(tensor, mh1=mh1):
                tensor.wait_ge(semh["dve"], mh1)
                b = take_bank(tensor, 1)[0]
                tensor.transpose(psb[b][:, 0:128], xa_t[:, :], ident[:, :])
                return b, inc(tensor.nop(), "pe")
            bx, mh2 = E("tensor", xtr0)

            def xtr0e(vector, bx=bx, mh2=mh2):
                vector.wait_ge(semh["pe"], mh2)
                vector.tensor_copy(out=xta_t[:, :], in_=psb[bx][:, 0:128])
                v = inc(vector.nop(), "dve")
                consumed(bx, "dve", v)
                return v
            mdep = E("vector", xtr0e)

            xcur, xtcur, xnxt, xtnxt = xa_t, xta_t, xb_t, xtb_t
            wcur, wnxt = wa_t, wb_t
            for it in range(5):
                def sl(tensor, xcur=xcur, xtcur=xtcur, wcur=wcur, it=it, mdep=mdep):
                    tensor.wait_ge(semh["dve"], mdep)
                    bs = take_bank(tensor, 2 if it < 4 else 1)
                    tensor.matmul(psb[bs[0]][:, 0:64], xcur[:, :], wcur[:, :], start=True, stop=True)
                    if it < 4:
                        tensor.matmul(psb[bs[1]][:, 0:128], xtcur[:, :], xcur[:, :], start=True, stop=True)
                    return bs, inc(tensor.nop(), "pe")
                bs, mb = E("tensor", sl)

                def sle(vector, bs=bs, mb=mb, wcur=wcur, wnxt=wnxt, xnxt=xnxt, it=it):
                    vector.wait_ge(semh["pe"], mb)
                    vector.tensor_tensor(out=wnxt[:, :], in0=psb[bs[0]][:, 0:64], in1=wcur[:, :], op=ALU.add)
                    if it < 4:
                        vector.tensor_copy(out=xnxt[:, :], in_=psb[bs[1]][:, 0:128])
                    v = inc(vector.nop(), "dve")
                    for b2 in bs:
                        consumed(b2, "dve", v)
                    return v
                mdep = E("vector", sle)

                if it < 4:
                    def sltr(tensor, xnxt=xnxt, mdep=mdep):
                        tensor.wait_ge(semh["dve"], mdep)
                        b = take_bank(tensor, 1)[0]
                        tensor.transpose(psb[b][:, 0:128], xnxt[:, :], ident[:, :])
                        return b, inc(tensor.nop(), "pe")
                    bx2, mtx = E("tensor", sltr)

                    def sltre(vector, bx2=bx2, mtx=mtx, xtnxt=xtnxt):
                        vector.wait_ge(semh["pe"], mtx)
                        vector.tensor_copy(out=xtnxt[:, :], in_=psb[bx2][:, 0:128])
                        v = inc(vector.nop(), "dve")
                        consumed(bx2, "dve", v)
                        return v
                    mdep = E("vector", sltre)
                xcur, xnxt = xnxt, xcur
                xtcur, xtnxt = xtnxt, xtcur
                wcur, wnxt = wnxt, wcur
            wfin = wcur

            if DEBUG_SCAN and c == 0 and hh == 0:
                def dbg2(sync, mdep=mdep, wfin=wfin):
                    sync.wait_ge(semh["dve"], mdep)
                    sync.dma_start(out=sp_dbg[:, 8320:8320 + 128], in_=xa_t[:, :]).then_inc(semh["st"], 16)
                    sync.dma_start(out=sp_dbg[:, 8448:8448 + 64], in_=wa_t[:, :]).then_inc(semh["st"], 16)
                    sync.dma_start(out=sp_dbg[:, 8512:8512 + 64], in_=wfin[:, :]).then_inc(semh["st"], 16)
                    sync.dma_start(out=sp_dbg[:, 8576:8576 + 128], in_=m2t_t[:, :]).then_inc(semh["st"], 16)
                    dinc("st", 64)
                    return cnt["st"]
                mdb2 = E("sync", dbg2)

                def dbg2w(vector, mdb2=mdb2):
                    vector.wait_ge(semh["st"], mdb2)
                    return inc(vector.nop(), "dve")
                mdep = E("vector", dbg2w)

            wtl = wnxt

            def wtld(vector, wfin=wfin, wtl=wtl, bcol=bcol, mdep=mdep):
                vector.wait_ge(semh["dve"], mdep)
                vector.tensor_scalar(out=wtl[:, :], in0=wfin[:, :], scalar1=bcol, scalar2=None,
                                     op0=ALU.mult, op1=ALU.bypass)
                return inc(vector.nop(), "dve")
            mdep = E("vector", wtld)

            def omm(tensor, wfin=wfin, wtl=wtl, scur_h=scur_h, qg_h=qg_h, ktm_h=ktm_h, hh=hh, mdep=mdep):
                tensor.wait_ge(semh["dve"], mdep)
                bs = take_bank(tensor, 2)
                tensor.matmul(psb[bs[0]][0:64, 0:128], wfin[:, :], m2t_t[:, :], start=True, stop=False)
                tensor.matmul(psb[bs[0]][0:64, 0:128], scur_h, qg_h, start=False, stop=True)
                lo2 = (hh % 2) * 64
                tensor.matmul(psb[bs[1]][lo2:lo2 + 64, 0:64], ktm_h, wtl[:, :], start=True, stop=True)
                return bs, inc(tensor.nop(), "pe")
            bs, mo = E("tensor", omm)

            def oev(vector, bs=bs, mo=mo, hh=hh, scur_h=scur_h, snew_h=snew_h):
                vector.wait_ge(semh["pe"], mo)
                vector.tensor_scalar(out=ohp_t[:, hh * 128:(hh + 1) * 128], in0=psb[bs[0]][0:64, 0:128],
                                     scalar1=hnw_cols[:, hh:hh + 1], scalar2=None, op0=ALU.mult, op1=ALU.bypass)
                vector.tensor_copy(out=o2s_t[:, hh * 128:(hh + 1) * 128], in_=psb[bs[0]][0:64, 0:128])
                v = inc(vector.nop(), "dve")
                consumed(bs[0], "dve", v)
                lo2 = (hh % 2) * 64
                jj = hh // 2
                vector.tensor_scalar(out=snew_h, in0=scur_h, scalar1=GCFULL[lo2:lo2 + 64, jj:jj + 1],
                                     scalar2=None, op0=ALU.mult, op1=ALU.bypass)
                vector.scalar_tensor_tensor(out=snew_h, in0=psb[bs[1]][lo2:lo2 + 64, 0:64],
                                            scalar=GCR[lo2:lo2 + 64, jj:jj + 1], in1=snew_h,
                                            op0=ALU.mult, op1=ALU.add)
                v = inc(vector.nop(), "dve")
                consumed(bs[1], "dve", v)
                return v
            m_prev = E("vector", oev)

        if DEBUG_SCAN and c == 0:
            def dbg3(sync, m_prev=m_prev):
                sync.wait_ge(semh["dve"], m_prev)
                sync.dma_start(out=sp_dbg[0:64, 8704:8704 + 2048], in_=o2s_t[:, :]).then_inc(semh["st"], 16)
                sync.dma_start(out=sp_dbg[:, 10752:10752 + 512], in_=Snew[:, :]).then_inc(semh["st"], 16)
                dinc("st", 32)
                return cnt["st"]
            mdb3 = E("sync", dbg3)

            def dbg3w(vector, mdb3=mdb3):
                vector.wait_ge(semh["st"], mdb3)
                return inc(vector.nop(), "dve")
            m_prev = E("vector", dbg3w)

        def onorm_a(scalar, m_prev=m_prev):
            scalar.wait_ge(semh["dve"], m_prev)
            scalar.activation(out=o2s_t[:, :], in_=o2s_t[:, :], func=AF.Square)
            return inc(scalar.nop(), "act")
        ma = E("scalar", onorm_a)

        def onorm_t(tensor, ma=ma):
            tensor.wait_ge(semh["act"], ma)
            bs = take_bank(tensor, 4)
            for q in range(4):
                tensor.matmul(psb[bs[q]][0:64, 0:512], ones128[0:64, 0:64],
                              o2s_t[:, q * 512:(q + 1) * 512], start=True, stop=True)
            return bs, inc(tensor.nop(), "pe")
        bs5, mb5 = E("tensor", onorm_t)

        def onorm_a2(scalar, bs5=bs5, mb5=mb5):
            scalar.wait_ge(semh["pe"], mb5)
            for q in range(4):
                scalar.activation(out=AR[0:64, RNR_ + q * 512:RNR_ + (q + 1) * 512],
                                  in_=psb[bs5[q]][0:64, 0:512], func=AF.Ln,
                                  bias=cpool[0:64, 0:1], scale=1.0 / 64.0)
            scalar.activation(out=AR[0:64, RNR_:RNR_ + 2048], in_=AR[0:64, RNR_:RNR_ + 2048],
                              func=AF.Exp, scale=-0.5)
            v = inc(scalar.nop(), "act")
            for b2 in bs5:
                consumed(b2, "act", v)
            return v
        mc5 = E("scalar", onorm_a2)

        def onorm_v(vector, mc5=mc5):
            vector.wait_ge(semh["act"], mc5)
            for q in range(4):
                vector.tensor_tensor(out=ohp_t[:, q * 512:(q + 1) * 512],
                                     in0=ohp_t[:, q * 512:(q + 1) * 512],
                                     in1=AR[0:64, RNR_ + q * 512:RNR_ + (q + 1) * 512], op=ALU.mult)
            return inc(vector.nop(), "dve")
        me5 = E("vector", onorm_v)

        def osp(sync, c=c, me5=me5):
            sync.wait_ge(semh["dve"], me5)
            sync.dma_start(out=sp_on[:, c * 2048:(c + 1) * 2048], in_=ohp_t[:, :]).then_inc(semh["st"], 16)
            return dinc("st")
        msp = E("sync", osp)

        def fin(vector, msp=msp):
            vector.wait_ge(semh["st"], msp)
            return inc(vector.nop(), "dve")
        sa2_done[c] = E("vector", fin)

    SA2_DONE = sa2_done[NCH - 1]

    if KSTOP < 5:
        return _finish(nc, blockctx, stack, E, semh, cnt)

    M_FFNWB = emit_bcld(bc0, 7, dep_s="dve", dep_v=SA2_DONE)

    OW1, U2W1, ON1, SG1 = 0, 16384, 17408, 25600
    H1T1, Y1T1, GSC1, Y2T1, SQ1, RS1 = 26112, 30208, 34304, 34816, 38912, 43008

    def sb1_wld(sync):
        sync.wait_ge(semh["dve"], SA2_DONE)
        for hh in range(HN):
            sync.dma_start(out=AR[0:64, OW1 + hh * 1024:OW1 + (hh + 1) * 1024],
                           in_=outw_h[hh * 64:(hh + 1) * 64, :]).then_inc(semh["ld"], 16)
            dinc("ld")
        sync.dma_start(out=AR[0:64, U2W1:U2W1 + 1024], in_=u2w_h[:, :]).then_inc(semh["ld"], 16)
        return dinc("ld")
    M_SB1W = E("sync", sb1_wld)

    sb1_done = {}
    for t5 in range(NT5):
        def ld(sync, t5=t5):
            if t5 > 0:
                sync.wait_ge(semh["dve"], sb1_done[t5 - 1])
            else:
                sync.wait_ge(semh["ld"], M_SB1W)
                sync.wait_ge(semh["dve"], M_FFNWB)
            sync.dma_start(out=AR[0:64, ON1:ON1 + 8192],
                           in_=sp_on[:, t5 * 8192:(t5 + 1) * 8192]).then_inc(semh["ld"], 16)
            sync.dma_start(out=AR[0:64, SG1:SG1 + 512],
                           in_=sp_sg[:, t5 * 512:(t5 + 1) * 512]).then_inc(semh["ld"], 16)
            sync.dma_start(out=AR[:, H1T1:H1T1 + 4096],
                           in_=sp_h1[:, t5 * 4096:(t5 + 1) * 4096]).then_inc(semh["ld"], 16)
            dinc("ld", 48)
            return cnt["ld"]
        mld = E("sync", ld)

        m_y1 = None
        for mc in range(8):
            def om(tensor, mc=mc, mld=mld):
                tensor.wait_ge(semh["ld"], mld)
                bs = take_bank(tensor, 2)
                on3 = AR[0:64, ON1:ON1 + 8192].rearrange("p (cc q) -> p cc q", q=2048)
                for hh in range(HN):
                    tensor.matmul(psb[bs[0]][:, 0:512],
                                  AR[0:64, OW1 + hh * 1024 + mc * 128:OW1 + hh * 1024 + (mc + 1) * 128],
                                  on3[:, :, hh * 128:(hh + 1) * 128],
                                  start=(hh == 0), stop=(hh == HN - 1))
                tensor.matmul(psb[bs[1]][:, 0:512], AR[0:64, U2W1 + mc * 128:U2W1 + (mc + 1) * 128],
                              AR[0:64, SG1:SG1 + 512], start=True, stop=True)
                return bs, inc(tensor.nop(), "pe")
            bs, ma = E("tensor", om)

            def ge(scalar, bs=bs, ma=ma, mc=mc):
                scalar.wait_ge(semh["pe"], ma)
                scalar.activation(out=AR[:, GSC1:GSC1 + 512], in_=psb[bs[1]][:, 0:512],
                                  func=AF.Exp, scale=-1.0, bias=u2b_cols[:, mc:mc + 1])
                v = inc(scalar.nop(), "act")
                consumed(bs[1], "act", v)
                return v
            mb = E("scalar", ge)

            def gv(vector, bs=bs, mb=mb, mc=mc):
                vector.wait_ge(semh["act"], mb)
                vector.tensor_scalar(out=AR[:, GSC1:GSC1 + 512], in0=AR[:, GSC1:GSC1 + 512],
                                     scalar1=1.0, scalar2=None, op0=ALU.add, op1=ALU.bypass)
                vector.reciprocal(out=AR[:, GSC1:GSC1 + 512], in_=AR[:, GSC1:GSC1 + 512])
                vector.tensor_tensor(out=AR[:, GSC1:GSC1 + 512], in0=psb[bs[0]][:, 0:512],
                                     in1=AR[:, GSC1:GSC1 + 512], op=ALU.mult)
                v = inc(vector.nop(), "dve")
                consumed(bs[0], "dve", v)
                y1 = AR[:, Y1T1:Y1T1 + 4096].rearrange("p (cc q) -> p cc q", q=1024)[:, :, mc * 128:(mc + 1) * 128]
                h1 = AR[:, H1T1:H1T1 + 4096].rearrange("p (cc q) -> p cc q", q=1024)[:, :, mc * 128:(mc + 1) * 128]
                g = AR[:, GSC1:GSC1 + 512].rearrange("p (cc q) -> p cc q", q=128)
                vector.tensor_tensor(out=y1, in0=h1, in1=g, op=ALU.add)
                return inc(vector.nop(), "dve")
            m_y1 = E("vector", gv)

        def ysq(scalar, m_y1=m_y1):
            scalar.wait_ge(semh["dve"], m_y1)
            scalar.activation(out=AR[:, SQ1:SQ1 + 4096], in_=AR[:, Y1T1:Y1T1 + 4096], func=AF.Square)
            return inc(scalar.nop(), "act")
        ma = E("scalar", ysq)

        def yssq(tensor, ma=ma):
            tensor.wait_ge(semh["act"], ma)
            b = take_bank(tensor, 1)[0]
            for cc in range(4):
                for j in range(DC):
                    tensor.matmul(psb[b][:, cc:cc + 1],
                                  AR[:, SQ1 + cc * 1024 + j * 128:SQ1 + cc * 1024 + (j + 1) * 128],
                                  cpool[:, 2:3], start=(j == 0), stop=(j == DC - 1))
            return b, inc(tensor.nop(), "pe")
        b6, mb6 = E("tensor", yssq)

        def yrs(scalar, b6=b6, mb6=mb6):
            scalar.wait_ge(semh["pe"], mb6)
            scalar.activation(out=AR[:, RS1:RS1 + 4], in_=psb[b6][:, 0:4], func=AF.Ln,
                              bias=cpool[:, 0:1], scale=1.0 / 1024.0)
            scalar.activation(out=AR[:, RS1:RS1 + 4], in_=AR[:, RS1:RS1 + 4], func=AF.Exp, scale=-0.5)
            v = inc(scalar.nop(), "act")
            consumed(b6, "act", v)
            return v
        mc6 = E("scalar", yrs)

        def yrst(tensor, mc6=mc6):
            tensor.wait_ge(semh["act"], mc6)
            b = take_bank(tensor, 1)[0]
            tensor.transpose(psb[b][0:4, 0:128], AR[:, RS1:RS1 + 4], ident[:, :])
            return b, inc(tensor.nop(), "pe")
        b7, md6 = E("tensor", yrst)

        def yrse(vector, b7=b7, md6=md6):
            vector.wait_ge(semh["pe"], md6)
            vector.tensor_copy(out=AR[0:4, RS1 + 4:RS1 + 132], in_=psb[b7][0:4, 0:128])
            v = inc(vector.nop(), "dve")
            consumed(b7, "dve", v)
            return v
        md6b = E("vector", yrse)

        def yrsd(sync, md6b=md6b):
            sync.wait_ge(semh["dve"], md6b)
            sync.dma_start(out=sp_row[0:1, 0:512].rearrange("o (r q) -> (o r) q", q=128),
                           in_=AR[0:4, RS1 + 4:RS1 + 132]).then_inc(semh["st"], 16)
            dinc("st")
            sync.wait_ge(semh["st"], cnt["st"])
            sync.dma_start(out=AR[0:1, RS1 + 4:RS1 + 516],
                           in_=sp_row[0:1, 0:512]).then_inc(semh["ld"], 16)
            return dinc("ld")
        md6c = E("sync", yrsd)

        def yrsr(vector, md6c=md6c):
            vector.wait_ge(semh["ld"], md6c)
            return inc(vector.nop(), "dve")
        me6 = E("vector", yrsr)

        def yrsc(vector, me6=me6):
            vector.wait_ge(semh["ld"], md6c)
            if zb_last[0]:
                vector.wait_ge(semh["pe"], zb_last[0])
            vector.tensor_copy(out=zb[0:1, 0:512], in_=AR[0:1, RS1 + 4:RS1 + 516])
            return inc(vector.nop(), "dve")
        me6b = E("vector", yrsc)

        def yrsb(tensor, me6b=me6b):
            tensor.wait_ge(semh["dve"], me6b)
            b = take_bank(tensor, 1)[0]
            for cc in range(4):
                tensor.matmul(psb[b][:, cc * 128:(cc + 1) * 128], ones128[:, :],
                              zb[:, cc * 128:(cc + 1) * 128], start=True, stop=True)
            zb_last[0] = cnt["pe"] + 1
            return b, inc(tensor.nop(), "pe")
        b8, mf6 = E("tensor", yrsb)

        def y2v(vector, b8=b8, mf6=mf6):
            vector.wait_ge(semh["pe"], mf6)
            for cc in range(4):
                for j in range(DC):
                    vector.tensor_tensor(
                        out=AR[:, Y2T1 + cc * 1024 + j * 128:Y2T1 + cc * 1024 + (j + 1) * 128],
                        in0=AR[:, Y1T1 + cc * 1024 + j * 128:Y1T1 + cc * 1024 + (j + 1) * 128],
                        in1=psb[b8][:, cc * 128:(cc + 1) * 128], op=ALU.mult)
            v = inc(vector.nop(), "dve")
            consumed(b8, "dve", v)
            for cc in range(4):
                vector.tensor_tensor(out=AR[:, Y2T1 + cc * 1024:Y2T1 + (cc + 1) * 1024],
                                     in0=AR[:, Y2T1 + cc * 1024:Y2T1 + (cc + 1) * 1024],
                                     in1=bc0[:, :], op=ALU.mult)
            return inc(vector.nop(), "dve")
        m_y2 = E("vector", y2v)

        def ysp(sync, t5=t5, m_y2=m_y2):
            sync.wait_ge(semh["dve"], m_y2)
            sync.dma_start(out=sp_y1[:, t5 * 4096:(t5 + 1) * 4096],
                           in_=AR[:, Y1T1:Y1T1 + 4096]).then_inc(semh["st"], 16)
            sync.dma_start(out=sp_y2[:, t5 * 4096:(t5 + 1) * 4096],
                           in_=AR[:, Y2T1:Y2T1 + 4096]).then_inc(semh["st"], 16)
            dinc("st", 32)
            return cnt["st"]
        msp = E("sync", ysp)

        def fin(vector, msp=msp):
            vector.wait_ge(semh["st"], msp)
            return inc(vector.nop(), "dve")
        sb1_done[t5] = E("vector", fin)

    SB1_DONE = sb1_done[NT5 - 1]

    if KSTOP < 6:
        return _finish(nc, blockctx, stack, E, semh, cnt)

    FW1, FW3, FW2 = 0, 8192, 16384
    Y2T2, SL2, SF2 = 24576, 28672, 32768
    FO2 = Y2T2

    sb2_prev = SB1_DONE
    for g in range(4):
        def wld(sync, g=g, prev=sb2_prev):
            sync.wait_ge(semh["dve"], prev)
            for j in range(DC):
                sync.dma_start(out=AR[:, FW1 + j * 1024:FW1 + (j + 1) * 1024],
                               in_=fw1_h[j * 128:(j + 1) * 128, g * 1024:(g + 1) * 1024]).then_inc(semh["ld"], 16)
                sync.dma_start(out=AR[:, FW3 + j * 1024:FW3 + (j + 1) * 1024],
                               in_=fw3_h[j * 128:(j + 1) * 128, g * 1024:(g + 1) * 1024]).then_inc(semh["ld"], 16)
                sync.dma_start(out=AR[:, FW2 + j * 1024:FW2 + (j + 1) * 1024],
                               in_=fw2_h[g * 1024 + j * 128:g * 1024 + (j + 1) * 128, :]).then_inc(semh["ld"], 16)
                dinc("ld", 48)
            return cnt["ld"]
        mwld = E("sync", wld)

        tile_done = {}
        for t5 in range(NT5):
            def ld(sync, t5=t5, mwld=mwld):
                if t5 > 0:
                    sync.wait_ge(semh["dve"], tile_done[t5 - 1])
                else:
                    sync.wait_ge(semh["ld"], mwld)
                sync.dma_start(out=AR[:, Y2T2:Y2T2 + 4096],
                               in_=sp_y2[:, t5 * 4096:(t5 + 1) * 4096]).then_inc(semh["ld"], 16)
                return dinc("ld")
            mld = E("sync", ld)

            msl = None
            m_slv_prev = [None]
            for mc in range(8):
                def g13(tensor, mc=mc, mld=mld):
                    if mc == 0:
                        tensor.wait_ge(semh["ld"], mld)
                    bs = take_bank(tensor, 2)
                    y23 = AR[:, Y2T2:Y2T2 + 4096].rearrange("p (cc q) -> p cc q", q=1024)
                    for which, wb in ((0, FW1), (1, FW3)):
                        for j in range(DC):
                            tensor.matmul(psb[bs[which]][:, 0:512],
                                          AR[:, wb + j * 1024 + mc * 128:wb + j * 1024 + (mc + 1) * 128],
                                          y23[:, :, j * 128:(j + 1) * 128],
                                          start=(j == 0), stop=(j == DC - 1))
                    return bs, inc(tensor.nop(), "pe")
                bs, ma = E("tensor", g13)

                def sev(scalar, bs=bs, ma=ma):
                    scalar.wait_ge(semh["pe"], ma)
                    if m_slv_prev[0] is not None:
                        scalar.wait_ge(semh["dve"], m_slv_prev[0])
                    scalar.activation(out=AR[:, SF2:SF2 + 512], in_=psb[bs[0]][:, 0:512], func=AF.Silu)
                    v = inc(scalar.nop(), "act")
                    consumed(bs[0], "act", v)
                    return v
                mb = E("scalar", sev)

                def slv(vector, bs=bs, mb=mb, mc=mc):
                    vector.wait_ge(semh["act"], mb)
                    vector.tensor_tensor(out=AR[:, SL2 + mc * 512:SL2 + (mc + 1) * 512],
                                         in0=AR[:, SF2:SF2 + 512], in1=psb[bs[1]][:, 0:512], op=ALU.mult)
                    v = inc(vector.nop(), "dve")
                    consumed(bs[1], "dve", v)
                    return v
                msl = E("vector", slv)
                m_slv_prev[0] = msl

            for mo in range(8):
                def w2m(tensor, mo=mo, msl=msl):
                    if mo == 0:
                        tensor.wait_ge(semh["dve"], msl)
                    b = take_bank(tensor, 1)[0]
                    for kc in range(DC):
                        tensor.matmul(psb[b][:, 0:512],
                                      AR[:, FW2 + kc * 1024 + mo * 128:FW2 + kc * 1024 + (mo + 1) * 128],
                                      AR[:, SL2 + kc * 512:SL2 + (kc + 1) * 512],
                                      start=(kc == 0), stop=(kc == DC - 1))
                    return b, inc(tensor.nop(), "pe")
                b9, mw = E("tensor", w2m)

                def w2e(vector, b9=b9, mw=mw, mo=mo):
                    vector.wait_ge(semh["pe"], mw)
                    fo3 = AR[:, FO2:FO2 + 4096].rearrange("p (cc q) -> p cc q", q=1024)
                    ps3 = psb[b9][:, 0:512].rearrange("p (cc q) -> p cc q", q=128)
                    vector.tensor_copy(out=fo3[:, :, mo * 128:(mo + 1) * 128], in_=ps3)
                    v = inc(vector.nop(), "dve")
                    consumed(b9, "dve", v)
                    return v
                mev = E("vector", w2e)

            def spo(sync, g=g, t5=t5, mev=mev):
                sync.wait_ge(semh["dve"], mev)
                sync.dma_start(out=sp_fp[g][:, t5 * 4096:(t5 + 1) * 4096],
                               in_=AR[:, FO2:FO2 + 4096]).then_inc(semh["st"], 16)
                return dinc("st")
            msp = E("sync", spo)

            def fin(vector, msp=msp):
                vector.wait_ge(semh["st"], msp)
                return inc(vector.nop(), "dve")
            tile_done[t5] = E("vector", fin)
        sb2_prev = tile_done[NT5 - 1]

    SB2_DONE = sb2_prev

    if KSTOP < 7:
        return _finish(nc, blockctx, stack, E, semh, cnt)

    M_B30 = emit_bcld(bc0, 3, dep_s="dve", dep_v=SB2_DONE)
    M_B31 = emit_bcld(bc1, 2, dep_s="dve", dep_v=M_B30)
    M_B32 = emit_bcld(bc2, 4, dep_s="dve", dep_v=M_B31)

    Y1T3, P03, P13, P23, P33 = 0, 1024, 2048, 3072, 4096
    XT3, XM3, H23, CL3, SQ3 = 5120, 6144, 7168, 8192, 8704

    def hm_init(vector):
        vector.memset(hmax[:, :], 0.0)
        return inc(vector.nop(), "dve")
    M_HM0 = E("vector", hm_init)

    sb3_done = {}
    for ti in range(NCH):
        def ld(sync, ti=ti):
            if ti > 0:
                sync.wait_ge(semh["dve"], sb3_done[ti - 1])
            else:
                sync.wait_ge(semh["dve"], M_B32)
                sync.wait_ge(semh["dve"], M_HM0)
            sync.dma_start(out=AR[:, Y1T3:Y1T3 + 1024],
                           in_=sp_y1[:, ti * 1024:(ti + 1) * 1024]).then_inc(semh["ld"], 16)
            for g in range(4):
                sync.dma_start(out=AR[:, P03 + g * 1024:P03 + (g + 1) * 1024],
                               in_=sp_fp[g][:, ti * 1024:(ti + 1) * 1024]).then_inc(semh["ld"], 16)
            sync.dma_start(out=AR[:, XT3:XT3 + 1024],
                           in_=x_h[ti * 128:(ti + 1) * 128, :]).then_inc(semh["ld"], 16)
            dinc("ld", 96)
            return cnt["ld"]
        mld = E("sync", ld)

        def addp(vector, mld=mld):
            vector.wait_ge(semh["ld"], mld)
            vector.tensor_tensor(out=AR[:, Y1T3:Y1T3 + 1024], in0=AR[:, Y1T3:Y1T3 + 1024],
                                 in1=AR[:, P03:P03 + 1024], op=ALU.add)
            vector.tensor_tensor(out=AR[:, P13:P13 + 1024], in0=AR[:, P13:P13 + 1024],
                                 in1=AR[:, P23:P23 + 1024], op=ALU.add)
            vector.tensor_tensor(out=AR[:, Y1T3:Y1T3 + 1024], in0=AR[:, Y1T3:Y1T3 + 1024],
                                 in1=AR[:, P33:P33 + 1024], op=ALU.add)
            vector.tensor_tensor(out=AR[:, Y1T3:Y1T3 + 1024], in0=AR[:, Y1T3:Y1T3 + 1024],
                                 in1=AR[:, P13:P13 + 1024], op=ALU.add)
            return inc(vector.nop(), "dve")
        ma = E("vector", addp)

        def aoutt(tensor, ma=ma):
            tensor.wait_ge(semh["dve"], ma)
            bs = take_bank(tensor, 2)
            for j in range(DC):
                tensor.transpose(psb[bs[j // 4]][:, (j % 4) * 128:(j % 4) * 128 + 128],
                                 AR[:, Y1T3 + j * 128:Y1T3 + (j + 1) * 128], ident[:, :])
            return bs, inc(tensor.nop(), "pe")
        bs, mb = E("tensor", aoutt)

        def xmid(vector, bs=bs, mb=mb, ti=ti):
            vector.wait_ge(semh["pe"], mb)
            for h in range(2):
                vector.tensor_tensor(out=AR[:, XM3 + h * 512:XM3 + (h + 1) * 512],
                                     in0=psb[bs[h]][:, 0:512],
                                     in1=bc2[:, h * 512:(h + 1) * 512], op=ALU.mult)
            v = inc(vector.nop(), "dve")
            for b2 in bs:
                consumed(b2, "dve", v)
            vector.tensor_tensor(out=AR[:, XM3:XM3 + 1024], in0=AR[:, XM3:XM3 + 1024],
                                 in1=AR[:, XT3:XT3 + 1024], op=ALU.add)
            xm = AR[:, XM3:XM3 + 1024]
            vector.tensor_reduce(out=AR[:, CL3:CL3 + 1], in_=xm, axis=AX.X, op=ALU.add)
            vector.tensor_tensor(out=AR[:, SQ3:SQ3 + 1024], in0=xm, in1=xm, op=ALU.mult)
            vector.tensor_reduce(out=AR[:, CL3 + 1:CL3 + 2], in_=AR[:, SQ3:SQ3 + 1024],
                                 axis=AX.X, op=ALU.add)
            vector.tensor_scalar(out=AR[:, CL3:CL3 + 1], in0=AR[:, CL3:CL3 + 1],
                                 scalar1=1.0 / 1024.0, scalar2=None, op0=ALU.mult, op1=ALU.bypass)
            vector.tensor_scalar(out=AR[:, CL3 + 1:CL3 + 2], in0=AR[:, CL3 + 1:CL3 + 2],
                                 scalar1=1.0 / 1024.0, scalar2=None, op0=ALU.mult, op1=ALU.bypass)
            vector.tensor_tensor(out=AR[:, CL3 + 2:CL3 + 3], in0=AR[:, CL3:CL3 + 1],
                                 in1=AR[:, CL3:CL3 + 1], op=ALU.mult)
            vector.tensor_tensor(out=AR[:, CL3 + 1:CL3 + 2], in0=AR[:, CL3 + 1:CL3 + 2],
                                 in1=AR[:, CL3 + 2:CL3 + 3], op=ALU.subtract)
            return inc(vector.nop(), "dve")
        mc2 = E("vector", xmid)

        def rst(scalar, mc2=mc2):
            scalar.wait_ge(semh["dve"], mc2)
            scalar.activation(out=AR[:, CL3 + 3:CL3 + 4], in_=AR[:, CL3 + 1:CL3 + 2],
                              func=AF.Ln, bias=cpool[:, 1:2], scale=1.0)
            scalar.activation(out=AR[:, CL3 + 3:CL3 + 4], in_=AR[:, CL3 + 3:CL3 + 4],
                              func=AF.Exp, scale=-0.5)
            return inc(scalar.nop(), "act")
        md = E("scalar", rst)

        def h2c(vector, md=md, ti=ti):
            vector.wait_ge(semh["act"], md)
            vector.tensor_copy(out=mean2[:, ti:ti + 1], in_=AR[:, CL3:CL3 + 1])
            vector.tensor_copy(out=rstd2[:, ti:ti + 1], in_=AR[:, CL3 + 3:CL3 + 4])
            vector.tensor_scalar(out=AR[:, H23:H23 + 1024], in0=AR[:, XM3:XM3 + 1024],
                                 scalar1=AR[:, CL3:CL3 + 1], scalar2=AR[:, CL3 + 3:CL3 + 4],
                                 op0=ALU.subtract, op1=ALU.mult)
            vector.tensor_tensor(out=AR[:, H23:H23 + 1024], in0=AR[:, H23:H23 + 1024],
                                 in1=bc0[:, :], op=ALU.mult)
            vector.tensor_tensor(out=AR[:, H23:H23 + 1024], in0=AR[:, H23:H23 + 1024],
                                 in1=bc1[:, :], op=ALU.add)
            vector.tensor_reduce(out=AR[:, CL3 + 4:CL3 + 5], in_=AR[:, H23:H23 + 1024],
                                 axis=AX.X, op=ALU.max, apply_absolute_value=True)
            vector.tensor_tensor(out=hmax[:, :], in0=hmax[:, :], in1=AR[:, CL3 + 4:CL3 + 5], op=ALU.max)
            return inc(vector.nop(), "dve")
        me = E("vector", h2c)

        def spx(sync, ti=ti, me=me):
            sync.wait_ge(semh["dve"], me)
            sync.dma_start(out=sp_xm[ti * 128:(ti + 1) * 128, :],
                           in_=AR[:, XM3:XM3 + 1024]).then_inc(semh["st"], 16)
            return dinc("st")
        msp = E("sync", spx)

        def fin(vector, msp=msp):
            vector.wait_ge(semh["st"], msp)
            return inc(vector.nop(), "dve")
        sb3_done[ti] = E("vector", fin)

    SB3_DONE = sb3_done[NCH - 1]

    def hmx_fold(sync):
        sync.wait_ge(semh["dve"], SB3_DONE)
        sync.dma_start(out=sp_row[0:1, 0:128].rearrange("o (r q) -> (o r) q", q=1),
                       in_=hmax[:, :]).then_inc(semh["st"], 16)
        dinc("st")
        sync.wait_ge(semh["st"], cnt["st"])
        sync.dma_start(out=AR[0:1, 0:128], in_=sp_row[0:1, 0:128]).then_inc(semh["ld"], 16)
        return dinc("ld")
    m = E("sync", hmx_fold)

    def hmx_red(vector, m=m):
        vector.wait_ge(semh["ld"], m)
        vector.tensor_reduce(out=tiny[0:1, 0:1], in_=AR[0:1, 0:128], axis=AX.X, op=ALU.max)
        return inc(vector.nop(), "dve")
    m = E("vector", hmx_red)

    def cc1(gpsimd, m=m):
        gpsimd.wait_ge(semh["dve"], m)
        gpsimd.dma_start(out=ccb[0][:, :], in_=tiny[0:1, 0:1]).then_inc(semh["gdma"], 16)
        gpsimd.wait_ge(semh["gdma"], dinc("gdma"))
        gpsimd.collective_compute("AllReduce", ALU.max, replica_groups=[list(range(8))],
                                  ins=[ccb[0][:, :]], outs=[ccb[1][:, :]]).then_inc(semh["cc"], 1)
        gpsimd.wait_ge(semh["cc"], dinc("cc", 1))
        gpsimd.dma_start(out=tiny[0:1, 1:2], in_=ccb[1][:, :]).then_inc(semh["gdma"], 16)
        return dinc("gdma")
    m_cc1 = E("gpsimd", cc1)

    def sx_v(vector, m_cc1=m_cc1):
        vector.wait_ge(semh["gdma"], m_cc1)
        vector.tensor_tensor(out=tiny[0:1, 2:3], in0=tiny[0:1, 1:2], in1=sig_clip[0:1, 0:1], op=ALU.mult)
        vector.tensor_scalar(out=tiny[0:1, 2:3], in0=tiny[0:1, 2:3], scalar1=1.0 / 127.0,
                             scalar2=None, op0=ALU.mult, op1=ALU.bypass)
        vector.reciprocal(out=tiny[0:1, 3:4], in_=tiny[0:1, 2:3])
        return inc(vector.nop(), "dve")
    m = E("vector", sx_v)

    def sx_c(vector, m=m):
        vector.wait_ge(semh["dve"], m)
        if zb_last[0]:
            vector.wait_ge(semh["pe"], zb_last[0])
        vector.tensor_copy(out=zb[0:1, 0:1], in_=tiny[0:1, 3:4])
        vector.tensor_copy(out=zb[0:1, 1:2], in_=tiny[0:1, 2:3])
        return inc(vector.nop(), "dve")
    m = E("vector", sx_c)

    def sx_b(tensor, m=m):
        tensor.wait_ge(semh["dve"], m)
        b = take_bank(tensor, 1)[0]
        tensor.matmul(psb[b][:, 0:2], ones128[:, :], zb[:, 0:2], start=True, stop=True)
        zb_last[0] = cnt["pe"] + 1
        return b, inc(tensor.nop(), "pe")
    b, m = E("tensor", sx_b)

    def sx_e(vector, b=b, m=m):
        vector.wait_ge(semh["pe"], m)
        vector.tensor_copy(out=rsx_col[:, :], in_=psb[b][:, 0:1])
        vector.tensor_scalar(out=gsc_g[:, :], in0=swg_col[:, :], scalar1=psb[b][:, 1:2],
                             scalar2=None, op0=ALU.mult, op1=ALU.bypass)
        vector.tensor_scalar(out=gsc_u[:, :], in0=swu_col[:, :], scalar1=psb[b][:, 1:2],
                             scalar2=None, op0=ALU.mult, op1=ALU.bypass)
        v = inc(vector.nop(), "dve")
        consumed(b, "dve", v)
        return v
    M_SX = E("vector", sx_e)

    if KSTOP < 8:
        return _finish(nc, blockctx, stack, E, semh, cnt)

    WGQ1 = 0
    WUQ1 = 11264
    XM1, H21, V1, IC1, F1, R1 = 22528, 23552, 24576, 25600, 26624, 27648
    HQB1 = 28672
    GS1 = 31744
    GU1 = 32256

    def sc1_wld(sync):
        sync.wait_ge(semh["dve"], M_SX)
        sync.dma_start(out=ARB[:, 2 * WGQ1:2 * WGQ1 + DC * II], in_=sp_wgq[:, :]).then_inc(semh["ld"], 16)
        sync.dma_start(out=ARB[:, 2 * WUQ1:2 * WUQ1 + DC * II], in_=sp_wuq[:, :]).then_inc(semh["ld"], 16)
        dinc("ld", 32)
        return cnt["ld"]
    M_SC1W = E("sync", sc1_wld)

    def gum_init(vector):
        vector.memset(gumax[:, :], 0.0)
        return inc(vector.nop(), "dve")
    M_GU0 = E("vector", gum_init)

    sc1_done = {}
    for t5 in range(NT5):
        mq = None
        for q4 in range(4):
            ti = t5 * 4 + q4

            def ld(sync, ti=ti, t5=t5, q4=q4):
                if t5 == 0 and q4 == 0:
                    sync.wait_ge(semh["ld"], M_SC1W)
                    sync.wait_ge(semh["dve"], M_GU0)
                elif q4 == 0:
                    sync.wait_ge(semh["dve"], sc1_done[t5 - 1])
                else:
                    sync.wait_ge(semh["dve"], cnt["dve"])
                sync.dma_start(out=AR[:, XM1:XM1 + 1024],
                               in_=sp_xm[ti * 128:(ti + 1) * 128, :]).then_inc(semh["ld"], 16)
                return dinc("ld")
            mld = E("sync", ld)

            def qv(vector, mld=mld, ti=ti):
                vector.wait_ge(semh["ld"], mld)
                vector.tensor_scalar(out=AR[:, H21:H21 + 1024], in0=AR[:, XM1:XM1 + 1024],
                                     scalar1=mean2[:, ti:ti + 1], scalar2=rstd2[:, ti:ti + 1],
                                     op0=ALU.subtract, op1=ALU.mult)
                vector.tensor_tensor(out=AR[:, H21:H21 + 1024], in0=AR[:, H21:H21 + 1024],
                                     in1=bc0[:, :], op=ALU.mult)
                vector.tensor_tensor(out=AR[:, H21:H21 + 1024], in0=AR[:, H21:H21 + 1024],
                                     in1=bc1[:, :], op=ALU.add)
                vector.tensor_scalar(out=AR[:, V1:V1 + 1024], in0=AR[:, H21:H21 + 1024],
                                     scalar1=rsx_col[:, 0:1], scalar2=None, op0=ALU.mult, op1=ALU.bypass)
                icst = AR[:, IC1:IC1 + 1024].bitcast(I32)
                vector.tensor_copy(out=icst, in_=AR[:, V1:V1 + 1024])
                vector.tensor_copy(out=AR[:, R1:R1 + 1024], in_=icst)
                vector.tensor_tensor(out=AR[:, F1:F1 + 1024], in0=AR[:, V1:V1 + 1024],
                                     in1=AR[:, R1:R1 + 1024], op=ALU.subtract)
                vector.scalar_tensor_tensor(out=AR[:, V1:V1 + 1024], in0=AR[:, F1:F1 + 1024],
                                            scalar=0.5, in1=AR[:, R1:R1 + 1024], op0=ALU.is_gt, op1=ALU.add)
                vector.scalar_tensor_tensor(out=AR[:, R1:R1 + 1024], in0=AR[:, F1:F1 + 1024],
                                            scalar=-0.5, in1=AR[:, V1:V1 + 1024], op0=ALU.is_ge, op1=ALU.add)
                vector.tensor_scalar(out=AR[:, V1:V1 + 1024], in0=AR[:, R1:R1 + 1024],
                                     scalar1=-1.0, scalar2=127.0, op0=ALU.add, op1=ALU.min)
                vector.tensor_scalar(out=AR[:, V1:V1 + 1024], in0=AR[:, V1:V1 + 1024],
                                     scalar1=-128.0, scalar2=None, op0=ALU.max, op1=ALU.bypass)
                return inc(vector.nop(), "dve")
            ma = E("vector", qv)

            def qt(tensor, ma=ma):
                tensor.wait_ge(semh["dve"], ma)
                bs = take_bank(tensor, 2)
                for j in range(DC):
                    tensor.transpose(psb[bs[j // 4]][:, (j % 4) * 128:(j % 4) * 128 + 128],
                                     AR[:, V1 + j * 128:V1 + (j + 1) * 128], ident[:, :])
                return bs, inc(tensor.nop(), "pe")
            bs, mb = E("tensor", qt)

            def qe(vector, bs=bs, mb=mb, q4=q4):
                vector.wait_ge(semh["pe"], mb)
                for h in range(2):
                    vector.tensor_copy(out=ARB[:, 2 * HQB1 + q4 * 1024 + h * 512:
                                                2 * HQB1 + q4 * 1024 + (h + 1) * 512],
                                       in_=psb[bs[h]][:, 0:512])
                v = inc(vector.nop(), "dve")
                for b2 in bs:
                    consumed(b2, "dve", v)
                return v
            mq = E("vector", qe)

        mgu = None
        m_guv_prev = [None]
        for icx in range(IC):
            def gm(tensor, icx=icx, mq=mq):
                if icx == 0:
                    tensor.wait_ge(semh["dve"], mq)
                bs = take_bank(tensor, 2)
                for which, wb in ((0, WGQ1), (1, WUQ1)):
                    for kc in range(DC):
                        lhs = ARB[:, 2 * wb + kc * II + icx * 128:2 * wb + kc * II + (icx + 1) * 128]
                        rhs = ARB[:, 2 * HQB1:2 * HQB1 + 4096].rearrange("p (cc q) -> p cc q", q=1024)[:, :, kc * 128:(kc + 1) * 128]
                        tensor.matmul(psb[bs[which]][:, 0:512], lhs, rhs,
                                      start=(kc == 0), stop=(kc == DC - 1))
                return bs, inc(tensor.nop(), "pe")
            bs, ma = E("tensor", gm)

            def gev(scalar, bs=bs, ma=ma, icx=icx):
                scalar.wait_ge(semh["pe"], ma)
                if m_guv_prev[0] is not None:
                    scalar.wait_ge(semh["dve"], m_guv_prev[0])
                scalar.activation(out=AR[:, GS1:GS1 + 512], in_=psb[bs[0]][:, 0:512], func=AF.Silu,
                                  scale=gsc_g[:, icx:icx + 1])
                v = inc(scalar.nop(), "act")
                consumed(bs[0], "act", v)
                return v
            mb = E("scalar", gev)

            def guv(vector, bs=bs, mb=mb, icx=icx, t5=t5):
                vector.wait_ge(semh["act"], mb)
                vector.tensor_scalar(out=AR[:, GU1:GU1 + 512], in0=psb[bs[1]][:, 0:512],
                                     scalar1=gsc_u[:, icx:icx + 1], scalar2=None, op0=ALU.mult, op1=ALU.bypass)
                v = inc(vector.nop(), "dve")
                consumed(bs[1], "dve", v)
                vector.tensor_tensor(out=AR[:, GU1:GU1 + 512], in0=AR[:, GU1:GU1 + 512],
                                     in1=AR[:, GS1:GS1 + 512], op=ALU.mult)
                vector.tensor_reduce(out=AR[:, F1:F1 + 1], in_=AR[:, GU1:GU1 + 512],
                                     axis=AX.X, op=ALU.max, apply_absolute_value=True)
                vector.tensor_tensor(out=gumax[:, :], in0=gumax[:, :], in1=AR[:, F1:F1 + 1], op=ALU.max)
                return inc(vector.nop(), "dve")
            mc3 = E("vector", guv)
            m_guv_prev[0] = mc3

            def gsp(sync, icx=icx, t5=t5, mc3=mc3):
                sync.wait_ge(semh["dve"], mc3)
                sync.dma_start(out=sp_gu[:, icx * T + t5 * 512:icx * T + (t5 + 1) * 512],
                               in_=AR[:, GU1:GU1 + 512]).then_inc(semh["st"], 16)
                return dinc("st")
            mgu = E("sync", gsp)

        def fin(vector, mgu=mgu):
            vector.wait_ge(semh["st"], mgu)
            return inc(vector.nop(), "dve")
        sc1_done[t5] = E("vector", fin)

    SC1_DONE = sc1_done[NT5 - 1]

    def gmx_fold(sync):
        sync.wait_ge(semh["dve"], SC1_DONE)
        sync.dma_start(out=sp_row[0:1, 128:256].rearrange("o (r q) -> (o r) q", q=1),
                       in_=gumax[:, :]).then_inc(semh["st"], 16)
        dinc("st")
        sync.wait_ge(semh["st"], cnt["st"])
        sync.dma_start(out=AR[0:1, 0:128], in_=sp_row[0:1, 128:256]).then_inc(semh["ld"], 16)
        return dinc("ld")
    m = E("sync", gmx_fold)

    def gmx_red(vector, m=m):
        vector.wait_ge(semh["ld"], m)
        vector.tensor_reduce(out=tiny[0:1, 4:5], in_=AR[0:1, 0:128], axis=AX.X, op=ALU.max)
        return inc(vector.nop(), "dve")
    m = E("vector", gmx_red)

    def cc2(gpsimd, m=m):
        gpsimd.wait_ge(semh["dve"], m)
        gpsimd.dma_start(out=ccb[2][:, :], in_=tiny[0:1, 4:5]).then_inc(semh["gdma"], 16)
        gpsimd.wait_ge(semh["gdma"], dinc("gdma"))
        gpsimd.collective_compute("AllReduce", ALU.max, replica_groups=[list(range(8))],
                                  ins=[ccb[2][:, :]], outs=[ccb[3][:, :]]).then_inc(semh["cc"], 1)
        gpsimd.wait_ge(semh["cc"], dinc("cc", 1))
        gpsimd.dma_start(out=tiny[0:1, 5:6], in_=ccb[3][:, :]).then_inc(semh["gdma"], 16)
        return dinc("gdma")
    m_cc2 = E("gpsimd", cc2)

    def sd_v(vector, m_cc2=m_cc2):
        vector.wait_ge(semh["gdma"], m_cc2)
        vector.tensor_tensor(out=tiny[0:1, 6:7], in0=tiny[0:1, 5:6], in1=sig_clip[0:1, 1:2], op=ALU.mult)
        vector.tensor_scalar(out=tiny[0:1, 6:7], in0=tiny[0:1, 6:7], scalar1=1.0 / 127.0,
                             scalar2=None, op0=ALU.mult, op1=ALU.bypass)
        vector.reciprocal(out=tiny[0:1, 7:8], in_=tiny[0:1, 6:7])
        return inc(vector.nop(), "dve")
    m = E("vector", sd_v)

    def sd_c(vector, m=m):
        vector.wait_ge(semh["dve"], m)
        if zb_last[0]:
            vector.wait_ge(semh["pe"], zb_last[0])
        vector.tensor_copy(out=zb[0:1, 0:1], in_=tiny[0:1, 7:8])
        vector.tensor_copy(out=zb[0:1, 1:2], in_=tiny[0:1, 6:7])
        return inc(vector.nop(), "dve")
    m = E("vector", sd_c)

    def sd_b(tensor, m=m):
        tensor.wait_ge(semh["dve"], m)
        b = take_bank(tensor, 1)[0]
        tensor.matmul(psb[b][:, 0:2], ones128[:, :], zb[:, 0:2], start=True, stop=True)
        return b, inc(tensor.nop(), "pe")
    b, m = E("tensor", sd_b)

    def sd_e(vector, b=b, m=m):
        vector.wait_ge(semh["pe"], m)
        vector.tensor_copy(out=rsd_col[:, :], in_=psb[b][:, 0:1])
        vector.tensor_scalar(out=dsc[:, :], in0=swd_col[:, :], scalar1=psb[b][:, 1:2],
                             scalar2=None, op0=ALU.mult, op1=ALU.bypass)
        v = inc(vector.nop(), "dve")
        consumed(b, "dve", v)
        return v
    M_SD = E("vector", sd_e)

    if KSTOP < 9:
        return _finish(nc, blockctx, stack, E, semh, cnt)

    M_GBB = emit_bcld(bc2, 5, dep_s="dve", dep_v=M_SD)

    WDQ2 = 0
    GUT2 = 11264
    GUQ2 = 22528
    V2, IC2, F2, R2 = 28160, 29184, 30208, 31232
    XMT2 = GUT2
    OTM2 = GUT2 + 4096
    OT2 = GUT2 + 8192

    def sc2_wld(sync):
        sync.wait_ge(semh["dve"], M_GBB)
        sync.dma_start(out=ARB[:, 2 * WDQ2:2 * WDQ2 + IC * D], in_=sp_wdq[:, :]).then_inc(semh["ld"], 16)
        return dinc("ld")
    M_SC2W = E("sync", sc2_wld)

    sc2_done = {}
    m_out_final = None
    for t5 in range(NT5):
        def ld(sync, t5=t5):
            if t5 > 0:
                sync.wait_ge(semh["dve"], sc2_done[t5 - 1])
            else:
                sync.wait_ge(semh["ld"], M_SC2W)
            sync.dma_start(out=AR[:, GUT2:GUT2 + IC * 512].rearrange("p (k t) -> p k t", t=512),
                           in_=sp_gu.rearrange("p (k t) -> p k t", t=T)[:, :, t5 * 512:(t5 + 1) * 512]
                           ).then_inc(semh["ld"], 16)
            return dinc("ld")
        mld = E("sync", ld)

        mq = None
        for half in range(11):
            def qv2(vector, half=half, mld=mld):
                if half == 0:
                    vector.wait_ge(semh["ld"], mld)
                base = GUT2 + half * 1024
                vector.tensor_scalar(out=AR[:, V2:V2 + 1024], in0=AR[:, base:base + 1024],
                                     scalar1=rsd_col[:, 0:1], scalar2=None, op0=ALU.mult, op1=ALU.bypass)
                icst = AR[:, IC2:IC2 + 1024].bitcast(I32)
                vector.tensor_copy(out=icst, in_=AR[:, V2:V2 + 1024])
                vector.tensor_copy(out=AR[:, R2:R2 + 1024], in_=icst)
                vector.tensor_tensor(out=AR[:, F2:F2 + 1024], in0=AR[:, V2:V2 + 1024],
                                     in1=AR[:, R2:R2 + 1024], op=ALU.subtract)
                vector.scalar_tensor_tensor(out=AR[:, V2:V2 + 1024], in0=AR[:, F2:F2 + 1024],
                                            scalar=0.5, in1=AR[:, R2:R2 + 1024], op0=ALU.is_gt, op1=ALU.add)
                vector.scalar_tensor_tensor(out=AR[:, R2:R2 + 1024], in0=AR[:, F2:F2 + 1024],
                                            scalar=-0.5, in1=AR[:, V2:V2 + 1024], op0=ALU.is_ge, op1=ALU.add)
                vector.tensor_scalar(out=AR[:, V2:V2 + 1024], in0=AR[:, R2:R2 + 1024],
                                     scalar1=-1.0, scalar2=127.0, op0=ALU.add, op1=ALU.min)
                vector.tensor_scalar(out=ARB[:, 2 * GUQ2 + half * 1024:2 * GUQ2 + (half + 1) * 1024],
                                     in0=AR[:, V2:V2 + 1024],
                                     scalar1=-128.0, scalar2=None, op0=ALU.max, op1=ALU.bypass)
                return inc(vector.nop(), "dve")
            mq = E("vector", qv2)

        def xld(sync, t5=t5, mq=mq):
            sync.wait_ge(semh["dve"], mq)
            sync.dma_start(out=AR[:, XMT2:XMT2 + 4096].rearrange("p (cc d) -> p cc d", d=1024),
                           in_=sp_xm[t5 * 512:(t5 + 1) * 512, :].rearrange("(cc r) d -> r cc d", r=128)
                           ).then_inc(semh["ld"], 16)
            return dinc("ld")
        mxl = E("sync", xld)

        for mo in range(8):
            def dm(tensor, mo=mo, mq=mq):
                if mo == 0:
                    tensor.wait_ge(semh["dve"], mq)
                b = take_bank(tensor, 1)[0]
                for kc in range(IC):
                    lhs = ARB[:, 2 * WDQ2 + kc * D + mo * 128:2 * WDQ2 + kc * D + (mo + 1) * 128]
                    rhs = ARB[:, 2 * GUQ2 + kc * 512:2 * GUQ2 + (kc + 1) * 512]
                    tensor.matmul(psb[b][:, 0:512], lhs, rhs, start=(kc == 0), stop=(kc == IC - 1))
                return b, inc(tensor.nop(), "pe")
            bd, ma = E("tensor", dm)

            def de(vector, bd=bd, ma=ma, mo=mo):
                vector.wait_ge(semh["pe"], ma)
                vector.tensor_scalar(out=AR[:, OT2:OT2 + 512], in0=psb[bd][:, 0:512],
                                     scalar1=dsc[:, mo:mo + 1], scalar2=None, op0=ALU.mult, op1=ALU.bypass)
                v = inc(vector.nop(), "dve")
                consumed(bd, "dve", v)
                return v
            mb = E("vector", de)

            def dt(tensor, mb=mb):
                tensor.wait_ge(semh["dve"], mb)
                bs = take_bank(tensor, 2)
                for cc in range(4):
                    tensor.transpose(psb[bs[cc // 2]][:, (cc % 2) * 128:(cc % 2) * 128 + 128],
                                     AR[:, OT2 + cc * 128:OT2 + (cc + 1) * 128], ident[:, :])
                return bs, inc(tensor.nop(), "pe")
            bs, mc4 = E("tensor", dt)

            def dv(vector, bs=bs, mc4=mc4, mo=mo, mxl=mxl):
                vector.wait_ge(semh["pe"], mc4)
                if mo == 0:
                    vector.wait_ge(semh["ld"], mxl)
                for cc in range(4):
                    ps = psb[bs[cc // 2]][:, (cc % 2) * 128:(cc % 2) * 128 + 128]
                    dst = AR[:, OTM2 + cc * 1024 + mo * 128:OTM2 + cc * 1024 + (mo + 1) * 128]
                    xm = AR[:, XMT2 + cc * 1024 + mo * 128:XMT2 + cc * 1024 + (mo + 1) * 128]
                    vector.tensor_tensor(out=dst, in0=ps, in1=bc2[:, mo * 128:(mo + 1) * 128], op=ALU.mult)
                    vector.tensor_tensor(out=dst, in0=dst, in1=xm, op=ALU.add)
                v = inc(vector.nop(), "dve")
                for b2 in bs:
                    consumed(b2, "dve", v)
                return v
            mfin = E("vector", dv)

        def outdma(sync, t5=t5, mfin=mfin):
            sync.wait_ge(semh["dve"], mfin)
            sync.dma_start(out=out_h[t5 * 512:(t5 + 1) * 512, :].rearrange("(cc r) d -> r cc d", r=128),
                           in_=AR[:, OTM2:OTM2 + 4096].rearrange("p (cc d) -> p cc d", d=1024)
                           ).then_inc(semh["st"], 16)
            return dinc("st")
        mo_dma = E("sync", outdma)
        m_out_final = mo_dma

        def fin(vector, mo_dma=mo_dma):
            vector.wait_ge(semh["st"], mo_dma)
            return inc(vector.nop(), "dve")
        sc2_done[t5] = E("vector", fin)

    def finale(sync):
        sync.wait_ge(semh["st"], m_out_final)
    E("sync", finale)

    blockctx.__exit__(None, None, None)
    stack.close()
    return nc


_NC_CACHE = [None]


def kernel(**inputs):
    mid = 64
    s = np.arange(128)
    csrel = -(((s[:, None] <= np.arange(128)[None, :]).astype(np.float32))
              - ((s[:, None] <= mid).astype(np.float32)))
    cscols = np.stack([-(s <= mid).astype(np.float32),
                       -np.ones(128, np.float32),
                       -(s > mid).astype(np.float32)], axis=1).astype(np.float32)
    e2b = np.float32(np.exp(2 * EXPB))
    r = s[:, None]
    cidx = np.arange(128)[None, :]
    msk_xu = ((cidx > r).astype(np.float32) * e2b).astype(np.float32)
    msk_xl = ((cidx < r).astype(np.float32) * e2b).astype(np.float32)
    msk_m2 = ((cidx >= r).astype(np.float32) * e2b).astype(np.float32)
    hsel = np.zeros((128, 128), np.float32)
    hselT = np.zeros((16, 8 * 128), np.float32)
    for j in range(8):
        hsel[0:64, j * 16 + 2 * j] = 1.0
        hsel[64:128, j * 16 + 2 * j + 1] = 1.0
        hselT[2 * j, j * 128:j * 128 + 64] = 1.0
        hselT[2 * j + 1, j * 128 + 64:j * 128 + 128] = 1.0
    ident = np.eye(128, dtype=np.float32)
    ones1 = np.ones((1, 128), np.float32)
    ones64c = np.ones((64, 1), np.float32)
    cpool = np.zeros((128, 8), np.float32)
    cpool[:, 0] = 1e-6
    cpool[:, 1] = 1e-5
    cpool[:, 2] = 1.0
    cpool[:, 3] = -EXPB
    cpool[:, 4] = EXPB

    if _NC_CACHE[0] is None:
        _NC_CACHE[0] = build()
    nc = _NC_CACHE[0]

    x = np.ascontiguousarray(inputs["L0_attn_x"], dtype=np.float32)
    c_all = np.ascontiguousarray(inputs["L0_c"], dtype=np.float32)
    clip2 = np.stack([inputs["L0_ffn_clip_x"][0], inputs["L0_ffn_clip_down"][0]]).reshape(1, 2).astype(np.float32)

    common = dict(
        clip2=clip2,
        wmod=np.ascontiguousarray(inputs["L0_wmod"], np.float32),
        norm_in_w=np.ascontiguousarray(np.tile(np.asarray(inputs["L0_attn_norm_in_w"], np.float32).reshape(1, D), (128, 1))),
        conv_w=np.ascontiguousarray(np.transpose(np.asarray(inputs["L0_attn_conv_w"], np.float32)[:, 0, :].reshape(8, 128, 4), (1, 0, 2)).reshape(128, 32)),
        conv_b=np.ascontiguousarray(np.asarray(inputs["L0_attn_conv_b"], np.float32).reshape(8, 128).T),
        q_w=np.ascontiguousarray(inputs["L0_attn_q_w"], np.float32),
        k_w=np.ascontiguousarray(inputs["L0_attn_k_w"], np.float32),
        v_w=np.ascontiguousarray(inputs["L0_attn_v_w"], np.float32),
        alpha_w=np.ascontiguousarray(inputs["L0_attn_alpha_w"], np.float32),
        alpha_b=np.ascontiguousarray(np.tile(np.asarray(inputs["L0_attn_alpha_b"], np.float32).reshape(1, D), (128, 1))),
        beta_w=np.ascontiguousarray(inputs["L0_attn_beta_w"], np.float32),
        beta_b=np.ascontiguousarray(np.tile(np.asarray(inputs["L0_attn_beta_b"], np.float32).reshape(1, 16), (128, 1))),
        head_norm_w=np.ascontiguousarray(np.asarray(inputs["L0_attn_head_norm_w"], np.float32).T),
        u1_w=np.ascontiguousarray(inputs["L0_attn_u1_w"], np.float32),
        u1_b=np.asarray(inputs["L0_attn_u1_b"], np.float32).reshape(64, 1),
        u2_w=np.ascontiguousarray(inputs["L0_attn_u2_w"], np.float32),
        u2_b=np.ascontiguousarray(np.asarray(inputs["L0_attn_u2_b"], np.float32).reshape(8, 128).T),
        ff_norm_w=np.ascontiguousarray(np.tile(np.asarray(inputs["L0_attn_ff_norm_w"], np.float32).reshape(1, D), (128, 1))),
        ff_w1=np.ascontiguousarray(inputs["L0_attn_ff_w1"], np.float32),
        ff_w3=np.ascontiguousarray(inputs["L0_attn_ff_w3"], np.float32),
        ff_w2=np.ascontiguousarray(inputs["L0_attn_ff_w2"], np.float32),
        out_w=np.ascontiguousarray(inputs["L0_attn_out_w"], np.float32),
        wgT=np.ascontiguousarray(inputs["L0_ffn_w_gate"].T, np.float32),
        wuT=np.ascontiguousarray(inputs["L0_ffn_w_up"].T, np.float32),
        wdT=np.ascontiguousarray(inputs["L0_ffn_w_down"].T, np.float32),
        ident=ident, csrel=csrel.astype(np.float32), cscols=cscols,
        msk_xu=msk_xu, msk_xl=msk_xl, msk_m2=msk_m2,
        hsel=hsel, hselT=hselT, ones1=ones1, ones64c=ones64c, cpool=cpool,
        ones128=np.ones((128, 128), np.float32),
    )
    in_maps = []
    for b in range(8):
        m = dict(common)
        m["x"] = np.ascontiguousarray(x[b])
        m["cvec"] = np.ascontiguousarray(c_all[b].reshape(8, 128).T)
        in_maps.append(m)

    res = run_bass_kernel_spmd(nc, in_maps, core_ids=list(range(8)))
    globals()["_LAST_RES"] = res
    out = np.stack([res.results[b]["out"] for b in range(8)], axis=0)
    return out


# version 09 (final)
# speedup vs baseline: 1.0120x; 1.0120x over previous
import contextlib
import numpy as np
import concourse.bass as bass
import concourse.mybir as mybir
from concourse.bass_utils import run_bass_kernel_spmd

F32 = mybir.dt.float32
BF16 = mybir.dt.bfloat16
I32 = mybir.dt.int32
AF = mybir.ActivationFunctionType
ALU = mybir.AluOpType
AX = mybir.AxisListType

D = 1024
T = 8192
HN = 16
C = 128
NCH = T // C
DC = D // 128
FF = 4096
II = 2816
IC = II // 128
EXPB = 22.0
E2B = float(np.exp(2 * EXPB))
NT5 = T // 512



def _finish(nc, blockctx, stack, E, semh, cnt):
    def fin(sync):
        sync.wait_ge(semh["st"], cnt["st"])
        sync.wait_ge(semh["ld"], cnt["ld"])
    E("sync", fin)
    blockctx.__exit__(None, None, None)
    stack.close()
    return nc


def build():
    nc = bass.Bass("TRN2", target_bir_lowering=False, detect_race_conditions=False)

    def din(name, shape, dtype=F32):
        return nc.dram_tensor(name, shape, dtype, kind="ExternalInput").ap()

    x_h = din("x", [T, D])
    cvec_h = din("cvec", [128, 8])
    clip_h = din("clip2", [1, 2])
    wmod_h = din("wmod", [D, 6 * D])
    winw_h = din("norm_in_w", [128, 1024])
    convw_h = din("conv_w", [128, 32])
    convb_h = din("conv_b", [128, 8])
    qw_h = din("q_w", [D, D]); kw_h = din("k_w", [D, D]); vw_h = din("v_w", [D, D])
    aw_h = din("alpha_w", [D, D]); ab_h = din("alpha_b", [128, 1024])
    bw_h = din("beta_w", [D, HN]); bb_h = din("beta_b", [128, HN])
    hnw_h = din("head_norm_w", [64, HN])
    u1w_h = din("u1_w", [D, 64]); u1b_h = din("u1_b", [64, 1])
    u2w_h = din("u2_w", [64, D]); u2b_h = din("u2_b", [128, 8])
    ffnw_h = din("ff_norm_w", [128, 1024])
    fw1_h = din("ff_w1", [D, FF]); fw3_h = din("ff_w3", [D, FF]); fw2_h = din("ff_w2", [FF, D])
    outw_h = din("out_w", [D, D])
    wgT_h = din("wgT", [D, II]); wuT_h = din("wuT", [D, II]); wdT_h = din("wdT", [II, D])
    ident_h = din("ident", [128, 128])
    csrel_h = din("csrel", [128, 128])
    cscols_h = din("cscols", [128, 3])
    mxu_h = din("msk_xu", [128, 128])
    mxl_h = din("msk_xl", [128, 128])
    mm2_h = din("msk_m2", [128, 128])
    hsel_h = din("hsel", [128, 128])
    hselT_h = din("hselT", [16, 8 * 128])
    ones1_h = din("ones1", [1, 128])
    ones128_h = din("ones128", [128, 128])
    o64c_h = din("ones64c", [64, 1])
    cpool_h = din("cpool", [128, 8])

    out_h = nc.dram_tensor("out", [T, D], F32, kind="ExternalOutput").ap()

    import os as _os

    def scr(name, shape, dtype=F32):
        if name in _os.environ.get("KSPO", "").split(","):
            return nc.dram_tensor(name, shape, dtype, kind="ExternalOutput").ap()
        return nc.dram_tensor(name, shape, dtype).ap()

    sp_h1 = scr("sp_h1", [128, NCH * 1024])
    sp_hc = scr("sp_hc", [128, NCH * 1024])
    sp_sg = scr("sp_sg", [64, T])
    sp_on = scr("sp_on", [64, NCH * 2048])
    sp_qf = scr("sp_qf", [128, NCH * 1024])
    sp_kf = scr("sp_kf", [128, NCH * 1024])
    sp_vt = scr("sp_vt", [128, NCH * 1024])
    sp_sp = scr("sp_sp", [128, NCH * 1024])
    sp_bt = scr("sp_bt", [128, NCH * 16])
    sp_y1 = scr("sp_y1", [128, NCH * 1024])
    sp_y2 = scr("sp_y2", [128, NCH * 1024])
    sp_fp = [scr(f"sp_fp{g}", [128, NCH * 1024]) for g in range(4)]
    sp_xm = scr("sp_xm", [T, D])
    sp_gu = scr("sp_gu", [128, IC * T])
    sp_wgq = scr("sp_wgq", [128, DC * II], BF16)
    sp_wuq = scr("sp_wuq", [128, DC * II], BF16)
    sp_wdq = scr("sp_wdq", [128, IC * D], BF16)
    ccb = [scr(f"ccb{i}", [1, 1]) for i in range(4)]
    sp_row = scr("sp_row", [1, 4096])
    DEBUG_SCAN = _os.environ.get("KDBG", "0") == "1"
    KSTOP = int(_os.environ.get("KSTOP", "99"))
    KS0 = int(_os.environ.get("KS0", "99"))
    sp_dbg = scr("sp_dbg", [128, 16384]) if DEBUG_SCAN else None
    sp_mod = scr("sp_mod", [1, 48 * 128])
    sp_bc = scr("sp_bc", [128, 9 * 1024])

    _rownb = {}
    _ESZ = {F32: 4, I32: 4, BF16: 2}

    def sbt(name, shape, dtype=F32):
        _rownb[name] = shape[1] * _ESZ[dtype]
        return nc.alloc_sbuf_tensor(name, shape, dtype).ap()

    ident = sbt("identS", [128, 128])
    csrel = sbt("csrelS", [128, 128])
    cscols = sbt("cscolsS", [128, 3])
    mxu = sbt("mxuS", [128, 128]); mxl = sbt("mxlS", [128, 128]); mm2 = sbt("mm2S", [128, 128])
    hsel = sbt("hselS", [128, 128]); hselT = sbt("hselTS", [16, 8 * 128])
    ones1 = sbt("ones1S", [1, 128]); o64c = sbt("o64cS", [64, 1])
    ones128 = sbt("ones128S", [128, 128])
    zb = sbt("zbS", [128, 1024])
    bb_tile = sbt("bbtileS", [128, HN])
    cpool = sbt("cpoolS", [128, 8])
    modc = sbt("modcS", [128, 48])
    cb_cols = sbt("cbcolsS", [128, 8]); u2b_cols = sbt("u2bcolsS", [128, 8])
    u1b_col = sbt("u1bcolS", [64, 1])
    hnw_cols = sbt("hnwcolsS", [64, 16])
    cw_cols = sbt("cwcolsS", [128, 32])
    swg_col = sbt("swgcolS", [128, IC]); swu_col = sbt("swucolS", [128, IC]); swd_col = sbt("swdcolS", [128, 8])
    mean2 = sbt("mean2S", [128, NCH]); rstd2 = sbt("rstd2S", [128, NCH])
    hmax = sbt("hmaxS", [128, 1]); gumax = sbt("gumaxS", [128, 1])
    sig_clip = sbt("sigclipS", [1, 2])
    rsx_col = sbt("rsxcolS", [128, 1]); rsd_col = sbt("rsdcolS", [128, 1])
    gsc_g = sbt("gscgS", [128, IC]); gsc_u = sbt("gscuS", [128, IC]); dsc = sbt("dscS", [128, 8])
    bc0 = sbt("bc0S", [128, 1024]); bc1 = sbt("bc1S", [128, 1024]); bc2 = sbt("bc2S", [128, 1024])
    tiny = sbt("tinyS", [128, 16])

    AR_COLS = 45056
    AR = nc.alloc_sbuf_tensor("arena", [128, AR_COLS], F32).ap()
    ARB = AR.bitcast(BF16)
    _rownb["arena"] = AR_COLS * 4

    psb = [nc.alloc_psum_tensor(f"psb{i}", [128, 512], F32).ap() for i in range(8)]
    for i in range(8):
        _rownb[f"psb{i}"] = 512 * 4
    _APCLS = AR.__class__

    stack = contextlib.ExitStack()
    semh = {}
    cnt = {}
    for s in ("ld", "st", "pe", "dve", "act", "gdma", "cc"):
        semh[s] = stack.enter_context(nc.semaphore("sem_" + s))
        cnt[s] = 0

    blockctx = nc.Block()
    block = blockctx.__enter__()

    _HZL = 384
    _wr = {"vector": [], "scalar": []}
    _cum = {"vector": 0, "scalar": 0}
    _TRACK = ("tensor_tensor", "tensor_scalar", "scalar_tensor_tensor", "tensor_copy",
              "tensor_reduce", "reciprocal", "memset", "activation", "iota")

    def _rng(ap):
        name = ap.name
        rb = _rownb.get(name)
        sh = ap.shape
        if rb is None or len(sh) != 2:
            return (name, 0, 128, 0, 1 << 30, False)
        esz = _ESZ[ap.dtype]
        rl = rb // esz
        off = ap.offset
        p0 = off // rl
        b0 = (off % rl) * esz
        return (name, p0, p0 + sh[0], b0, b0 + sh[1] * esz, True)

    class _EWrap:
        def __init__(self, t, eng):
            self._t = t
            self._eng = eng

        def nop(self):
            return self.drain()

        def drain(self):
            if self._eng in _wr:
                _wr[self._eng].clear()
            return self._t.drain()

        def __getattr__(self, n):
            f = getattr(self._t, n)
            eng = self._eng
            if eng not in _wr or n not in _TRACK:
                return f

            def call(*a, **k):
                rs, ws = [], []
                if n == "memset":
                    ws.append(_rng(a[0] if a else k["out"]))
                else:
                    for kw, v in k.items():
                        if isinstance(v, _APCLS):
                            (ws if kw == "out" else rs).append(_rng(v))
                    for v in a:
                        if isinstance(v, _APCLS):
                            rs.append(_rng(v))
                wl = _wr[eng]
                cum = _cum[eng]
                wl[:] = [w for w in wl if cum - w[6] < (_HZL if w[5] else 16384)]
                hz = False
                for (nm, p0, p1, b0, b1, ex) in rs:
                    for (wn, wp0, wp1, wb0, wb1, wex, wce) in wl:
                        if nm != wn or p1 <= wp0 or wp1 <= p0 or b1 <= wb0 or wb1 <= b0:
                            continue
                        if not (ex and wex) or (cum - wce) + (wb1 - b0) < _HZL:
                            hz = True
                            break
                    if hz:
                        break
                if hz:
                    self.drain()
                width = 32
                for r in rs + ws:
                    width = max(width, (r[4] - r[3]) if r[5] else 4096)
                _cum[eng] = cum = cum + width
                for w in ws:
                    wl.append(w + (cum,))
                return f(*a, **k)
            return call

    def E(eng, fn):
        box = [None]

        def wrap(e):
            box[0] = fn(_EWrap(e, eng) if eng in ("tensor", "vector", "scalar") else e)
        getattr(block, eng)(wrap)
        return box[0]

    def inc(instr, s, k=1):
        cnt[s] += k
        instr.then_inc(semh[s], k)
        return cnt[s]

    def dinc(s, k=16):
        cnt[s] += k
        return cnt[s]

    bank_last = [None] * 8
    bank_rr = [0]

    def take_bank(eng, nb=1):
        out = []
        for _ in range(nb):
            b = bank_rr[0] % 8
            bank_rr[0] += 1
            if bank_last[b] is not None:
                s, v = bank_last[b]
                eng.wait_ge(semh[s], v)
                bank_last[b] = None
            out.append(b)
        return out

    def consumed(b, s, v):
        bank_last[b] = (s, v)

    A0, A1, A2, A3, A5 = 0, 22528, 25344, 28160, 30976
    VR0, MR0 = 35800, 38880
    mrows = AR[0:1, MR0:MR0 + 6144]
    A4 = A1
    A6 = 33792

    def s0_loads(sync):
        for dst, src in [(ident, ident_h), (csrel, csrel_h), (cscols, cscols_h),
                         (mxu, mxu_h), (mxl, mxl_h), (mm2, mm2_h), (hsel, hsel_h),
                         (hselT, hselT_h), (ones1, ones1_h), (o64c, o64c_h), (cpool, cpool_h)]:
            sync.dma_start(out=dst[:, :], in_=src[:, :]).then_inc(semh["ld"], 16)
            dinc("ld")
        for dst, srcw in [(cb_cols, convb_h), (u2b_cols, u2b_h), (u1b_col, u1b_h),
                          (bb_tile, bb_h), (hnw_cols, hnw_h), (cw_cols, convw_h),
                          (sig_clip, clip_h), (ones128, ones128_h)]:
            sync.dma_start(out=dst[:, :], in_=srcw[:, :]).then_inc(semh["ld"], 16)
            dinc("ld")
        sync.dma_start(out=tiny[:, 0:8], in_=cvec_h[:, :]).then_inc(semh["ld"], 16)
        for _ in range(1):
            dinc("ld")
    E("sync", s0_loads)
    LD0 = cnt["ld"]

    def s0_silc_a(scalar):
        scalar.wait_ge(semh["ld"], LD0)
        scalar.activation(out=tiny[:, 8:16], in_=tiny[:, 0:8], func=AF.Exp, scale=-1.0)
        scalar.activation(out=sig_clip[:, :], in_=sig_clip[:, :], func=AF.Exp, scale=-1.0)
        return inc(scalar.nop(), "act")
    m = E("scalar", s0_silc_a)

    def s0_silc_v(vector, m=m):
        vector.wait_ge(semh["act"], m)
        vector.tensor_scalar(out=tiny[:, 8:16], in0=tiny[:, 8:16], scalar1=1.0, scalar2=None,
                             op0=ALU.add, op1=ALU.bypass)
        vector.reciprocal(out=tiny[:, 8:16], in_=tiny[:, 8:16])
        vector.tensor_tensor(out=tiny[:, 8:16], in0=tiny[:, 0:8], in1=tiny[:, 8:16], op=ALU.mult)
        vector.tensor_scalar(out=sig_clip[:, :], in0=sig_clip[:, :], scalar1=1.0, scalar2=None,
                             op0=ALU.add, op1=ALU.bypass)
        vector.reciprocal(out=sig_clip[:, :], in_=sig_clip[:, :])
        vector.tensor_scalar(out=u2b_cols[:, :], in0=u2b_cols[:, :], scalar1=-1.0, scalar2=None,
                             op0=ALU.mult, op1=ALU.bypass)
        return inc(vector.nop(), "dve")
    M_SILC = E("vector", s0_silc_v)

    if KS0 < 2:
        return _finish(nc, blockctx, stack, E, semh, cnt)

    pe_wm = {}
    for kc in range(DC):
        def s0_wm_ld(sync, kc=kc):
            if kc >= 2:
                sync.wait_ge(semh["dve"], pe_wm[kc - 2])
            sync.dma_start(out=AR[:, A0 + (kc % 2) * 6144:A0 + (kc % 2) * 6144 + 6144],
                           in_=wmod_h[kc * 128:(kc + 1) * 128, :]).then_inc(semh["ld"], 16)
            return dinc("ld")
        mld = E("sync", s0_wm_ld)

        def s0_wm_mm(tensor, kc=kc, mld=mld):
            tensor.wait_ge(semh["ld"], mld)
            if kc == 0:
                tensor.wait_ge(semh["dve"], M_SILC)
            b = take_bank(tensor, 1)[0]
            for mc in range(48):
                base = A0 + (kc % 2) * 6144 + mc * 128
                tensor.matmul(psb[b][:, mc:mc + 1], AR[:, base:base + 128],
                              tiny[:, 8 + kc:9 + kc], start=True, stop=True)
            return b, inc(tensor.nop(), "pe")
        bwm, m_wmk = E("tensor", s0_wm_mm)
        pe_wm[kc] = m_wmk

        def s0_wm_acc(vector, kc=kc, bwm=bwm, m_wmk=m_wmk):
            vector.wait_ge(semh["pe"], m_wmk)
            if kc == 0:
                vector.tensor_copy(out=modc[:, :], in_=psb[bwm][:, 0:48])
            else:
                vector.tensor_tensor(out=modc[:, :], in0=modc[:, :], in1=psb[bwm][:, 0:48], op=ALU.add)
            v = inc(vector.nop(), "dve")
            consumed(bwm, "dve", v)
            return v
        pe_wm[kc] = E("vector", s0_wm_acc)

    if KS0 < 3:
        return _finish(nc, blockctx, stack, E, semh, cnt)

    def s0_modc(vector):
        vector.wait_ge(semh["dve"], pe_wm[DC - 1])
        return inc(vector.nop(), "dve")
    m = E("vector", s0_modc)

    def s0_gabs(scalar, m=m):
        scalar.wait_ge(semh["dve"], m)
        scalar.activation(out=AR[:, A1:A1 + 8], in_=modc[:, 16:24], func=AF.Abs)
        scalar.activation(out=AR[:, A1 + 16:A1 + 24], in_=modc[:, 40:48], func=AF.Abs)
        return inc(scalar.nop(), "act")
    m = E("scalar", s0_gabs)

    def s0_gab(vector, m=m):
        vector.wait_ge(semh["act"], m)
        for base, aoff in ((16, 0), (40, 16)):
            vector.tensor_scalar(out=AR[:, A1 + aoff:A1 + aoff + 8], in0=AR[:, A1 + aoff:A1 + aoff + 8],
                                 scalar1=0.25, scalar2=None, op0=ALU.max, op1=ALU.bypass)
            vector.tensor_scalar(out=AR[:, A1 + 8 + aoff:A1 + 16 + aoff], in0=modc[:, base:base + 8],
                                 scalar1=0.0, scalar2=2.0, op0=ALU.is_ge, op1=ALU.mult)
            vector.tensor_scalar(out=AR[:, A1 + 8 + aoff:A1 + 16 + aoff], in0=AR[:, A1 + 8 + aoff:A1 + 16 + aoff],
                                 scalar1=-1.0, scalar2=None, op0=ALU.add, op1=ALU.bypass)
            vector.tensor_tensor(out=modc[:, base:base + 8], in0=AR[:, A1 + aoff:A1 + aoff + 8],
                                 in1=AR[:, A1 + 8 + aoff:A1 + 16 + aoff], op=ALU.mult)
        return inc(vector.nop(), "dve")
    m = E("vector", s0_gab)

    def s0_modb(tensor, m=m):
        tensor.wait_ge(semh["dve"], m)
        b = take_bank(tensor, 1)[0]
        tensor.transpose(psb[b][0:48, 0:128], modc[:, :], ident[:, :])
        return b, inc(tensor.nop(), "pe")
    b, m = E("tensor", s0_modb)

    def s0_modre(vector, b=b, m=m):
        vector.wait_ge(semh["pe"], m)
        vector.tensor_copy(out=AR[0:48, MR0:MR0 + 128], in_=psb[b][0:48, 0:128])
        v = inc(vector.nop(), "dve")
        consumed(b, "dve", v)
        return v
    m = E("vector", s0_modre)

    def s0_modd(sync, m=m):
        sync.wait_ge(semh["dve"], m)
        sync.dma_start(out=sp_mod[0:1, :].rearrange("o (r q) -> (o r) q", q=128),
                       in_=AR[0:48, MR0:MR0 + 128]).then_inc(semh["st"], 16)
        dinc("st")
        sync.wait_ge(semh["st"], cnt["st"])
        sync.dma_start(out=mrows[:, :], in_=sp_mod[:, :]).then_inc(semh["ld"], 16)
        return dinc("ld")
    m = E("sync", s0_modd)

    def s0_modr(vector, m=m):
        vector.wait_ge(semh["ld"], m)
        return inc(vector.nop(), "dve")
    M_MODR = E("vector", s0_modr)

    def zb_init(vector):
        vector.memset(zb[:, :], 0.0)
        return inc(vector.nop(), "dve")
    M_ZB = E("vector", zb_init)
    zb_last = [0]

    def emit_bcast(dst, roff, addone, dep_s="dve", dep_v=None):
        dep = dep_v if dep_v is not None else M_MODR

        def cstep(vector):
            vector.wait_ge(semh[dep_s], dep)
            vector.wait_ge(semh["dve"], M_ZB)
            if zb_last[0]:
                vector.wait_ge(semh["pe"], zb_last[0])
            vector.tensor_copy(out=zb[0:1, 0:1024], in_=mrows[0:1, roff * 128:(roff + 8) * 128])
            return inc(vector.nop(), "dve")
        mc0 = E("vector", cstep)

        def tstep(tensor):
            tensor.wait_ge(semh["dve"], mc0)
            bs = take_bank(tensor, 2)
            for j in range(8):
                tensor.matmul(psb[bs[j // 4]][:, (j % 4) * 128:(j % 4) * 128 + 128],
                              ones128[:, :], zb[:, j * 128:(j + 1) * 128], start=True, stop=True)
            return bs, inc(tensor.nop(), "pe")
        bs, m2 = E("tensor", tstep)
        zb_last[0] = m2

        def vstep(vector):
            vector.wait_ge(semh["pe"], m2)
            for h in range(2):
                if addone:
                    vector.tensor_scalar(out=dst[:, h * 512:(h + 1) * 512], in0=psb[bs[h]][:, 0:512],
                                         scalar1=1.0, scalar2=None, op0=ALU.add, op1=ALU.bypass)
                else:
                    vector.tensor_copy(out=dst[:, h * 512:(h + 1) * 512], in_=psb[bs[h]][:, 0:512])
            v = inc(vector.nop(), "dve")
            for bb2 in bs:
                consumed(bb2, "dve", v)
            return v
        return E("vector", vstep)

    def emit_wquant(src_h, nchunks, ncols, scol, spill):
        def ld(sync):
            for kc in range(nchunks):
                sync.dma_start(out=AR[:, A0 + kc * ncols:A0 + (kc + 1) * ncols],
                               in_=src_h[kc * 128:(kc + 1) * 128, :]).then_inc(semh["ld"], 16)
                dinc("ld")
            return cnt["ld"]
        mld = E("sync", ld)

        def folda(scalar):
            scalar.wait_ge(semh["ld"], mld)
            scalar.activation(out=AR[:, A1:A1 + ncols], in_=AR[:, A0:A0 + ncols], func=AF.Abs)
            return inc(scalar.nop(), "act")
        mfa = E("scalar", folda)

        def fold(vector):
            first = True
            for kc in range(1, nchunks):
                pass
            return None
        mprev = mfa
        for kc in range(1, nchunks):
            def fa(scalar, kc=kc, mprev=mprev):
                if kc == 1:
                    scalar.wait_ge(semh["ld"], mld)
                else:
                    scalar.wait_ge(semh["dve"], mprev)
                scalar.activation(out=AR[:, A3:A3 + ncols], in_=AR[:, A0 + kc * ncols:A0 + (kc + 1) * ncols],
                                  func=AF.Abs)
                return inc(scalar.nop(), "act")
            ma2 = E("scalar", fa)

            def fv(vector, ma2=ma2):
                vector.wait_ge(semh["act"], ma2)
                vector.tensor_tensor(out=AR[:, A1:A1 + ncols], in0=AR[:, A1:A1 + ncols],
                                     in1=AR[:, A3:A3 + ncols], op=ALU.max)
                return inc(vector.nop(), "dve")
            mprev = E("vector", fv)
        mf = mprev

        nq = ncols // 128
        done_q = 0
        while done_q < nq:
            take = min(8, nq - done_q)

            def trq(tensor, done_q=done_q, take=take):
                tensor.wait_ge(semh["dve"], mf)
                bs = take_bank(tensor, (take + 3) // 4)
                for q in range(take):
                    tensor.transpose(psb[bs[q // 4]][:, (q % 4) * 128:(q % 4) * 128 + 128],
                                     AR[:, A1 + (done_q + q) * 128:A1 + (done_q + q + 1) * 128],
                                     ident[:, :])
                return bs, inc(tensor.nop(), "pe")
            bs, mt = E("tensor", trq)

            def rdq(vector, bs=bs, mt=mt, done_q=done_q, take=take):
                vector.wait_ge(semh["pe"], mt)
                for q in range(take):
                    vector.tensor_reduce(out=scol[:, done_q + q:done_q + q + 1],
                                         in_=psb[bs[q // 4]][:, (q % 4) * 128:(q % 4) * 128 + 128],
                                         axis=AX.X, op=ALU.max)
                v = inc(vector.nop(), "dve")
                for b2 in bs:
                    consumed(b2, "dve", v)
                return v
            E("vector", rdq)
            done_q += take

        def scfin(vector):
            vector.tensor_scalar(out=scol[:, 0:nq], in0=scol[:, 0:nq], scalar1=1.0 / 127.0,
                                 scalar2=None, op0=ALU.mult, op1=ALU.bypass)
            vector.reciprocal(out=AR[:, A3:A3 + nq], in_=scol[:, 0:nq])
            return inc(vector.nop(), "dve")
        mf2 = E("vector", scfin)

        def scoltr(tensor):
            tensor.wait_ge(semh["dve"], mf2)
            b = take_bank(tensor, 1)[0]
            tensor.transpose(psb[b][0:nq, 0:128], AR[:, A3:A3 + nq], ident[:, :])
            return b, inc(tensor.nop(), "pe")
        b, mtr = E("tensor", scoltr)

        def scolev(vector, b=b, mtr=mtr):
            vector.wait_ge(semh["pe"], mtr)
            vector.tensor_copy(out=AR[0:nq, A5:A5 + 128], in_=psb[b][0:nq, 0:128])
            v = inc(vector.nop(), "dve")
            consumed(b, "dve", v)
            return v
        m_scol0 = E("vector", scolev)

        def scolld(sync):
            sync.wait_ge(semh["dve"], m_scol0)
            sync.dma_start(out=sp_row[0:1, 0:ncols].rearrange("o (r q) -> (o r) q", q=128),
                           in_=AR[0:nq, A5:A5 + 128]).then_inc(semh["st"], 16)
            dinc("st")
            sync.wait_ge(semh["st"], cnt["st"])
            sync.dma_start(out=AR[0:1, A1 + ncols:A1 + 2 * ncols],
                           in_=sp_row[0:1, 0:ncols]).then_inc(semh["ld"], 16)
            return dinc("ld")
        m_scol = E("sync", scolld)

        nq2 = ncols // 128
        done = 0
        while done < nq2:
            take = min(8, nq2 - done)

            def bcp(vector, done=done, take=take):
                vector.wait_ge(semh["ld"], m_scol)
                vector.wait_ge(semh["dve"], M_ZB)
                if zb_last[0]:
                    vector.wait_ge(semh["pe"], zb_last[0])
                vector.tensor_copy(out=zb[0:1, 0:take * 128],
                                   in_=AR[0:1, A1 + ncols + done * 128:A1 + ncols + (done + take) * 128])
                return inc(vector.nop(), "dve")
            mcp = E("vector", bcp)

            def bw(tensor, done=done, take=take, mcp=mcp):
                tensor.wait_ge(semh["dve"], mcp)
                bs = take_bank(tensor, (take + 3) // 4)
                for q in range(take):
                    tensor.matmul(psb[bs[q // 4]][:, (q % 4) * 128:(q % 4) * 128 + 128],
                                  ones128[:, :], zb[:, q * 128:(q + 1) * 128], start=True, stop=True)
                return bs, inc(tensor.nop(), "pe")
            bs, mb = E("tensor", bw)
            zb_last[0] = mb

            def bv(vector, bs=bs, mb=mb, done=done, take=take):
                vector.wait_ge(semh["pe"], mb)
                for q in range(take):
                    qq = done + q
                    vector.tensor_copy(out=AR[:, A2 + qq * 128:A2 + (qq + 1) * 128],
                                       in_=psb[bs[q // 4]][:, (q % 4) * 128:(q % 4) * 128 + 128])
                v = inc(vector.nop(), "dve")
                for bb2 in bs:
                    consumed(bb2, "dve", v)
                return v
            E("vector", bv)
            done += take

        def quant(vector):
            for kc in range(nchunks):
                w0 = AR[:, A0 + kc * ncols:A0 + (kc + 1) * ncols]
                vector.tensor_tensor(out=AR[:, A3:A3 + ncols], in0=w0, in1=AR[:, A2:A2 + ncols], op=ALU.mult)
                icst = AR[:, A4:A4 + ncols].bitcast(I32)
                vector.tensor_copy(out=icst, in_=AR[:, A3:A3 + ncols])
                vector.tensor_copy(out=AR[:, A5:A5 + ncols], in_=icst)
                vector.tensor_tensor(out=AR[:, A1:A1 + ncols], in0=AR[:, A3:A3 + ncols],
                                     in1=AR[:, A5:A5 + ncols], op=ALU.subtract)
                vector.scalar_tensor_tensor(out=AR[:, A3:A3 + ncols], in0=AR[:, A1:A1 + ncols],
                                            scalar=0.5, in1=AR[:, A5:A5 + ncols], op0=ALU.is_gt, op1=ALU.add)
                vector.scalar_tensor_tensor(out=AR[:, A5:A5 + ncols], in0=AR[:, A1:A1 + ncols],
                                            scalar=-0.5, in1=AR[:, A3:A3 + ncols], op0=ALU.is_ge, op1=ALU.add)
                vector.tensor_scalar(out=AR[:, A3:A3 + ncols], in0=AR[:, A5:A5 + ncols],
                                     scalar1=-1.0, scalar2=127.0, op0=ALU.add, op1=ALU.min)
                vector.tensor_scalar(out=ARB[:, 2 * (A0 + kc * ncols):2 * (A0 + kc * ncols) + ncols],
                                     in0=AR[:, A3:A3 + ncols],
                                     scalar1=-128.0, scalar2=None, op0=ALU.max, op1=ALU.bypass)
            return inc(vector.nop(), "dve")
        mq = E("vector", quant)

        def sp(sync):
            sync.wait_ge(semh["dve"], mq)
            for kc in range(nchunks):
                sync.dma_start(out=spill[:, kc * ncols:(kc + 1) * ncols],
                               in_=ARB[:, 2 * (A0 + kc * ncols):2 * (A0 + kc * ncols) + ncols]
                               ).then_inc(semh["st"], 16)
                dinc("st")
            return cnt["st"]
        return E("sync", sp)

    if KS0 < 4:
        return _finish(nc, blockctx, stack, E, semh, cnt)

    _bc_specs = [
        0,
        8,
        24,
        32,
        16,
        40,
    ]
    KS04 = int(_os.environ.get("KS04", "99"))
    for bi, roff in enumerate(_bc_specs):
        if bi >= KS04:
            break
        mbt = emit_bcast(bc0, roff, addone=(bi in (1, 3)))
        if KS04 == bi + 100:
            pass

        if int(_os.environ.get("KSNOSPILL", "0")) == 1:
            continue

        def bsp(sync, bi=bi, mbt=mbt):
            sync.wait_ge(semh["dve"], mbt)
            sync.dma_start(out=sp_bc[:, bi * 1024:(bi + 1) * 1024], in_=bc0[:, :]).then_inc(semh["st"], 16)
            return dinc("st")
        msz = E("sync", bsp)

        def bspw(vector, msz=msz):
            vector.wait_ge(semh["st"], msz)
            return inc(vector.nop(), "dve")
        E("vector", bspw)

    def emit_bcld(dst, bi, dep_s="dve", dep_v=None):
        hsrc = {6: winw_h, 7: ffnw_h, 8: ab_h}
        def bld(sync):
            if dep_v is not None:
                sync.wait_ge(semh[dep_s], dep_v)
            if bi in hsrc:
                sync.dma_start(out=dst[:, :], in_=hsrc[bi][:, :]).then_inc(semh["ld"], 16)
            else:
                sync.dma_start(out=dst[:, :], in_=sp_bc[:, bi * 1024:(bi + 1) * 1024]).then_inc(semh["ld"], 16)
            return dinc("ld")
        m2 = E("sync", bld)

        def bldw(vector, m2=m2):
            vector.wait_ge(semh["ld"], m2)
            return inc(vector.nop(), "dve")
        return E("vector", bldw)

    if KS0 < 5:
        return _finish(nc, blockctx, stack, E, semh, cnt)

    emit_wquant(wgT_h, DC, II, swg_col, sp_wgq)
    emit_wquant(wuT_h, DC, II, swu_col, sp_wuq)
    m_wq = emit_wquant(wdT_h, IC, D, swd_col, sp_wdq)

    if KSTOP < 1:
        return _finish(nc, blockctx, stack, E, semh, cnt)

    emit_bcld(bc0, 1, dep_s="st", dep_v=m_wq)
    emit_bcld(bc1, 0)
    M_BC = emit_bcld(bc2, 6)

    W0, XT0, SQ0, H10, HN0, CL0 = 0, 512, 1536, 2560, 3584, 4608
    H1F0, HNE0, HCP0, ES0, US0, SG0 = 4640, 5664, 6720, 7744, 8768, 9088

    def s1_wld(sync):
        sync.wait_ge(semh["st"], m_wq)
        for j in range(DC):
            sync.dma_start(out=AR[:, W0 + j * 64:W0 + (j + 1) * 64],
                           in_=u1w_h[j * 128:(j + 1) * 128, :]).then_inc(semh["ld"], 16)
            dinc("ld")
        return cnt["ld"]
    M_U1W = E("sync", s1_wld)

    s1_done = {}
    for ti in range(NCH):
        def ld(sync, ti=ti):
            if ti > 0:
                sync.wait_ge(semh["dve"], s1_done[ti - 1])
            else:
                sync.wait_ge(semh["ld"], M_U1W)
                sync.wait_ge(semh["dve"], M_BC)
            sync.dma_start(out=AR[:, XT0:XT0 + 1024],
                           in_=x_h[ti * 128:(ti + 1) * 128, :]).then_inc(semh["ld"], 16)
            return dinc("ld")
        mld = E("sync", ld)

        def v1(vector, mld=mld):
            vector.wait_ge(semh["ld"], mld)
            x = AR[:, XT0:XT0 + 1024]
            vector.tensor_reduce(out=AR[:, CL0:CL0 + 1], in_=x, axis=AX.X, op=ALU.add)
            vector.tensor_tensor(out=AR[:, SQ0:SQ0 + 1024], in0=x, in1=x, op=ALU.mult)
            vector.tensor_reduce(out=AR[:, CL0 + 1:CL0 + 2], in_=AR[:, SQ0:SQ0 + 1024],
                                 axis=AX.X, op=ALU.add)
            vector.tensor_scalar(out=AR[:, CL0:CL0 + 1], in0=AR[:, CL0:CL0 + 1],
                                 scalar1=1.0 / 1024.0, scalar2=None, op0=ALU.mult, op1=ALU.bypass)
            vector.tensor_scalar(out=AR[:, CL0 + 1:CL0 + 2], in0=AR[:, CL0 + 1:CL0 + 2],
                                 scalar1=1.0 / 1024.0, scalar2=None, op0=ALU.mult, op1=ALU.bypass)
            vector.tensor_tensor(out=AR[:, CL0 + 2:CL0 + 3], in0=AR[:, CL0:CL0 + 1],
                                 in1=AR[:, CL0:CL0 + 1], op=ALU.mult)
            vector.tensor_tensor(out=AR[:, CL0 + 1:CL0 + 2], in0=AR[:, CL0 + 1:CL0 + 2],
                                 in1=AR[:, CL0 + 2:CL0 + 3], op=ALU.subtract)
            return inc(vector.nop(), "dve")
        m1 = E("vector", v1)

        def a1(scalar, m1=m1):
            scalar.wait_ge(semh["dve"], m1)
            scalar.activation(out=AR[:, CL0 + 3:CL0 + 4], in_=AR[:, CL0 + 1:CL0 + 2],
                              func=AF.Ln, bias=cpool[:, 1:2], scale=1.0)
            scalar.activation(out=AR[:, CL0 + 3:CL0 + 4], in_=AR[:, CL0 + 3:CL0 + 4],
                              func=AF.Exp, scale=-0.5)
            return inc(scalar.nop(), "act")
        m2 = E("scalar", a1)

        def v2(vector, m2=m2):
            vector.wait_ge(semh["act"], m2)
            x = AR[:, XT0:XT0 + 1024]
            vector.tensor_scalar(out=AR[:, H10:H10 + 1024], in0=x,
                                 scalar1=AR[:, CL0:CL0 + 1], scalar2=AR[:, CL0 + 3:CL0 + 4],
                                 op0=ALU.subtract, op1=ALU.mult)
            vector.tensor_tensor(out=AR[:, H10:H10 + 1024], in0=AR[:, H10:H10 + 1024],
                                 in1=bc0[:, :], op=ALU.mult)
            vector.tensor_tensor(out=AR[:, H10:H10 + 1024], in0=AR[:, H10:H10 + 1024],
                                 in1=bc1[:, :], op=ALU.add)
            vector.tensor_tensor(out=AR[:, SQ0:SQ0 + 1024], in0=AR[:, H10:H10 + 1024],
                                 in1=AR[:, H10:H10 + 1024], op=ALU.mult)
            vector.tensor_reduce(out=AR[:, CL0 + 4:CL0 + 5], in_=AR[:, SQ0:SQ0 + 1024],
                                 axis=AX.X, op=ALU.add)
            return inc(vector.nop(), "dve")
        m3 = E("vector", v2)

        def a2(scalar, m3=m3):
            scalar.wait_ge(semh["dve"], m3)
            scalar.activation(out=AR[:, CL0 + 5:CL0 + 6], in_=AR[:, CL0 + 4:CL0 + 5],
                              func=AF.Ln, bias=cpool[:, 0:1], scale=1.0 / 1024.0)
            scalar.activation(out=AR[:, CL0 + 5:CL0 + 6], in_=AR[:, CL0 + 5:CL0 + 6],
                              func=AF.Exp, scale=-0.5)
            return inc(scalar.nop(), "act")
        m4 = E("scalar", a2)

        def v3(vector, m4=m4):
            vector.wait_ge(semh["act"], m4)
            vector.tensor_scalar(out=AR[:, HN0:HN0 + 1024], in0=AR[:, H10:H10 + 1024],
                                 scalar1=AR[:, CL0 + 5:CL0 + 6], scalar2=None, op0=ALU.mult, op1=ALU.bypass)
            vector.tensor_tensor(out=AR[:, HN0:HN0 + 1024], in0=AR[:, HN0:HN0 + 1024],
                                 in1=bc2[:, :], op=ALU.mult)
            return inc(vector.nop(), "dve")
        m5 = E("vector", v3)

        def t1(tensor, m5=m5):
            tensor.wait_ge(semh["dve"], m5)
            bs = take_bank(tensor, 4)
            for j in range(DC):
                tensor.transpose(psb[bs[j // 2]][:, (j % 2) * 128:(j % 2) * 128 + 128],
                                 AR[:, H10 + j * 128:H10 + (j + 1) * 128], ident[:, :])
            return bs, inc(tensor.nop(), "pe")
        bs, m6 = E("tensor", t1)

        def v4(vector, bs=bs, m6=m6):
            vector.wait_ge(semh["pe"], m6)
            for j in range(DC):
                vector.tensor_copy(out=AR[:, H1F0 + j * 128:H1F0 + (j + 1) * 128],
                                   in_=psb[bs[j // 2]][:, (j % 2) * 128:(j % 2) * 128 + 128])
            v = inc(vector.nop(), "dve")
            for b2 in bs:
                consumed(b2, "dve", v)
            return v
        m7 = E("vector", v4)

        def t2(tensor, m5=m5):
            bs = take_bank(tensor, 4)
            for j in range(DC):
                tensor.transpose(psb[bs[j // 2]][:, (j % 2) * 128:(j % 2) * 128 + 128],
                                 AR[:, HN0 + j * 128:HN0 + (j + 1) * 128], ident[:, :])
            return bs, inc(tensor.nop(), "pe")
        bs2, m8 = E("tensor", t2)

        def v5(vector, ti=ti, bs=bs2, m8=m8):
            vector.wait_ge(semh["pe"], m8)
            hne3 = AR[:, HNE0:HNE0 + 1048].rearrange("p (j q) -> p j q", q=131)
            if ti > 0:
                vector.tensor_copy(out=hne3[:, :, 0:3], in_=hne3[:, :, 128:131])
            else:
                vector.memset(AR[:, HNE0:HNE0 + 1048], 0.0)
            for j in range(DC):
                vector.tensor_copy(out=AR[:, HNE0 + j * 131 + 3:HNE0 + j * 131 + 131],
                                   in_=psb[bs[j // 2]][:, (j % 2) * 128:(j % 2) * 128 + 128])
            v = inc(vector.nop(), "dve")
            for b2 in bs:
                consumed(b2, "dve", v)
            for j in range(DC):
                base = HNE0 + j * 131
                vector.tensor_scalar(out=AR[:, HCP0 + j * 128:HCP0 + (j + 1) * 128],
                                     in0=AR[:, base:base + 128], scalar1=cw_cols[:, j * 4:j * 4 + 1],
                                     scalar2=None, op0=ALU.mult, op1=ALU.bypass)
                for s in range(1, 4):
                    vector.scalar_tensor_tensor(out=AR[:, HCP0 + j * 128:HCP0 + (j + 1) * 128],
                                                in0=AR[:, base + s:base + s + 128],
                                                scalar=cw_cols[:, j * 4 + s:j * 4 + s + 1],
                                                in1=AR[:, HCP0 + j * 128:HCP0 + (j + 1) * 128],
                                                op0=ALU.mult, op1=ALU.add)
                vector.tensor_scalar(out=AR[:, HCP0 + j * 128:HCP0 + (j + 1) * 128],
                                     in0=AR[:, HCP0 + j * 128:HCP0 + (j + 1) * 128],
                                     scalar1=cb_cols[:, j:j + 1], scalar2=None, op0=ALU.add, op1=ALU.bypass)
            return inc(vector.nop(), "dve")
        m9 = E("vector", v5)

        def a3(scalar, m9=m9):
            scalar.wait_ge(semh["dve"], m9)
            scalar.activation(out=AR[:, ES0:ES0 + 1024], in_=AR[:, HCP0:HCP0 + 1024],
                              func=AF.Exp, scale=-1.0)
            return inc(scalar.nop(), "act")
        m10 = E("scalar", a3)

        def v6(vector, m10=m10):
            vector.wait_ge(semh["act"], m10)
            vector.tensor_scalar(out=AR[:, ES0:ES0 + 1024], in0=AR[:, ES0:ES0 + 1024],
                                 scalar1=1.0, scalar2=None, op0=ALU.add, op1=ALU.bypass)
            vector.reciprocal(out=AR[:, ES0:ES0 + 1024], in_=AR[:, ES0:ES0 + 1024])
            vector.tensor_tensor(out=AR[:, HCP0:HCP0 + 1024], in0=AR[:, HCP0:HCP0 + 1024],
                                 in1=AR[:, ES0:ES0 + 1024], op=ALU.mult)
            return inc(vector.nop(), "dve")
        m11 = E("vector", v6)

        def t3(tensor, m7=m7):
            tensor.wait_ge(semh["dve"], m7)
            b = take_bank(tensor, 1)[0]
            for j in range(DC):
                tensor.matmul(psb[b][0:64, 0:128], AR[:, W0 + j * 64:W0 + (j + 1) * 64],
                              AR[:, H1F0 + j * 128:H1F0 + (j + 1) * 128],
                              start=(j == 0), stop=(j == DC - 1))
            return b, inc(tensor.nop(), "pe")
        b3, m12 = E("tensor", t3)

        def v7(vector, b3=b3, m12=m12):
            vector.wait_ge(semh["pe"], m12)
            vector.tensor_scalar(out=AR[0:64, US0:US0 + 128], in0=psb[b3][0:64, 0:128],
                                 scalar1=u1b_col[:, 0:1], scalar2=None, op0=ALU.add, op1=ALU.bypass)
            v = inc(vector.nop(), "dve")
            consumed(b3, "dve", v)
            return v
        m13 = E("vector", v7)

        def a4(scalar, m13=m13):
            scalar.wait_ge(semh["dve"], m13)
            scalar.activation(out=AR[0:64, US0 + 128:US0 + 256], in_=AR[0:64, US0:US0 + 128],
                              func=AF.Exp, scale=-1.0)
            return inc(scalar.nop(), "act")
        m14 = E("scalar", a4)

        def v8(vector, m14=m14):
            vector.wait_ge(semh["act"], m14)
            vector.tensor_scalar(out=AR[0:64, US0 + 128:US0 + 256], in0=AR[0:64, US0 + 128:US0 + 256],
                                 scalar1=1.0, scalar2=None, op0=ALU.add, op1=ALU.bypass)
            vector.reciprocal(out=AR[0:64, US0 + 128:US0 + 256], in_=AR[0:64, US0 + 128:US0 + 256])
            vector.tensor_tensor(out=AR[0:64, SG0:SG0 + 128], in0=AR[0:64, US0:US0 + 128],
                                 in1=AR[0:64, US0 + 128:US0 + 256], op=ALU.mult)
            return inc(vector.nop(), "dve")
        m15 = E("vector", v8)

        def sp1(sync, ti=ti, m11=m11, m15=m15):
            sync.wait_ge(semh["dve"], m11)
            sync.dma_start(out=sp_h1[:, ti * 1024:(ti + 1) * 1024],
                           in_=AR[:, H1F0:H1F0 + 1024]).then_inc(semh["st"], 16)
            sync.dma_start(out=sp_hc[:, ti * 1024:(ti + 1) * 1024],
                           in_=AR[:, HCP0:HCP0 + 1024]).then_inc(semh["st"], 16)
            sync.wait_ge(semh["dve"], m15)
            sync.dma_start(out=sp_sg[:, ti * 128:(ti + 1) * 128],
                           in_=AR[0:64, SG0:SG0 + 128]).then_inc(semh["st"], 16)
            dinc("st", 48)
            return cnt["st"]
        msp = E("sync", sp1)

        def fin(vector, msp=msp):
            vector.wait_ge(semh["st"], msp)
            return inc(vector.nop(), "dve")
        s1_done[ti] = E("vector", fin)

    S1_DONE = s1_done[NCH - 1]

    if KSTOP < 2:
        return _finish(nc, blockctx, stack, E, semh, cnt)

    M_ABB = emit_bcld(bc2, 8, dep_s="dve", dep_v=S1_DONE)

    PW0, PW1 = 0, 8192
    PBW = 16384
    PHC = 16512
    PO0, PO1 = 17536, 18560
    PSQ = 19584
    PSS, PBS = 20608, 20736

    def qk_wld(sync):
        sync.wait_ge(semh["dve"], S1_DONE)
        for dst, srcw in ((PW0, qw_h), (PW1, kw_h)):
            for j in range(DC):
                sync.dma_start(out=AR[:, dst + j * 1024:dst + (j + 1) * 1024],
                               in_=srcw[j * 128:(j + 1) * 128, :]).then_inc(semh["ld"], 16)
                dinc("ld")
        return cnt["ld"]
    M_QKW = E("sync", qk_wld)

    qk_done = {}
    for c in range(NCH):
        def ld(sync, c=c):
            if c > 0:
                sync.wait_ge(semh["dve"], qk_done[c - 1])
            else:
                sync.wait_ge(semh["ld"], M_QKW)
                sync.wait_ge(semh["dve"], M_ABB)
            sync.dma_start(out=AR[:, PHC:PHC + 1024],
                           in_=sp_hc[:, c * 1024:(c + 1) * 1024]).then_inc(semh["ld"], 16)
            return dinc("ld")
        mld = E("sync", ld)

        def pj(tensor, mld=mld):
            tensor.wait_ge(semh["ld"], mld)
            bs = take_bank(tensor, 4)
            for wbase, boff in ((PW0, 0), (PW1, 2)):
                for mc in range(8):
                    b = bs[boff + mc // 4]
                    for kc in range(DC):
                        tensor.matmul(psb[b][:, (mc % 4) * 128:(mc % 4) * 128 + 128],
                                      AR[:, wbase + kc * 1024 + mc * 128:wbase + kc * 1024 + (mc + 1) * 128],
                                      AR[:, PHC + kc * 128:PHC + (kc + 1) * 128],
                                      start=(kc == 0), stop=(kc == DC - 1))
            return bs, inc(tensor.nop(), "pe")
        bs, m1 = E("tensor", pj)

        def pje(vector, bs=bs, m1=m1):
            vector.wait_ge(semh["pe"], m1)
            for h in range(2):
                vector.tensor_copy(out=AR[:, PO0 + h * 512:PO0 + (h + 1) * 512], in_=psb[bs[h]][:, 0:512])
                vector.tensor_copy(out=AR[:, PO1 + h * 512:PO1 + (h + 1) * 512], in_=psb[bs[2 + h]][:, 0:512])
            v = inc(vector.nop(), "dve")
            for b2 in bs:
                consumed(b2, "dve", v)
            return v
        mqk = E("vector", pje)

        def qknorm(base, dep):
            def sq(scalar):
                scalar.wait_ge(semh["dve"], dep)
                scalar.activation(out=AR[:, PSQ:PSQ + 1024], in_=AR[:, base:base + 1024], func=AF.Square)
                return inc(scalar.nop(), "act")
            ma = E("scalar", sq)

            def ssq(tensor, ma=ma):
                tensor.wait_ge(semh["act"], ma)
                bs = take_bank(tensor, 2)
                for j in range(DC):
                    b = bs[j // 4]
                    tensor.matmul(psb[b][0:64, (j % 4) * 128:(j % 4) * 128 + 128],
                                  ones128[0:64, 0:64], AR[0:64, PSQ + j * 128:PSQ + (j + 1) * 128],
                                  start=True, stop=True)
                    tensor.matmul(psb[b][64:128, (j % 4) * 128:(j % 4) * 128 + 128],
                                  ones128[64:128, 64:128], AR[64:128, PSQ + j * 128:PSQ + (j + 1) * 128],
                                  start=True, stop=True)
                return bs, inc(tensor.nop(), "pe")
            bs, mb = E("tensor", ssq)

            def rn_a(scalar, bs=bs, mb=mb):
                scalar.wait_ge(semh["pe"], mb)
                for h in range(2):
                    scalar.activation(out=AR[:, PSQ + h * 512:PSQ + (h + 1) * 512],
                                      in_=psb[bs[h]][:, 0:512], func=AF.Ln)
                for h in range(2):
                    scalar.activation(out=AR[:, PSQ + h * 512:PSQ + (h + 1) * 512],
                                      in_=AR[:, PSQ + h * 512:PSQ + (h + 1) * 512], func=AF.Exp, scale=0.5)
                v = inc(scalar.nop(), "act")
                for b2 in bs:
                    consumed(b2, "act", v)
                return v
            mc2 = E("scalar", rn_a)

            def app(vector, mc2=mc2):
                vector.wait_ge(semh["act"], mc2)
                vector.tensor_scalar(out=AR[:, PSQ:PSQ + 1024], in0=AR[:, PSQ:PSQ + 1024],
                                     scalar1=1e-6, scalar2=None, op0=ALU.add, op1=ALU.bypass)
                vector.reciprocal(out=AR[:, PSQ:PSQ + 1024], in_=AR[:, PSQ:PSQ + 1024])
                vector.tensor_tensor(out=AR[:, base:base + 1024], in0=AR[:, base:base + 1024],
                                     in1=AR[:, PSQ:PSQ + 1024], op=ALU.mult)
                return inc(vector.nop(), "dve")
            return E("vector", app)

        m_qn = qknorm(PO0, mqk)
        m_kn = qknorm(PO1, m_qn)

        def spq(sync, c=c, m_kn=m_kn):
            sync.wait_ge(semh["dve"], m_kn)
            sync.dma_start(out=sp_qf[:, c * 1024:(c + 1) * 1024],
                           in_=AR[:, PO0:PO0 + 1024]).then_inc(semh["st"], 16)
            sync.dma_start(out=sp_kf[:, c * 1024:(c + 1) * 1024],
                           in_=AR[:, PO1:PO1 + 1024]).then_inc(semh["st"], 16)
            dinc("st", 32)
            return cnt["st"]
        msp = E("sync", spq)

        def fin(vector, msp=msp):
            vector.wait_ge(semh["st"], msp)
            return inc(vector.nop(), "dve")
        qk_done[c] = E("vector", fin)

    QK_DONE = qk_done[NCH - 1]

    if KSTOP < 3:
        return _finish(nc, blockctx, stack, E, semh, cnt)

    def vz_wld(sync):
        sync.wait_ge(semh["dve"], QK_DONE)
        for dst, srcw in ((PW0, vw_h), (PW1, aw_h)):
            for j in range(DC):
                sync.dma_start(out=AR[:, dst + j * 1024:dst + (j + 1) * 1024],
                               in_=srcw[j * 128:(j + 1) * 128, :]).then_inc(semh["ld"], 16)
                dinc("ld")
        for j in range(DC):
            sync.dma_start(out=AR[:, PBW + j * 16:PBW + (j + 1) * 16],
                           in_=bw_h[j * 128:(j + 1) * 128, :]).then_inc(semh["ld"], 16)
            dinc("ld")
        return cnt["ld"]
    M_VZW = E("sync", vz_wld)

    vz_done = {}
    for c in range(NCH):
        def ld(sync, c=c):
            if c > 0:
                sync.wait_ge(semh["dve"], vz_done[c - 1])
            else:
                sync.wait_ge(semh["ld"], M_VZW)
            sync.dma_start(out=AR[:, PHC:PHC + 1024],
                           in_=sp_hc[:, c * 1024:(c + 1) * 1024]).then_inc(semh["ld"], 16)
            return dinc("ld")
        mld = E("sync", ld)

        def pj(tensor, mld=mld):
            tensor.wait_ge(semh["ld"], mld)
            bs = take_bank(tensor, 4)
            for wbase, boff in ((PW0, 0), (PW1, 2)):
                for half in range(2):
                    b = bs[boff + half]
                    for kc in range(DC):
                        tensor.matmul(psb[b][:, 0:512],
                                      AR[:, PHC + kc * 128:PHC + (kc + 1) * 128],
                                      AR[:, wbase + kc * 1024 + half * 512:wbase + kc * 1024 + (half + 1) * 512],
                                      start=(kc == 0), stop=(kc == DC - 1))
            return bs, inc(tensor.nop(), "pe")
        bs, m1 = E("tensor", pj)

        def bmm(tensor, mld=mld):
            b = take_bank(tensor, 1)[0]
            for kc in range(DC):
                tensor.matmul(psb[b][:, 0:16], AR[:, PHC + kc * 128:PHC + (kc + 1) * 128],
                              AR[:, PBW + kc * 16:PBW + (kc + 1) * 16],
                              start=(kc == 0), stop=(kc == DC - 1))
            return b, inc(tensor.nop(), "pe")
        b4, m4 = E("tensor", bmm)

        def pje(vector, bs=bs, m1=m1):
            vector.wait_ge(semh["pe"], m1)
            for half in range(2):
                vector.tensor_copy(out=AR[:, PO0 + half * 512:PO0 + (half + 1) * 512],
                                   in_=psb[bs[half]][:, 0:512])
            for half in range(2):
                vector.tensor_tensor(out=AR[:, PSQ + half * 512:PSQ + (half + 1) * 512],
                                     in0=psb[bs[2 + half]][:, 0:512],
                                     in1=bc2[:, half * 512:(half + 1) * 512], op=ALU.add)
            v = inc(vector.nop(), "dve")
            for b2 in bs:
                consumed(b2, "dve", v)
            return v
        m3 = E("vector", pje)

        def spx(scalar, m3=m3):
            scalar.wait_ge(semh["dve"], m3)
            scalar.activation(out=AR[:, PO1:PO1 + 1024], in_=AR[:, PSQ:PSQ + 1024],
                              func=AF.Exp, scale=-1.0)
            scalar.activation(out=AR[:, PO1:PO1 + 1024], in_=AR[:, PO1:PO1 + 1024],
                              func=AF.Ln, bias=cpool[:, 2:3], scale=1.0)
            return inc(scalar.nop(), "act")
        m_sp = E("scalar", spx)

        def bsig_v0(vector, b4=b4, m4=m4):
            vector.wait_ge(semh["pe"], m4)
            vector.tensor_tensor(out=AR[:, PBS:PBS + 16], in0=psb[b4][:, 0:16],
                                 in1=bb_tile[:, :], op=ALU.add)
            v = inc(vector.nop(), "dve")
            consumed(b4, "dve", v)
            return v
        m5 = E("vector", bsig_v0)

        def bsig_a(scalar, m5=m5):
            scalar.wait_ge(semh["dve"], m5)
            scalar.activation(out=AR[:, PBS:PBS + 16], in_=AR[:, PBS:PBS + 16],
                              func=AF.Exp, scale=-1.0)
            return inc(scalar.nop(), "act")
        m5a = E("scalar", bsig_a)

        def bsig_v(vector, m5a=m5a):
            vector.wait_ge(semh["act"], m5a)
            vector.tensor_scalar(out=AR[:, PBS:PBS + 16], in0=AR[:, PBS:PBS + 16],
                                 scalar1=1.0, scalar2=None, op0=ALU.add, op1=ALU.bypass)
            vector.reciprocal(out=AR[:, PBS:PBS + 16], in_=AR[:, PBS:PBS + 16])
            return inc(vector.nop(), "dve")
        m_bs = E("vector", bsig_v)

        def spv(sync, c=c, m_sp=m_sp, m_bs=m_bs):
            sync.wait_ge(semh["act"], m_sp)
            sync.wait_ge(semh["dve"], m_bs)
            sync.dma_start(out=sp_vt[:, c * 1024:(c + 1) * 1024],
                           in_=AR[:, PO0:PO0 + 1024]).then_inc(semh["st"], 16)
            sync.dma_start(out=sp_sp[:, c * 1024:(c + 1) * 1024],
                           in_=AR[:, PO1:PO1 + 1024]).then_inc(semh["st"], 16)
            sync.dma_start(out=sp_bt[:, c * 16:(c + 1) * 16],
                           in_=AR[:, PBS:PBS + 16]).then_inc(semh["st"], 16)
            dinc("st", 48)
            return cnt["st"]
        msp = E("sync", spv)

        def fin(vector, msp=msp):
            vector.wait_ge(semh["st"], msp)
            return inc(vector.nop(), "dve")
        vz_done[c] = E("vector", fin)

    VZ_DONE = vz_done[NCH - 1]

    if KSTOP < 4:
        return _finish(nc, blockctx, stack, E, semh, cnt)

    QF2, KF2, VT2, SPT2, LRT2, EP2 = 0, 1024, 2048, 3072, 4096, 5120
    EN2, SCR2, KMP2, KMN2, KTM2 = 6144, 7168, 8192, 9216, 10240
    GC2, BSB2 = 11264, 11296
    XA_, XB_, XTA_, XTB_ = 11552, 11680, 11808, 11936
    WA_, WB_, M2T_, SA_, SB_ = 12064, 12128, 12192, 12320, 12832
    OHP_, O2S_, RNR_ = 13344, 15392, 17440
    xa_t = AR[:, XA_:XA_ + 128]; xb_t = AR[:, XB_:XB_ + 128]
    xta_t = AR[:, XTA_:XTA_ + 128]; xtb_t = AR[:, XTB_:XTB_ + 128]
    wa_t = AR[:, WA_:WA_ + 64]; wb_t = AR[:, WB_:WB_ + 64]
    m2t_t = AR[:, M2T_:M2T_ + 128]
    s_a = AR[:, SA_:SA_ + 512]; s_b = AR[:, SB_:SB_ + 512]
    ohp_t = AR[0:64, OHP_:OHP_ + 2048]; o2s_t = AR[0:64, O2S_:O2S_ + 2048]
    rnrow_t = AR[0:1, RNR_:RNR_ + 2048]

    def sinit(vector):
        vector.memset(s_a[:, :], 0.0)
        return inc(vector.nop(), "dve")
    M_SINIT = E("vector", sinit)

    GCREF = AR[:, GC2:GC2 + 8]
    GCFULL = AR[:, GC2 + 8:GC2 + 16]
    GCR = AR[:, GC2 + 16:GC2 + 24]

    sa2_done = {}
    for c in range(NCH):
        Scur = s_a if (c % 2 == 0) else s_b
        Snew = s_b if (c % 2 == 0) else s_a

        def ld(sync, c=c):
            if c > 0:
                sync.wait_ge(semh["dve"], sa2_done[c - 1])
            else:
                sync.wait_ge(semh["dve"], VZ_DONE)
            sync.dma_start(out=AR[:, QF2:QF2 + 1024],
                           in_=sp_qf[:, c * 1024:(c + 1) * 1024]).then_inc(semh["ld"], 16)
            sync.dma_start(out=AR[:, KF2:KF2 + 1024],
                           in_=sp_kf[:, c * 1024:(c + 1) * 1024]).then_inc(semh["ld"], 16)
            sync.dma_start(out=AR[:, VT2:VT2 + 1024],
                           in_=sp_vt[:, c * 1024:(c + 1) * 1024]).then_inc(semh["ld"], 16)
            sync.dma_start(out=AR[:, SPT2:SPT2 + 1024],
                           in_=sp_sp[:, c * 1024:(c + 1) * 1024]).then_inc(semh["ld"], 16)
            sync.dma_start(out=AR[:, BSB2:BSB2 + 16],
                           in_=sp_bt[:, c * 16:(c + 1) * 16]).then_inc(semh["ld"], 16)
            dinc("ld", 80)
            return cnt["ld"]
        mld = E("sync", ld)

        def lrmm(tensor, mld=mld, c=c):
            tensor.wait_ge(semh["ld"], mld)
            if c == 0:
                tensor.wait_ge(semh["dve"], M_SINIT)
            bs = take_bank(tensor, 2)
            for half in range(2):
                tensor.matmul(psb[bs[half]][:, 0:512], csrel[:, :],
                              AR[:, SPT2 + half * 512:SPT2 + (half + 1) * 512], start=True, stop=True)
            b3 = take_bank(tensor, 1)[0]
            for q in range(3):
                for j in range(DC):
                    tensor.matmul(psb[b3][:, q * 8 + j:q * 8 + j + 1],
                                  AR[:, SPT2 + j * 128:SPT2 + (j + 1) * 128],
                                  cscols[:, q:q + 1], start=True, stop=True)
            return bs, b3, inc(tensor.nop(), "pe")
        bs, b3, m6 = E("tensor", lrmm)

        def lre(vector, bs=bs, m6=m6):
            vector.wait_ge(semh["pe"], m6)
            for half in range(2):
                vector.tensor_copy(out=AR[:, LRT2 + half * 512:LRT2 + (half + 1) * 512],
                                   in_=psb[bs[half]][:, 0:512])
            v = inc(vector.nop(), "dve")
            for b2 in bs:
                consumed(b2, "dve", v)
            return inc(vector.nop(), "dve")
        m7 = E("vector", lre)

        def gce(scalar, b3=b3, m6=m6):
            scalar.wait_ge(semh["pe"], m6)
            scalar.activation(out=GCREF, in_=psb[b3][:, 0:8], func=AF.Exp, bias=cpool[:, 4:5], scale=1.0)
            scalar.activation(out=GCFULL, in_=psb[b3][:, 8:16], func=AF.Exp, scale=1.0)
            scalar.activation(out=GCR, in_=psb[b3][:, 16:24], func=AF.Exp, bias=cpool[:, 4:5], scale=1.0)
            v = inc(scalar.nop(), "act")
            consumed(b3, "act", v)
            return v
        m_gc = E("scalar", gce)

        def ltr(tensor, m7=m7):
            tensor.wait_ge(semh["dve"], m7)
            bs = take_bank(tensor, 2)
            for j in range(DC):
                tensor.transpose(psb[bs[j // 4]][:, (j % 4) * 128:(j % 4) * 128 + 128],
                                 AR[:, LRT2 + j * 128:LRT2 + (j + 1) * 128], ident[:, :])
            return bs, inc(tensor.nop(), "pe")
        bs, m8 = E("tensor", ltr)

        def Ee(scalar, bs=bs, m8=m8):
            scalar.wait_ge(semh["pe"], m8)
            for h in range(2):
                scalar.activation(out=AR[:, EP2 + h * 512:EP2 + (h + 1) * 512], in_=psb[bs[h]][:, 0:512],
                                  func=AF.Exp, bias=cpool[:, 3:4], scale=1.0)
            for h in range(2):
                scalar.activation(out=AR[:, EN2 + h * 512:EN2 + (h + 1) * 512], in_=psb[bs[h]][:, 0:512],
                                  func=AF.Exp, bias=cpool[:, 3:4], scale=-1.0)
            v = inc(scalar.nop(), "act")
            for b2 in bs:
                consumed(b2, "act", v)
            return v
        m_E = E("scalar", Ee)

        def scaled(vector, m_E=m_E, m_gc=m_gc):
            vector.wait_ge(semh["act"], m_E)
            vector.wait_ge(semh["act"], m_gc)
            vector.tensor_tensor(out=AR[:, SCR2:SCR2 + 1024], in0=AR[:, KF2:KF2 + 1024],
                                 in1=AR[:, EP2:EP2 + 1024], op=ALU.mult)
            vector.tensor_tensor(out=AR[:, KMP2:KMP2 + 1024], in0=AR[:, KF2:KF2 + 1024],
                                 in1=AR[:, EN2:EN2 + 1024], op=ALU.mult)
            for j in range(DC):
                vector.tensor_scalar(out=AR[:, KF2 + j * 128:KF2 + (j + 1) * 128],
                                     in0=AR[:, SCR2 + j * 128:SCR2 + (j + 1) * 128],
                                     scalar1=GCREF[:, j:j + 1], scalar2=None, op0=ALU.mult, op1=ALU.bypass)
            return inc(vector.nop(), "dve")
        m9 = E("vector", scaled)

        def kmfin(vector, m9=m9):
            vector.wait_ge(semh["dve"], m9)
            vector.tensor_scalar(out=AR[:, KMN2:KMN2 + 1024], in0=AR[:, KMP2:KMP2 + 1024],
                                 scalar1=-1.0, scalar2=None, op0=ALU.mult, op1=ALU.bypass)
            vector.tensor_tensor(out=AR[:, QF2:QF2 + 1024], in0=AR[:, QF2:QF2 + 1024],
                                 in1=AR[:, EP2:EP2 + 1024], op=ALU.mult)
            for j in range(DC):
                vector.tensor_scalar(out=AR[:, EN2 + j * 128:EN2 + (j + 1) * 128],
                                     in0=AR[:, QF2 + j * 128:QF2 + (j + 1) * 128],
                                     scalar1=GCREF[:, j:j + 1], scalar2=None, op0=ALU.mult, op1=ALU.bypass)
            return inc(vector.nop(), "dve")
        m11 = E("vector", kmfin)

        def ktr(tensor, m11=m11):
            tensor.wait_ge(semh["dve"], m11)
            bs = take_bank(tensor, 2)
            for j in range(DC):
                tensor.transpose(psb[bs[j // 4]][:, (j % 4) * 128:(j % 4) * 128 + 128],
                                 AR[:, KMP2 + j * 128:KMP2 + (j + 1) * 128], ident[:, :])
            return bs, inc(tensor.nop(), "pe")
        bs, m12 = E("tensor", ktr)

        def kte(vector, bs=bs, m12=m12):
            vector.wait_ge(semh["pe"], m12)
            for h in range(2):
                vector.tensor_copy(out=AR[:, KTM2 + h * 512:KTM2 + (h + 1) * 512],
                                   in_=psb[bs[h]][:, 0:512])
            v = inc(vector.nop(), "dve")
            for b2 in bs:
                consumed(b2, "dve", v)
            return v
        m_prev = E("vector", kte)

        if DEBUG_SCAN and c == 0:
            def dbg1(sync, m_prev=m_prev):
                sync.wait_ge(semh["dve"], m_prev)
                for off, base in [(0, LRT2), (1024, EP2), (2048, EN2), (3072, SCR2),
                                  (4096, KMP2), (5120, QF2), (6144, KTM2), (7168, KF2)]:
                    sync.dma_start(out=sp_dbg[:, off:off + 1024],
                                   in_=AR[:, base:base + 1024]).then_inc(semh["st"], 16)
                    dinc("st")
                sync.dma_start(out=sp_dbg[:, 8192:8192 + 32], in_=AR[:, GC2:GC2 + 32]).then_inc(semh["st"], 16)
                dinc("st")
                return cnt["st"]
            mdb = E("sync", dbg1)

            def dbg1w(vector, mdb=mdb):
                vector.wait_ge(semh["st"], mdb)
                return inc(vector.nop(), "dve")
            m_prev = E("vector", dbg1w)

        for hh in range(HN):
            j = hh // 2
            lo = (hh % 2) * 64
            kmn_h = AR[lo:lo + 64, KMN2 + j * 128:KMN2 + (j + 1) * 128]
            kmp_h = AR[lo:lo + 64, KMP2 + j * 128:KMP2 + (j + 1) * 128]
            khp_h = AR[lo:lo + 64, SCR2 + j * 128:SCR2 + (j + 1) * 128]
            kg_h = AR[lo:lo + 64, KF2 + j * 128:KF2 + (j + 1) * 128]
            qh_h = AR[lo:lo + 64, QF2 + j * 128:QF2 + (j + 1) * 128]
            qg_h = AR[lo:lo + 64, EN2 + j * 128:EN2 + (j + 1) * 128]
            scur_h = Scur[lo:lo + 64, j * 64:(j + 1) * 64]
            snew_h = Snew[lo:lo + 64, j * 64:(j + 1) * 64]
            vt_h = AR[:, VT2 + hh * 64:VT2 + (hh + 1) * 64]
            ktm_h = AR[:, KTM2 + hh * 64:KTM2 + (hh + 1) * 64]

            def hmm(tensor, kmn_h=kmn_h, khp_h=khp_h, kmp_h=kmp_h, qh_h=qh_h,
                    kg_h=kg_h, scur_h=scur_h, m_prev=m_prev):
                tensor.wait_ge(semh["dve"], m_prev)
                bs = take_bank(tensor, 4)
                tensor.matmul(psb[bs[0]][:, 0:128], kmn_h, khp_h, start=True, stop=True)
                tensor.matmul(psb[bs[2]][:, 0:128], kmp_h, qh_h, start=True, stop=True)
                tensor.matmul(psb[bs[3]][:, 0:64], kg_h, scur_h, start=True, stop=True)
                return bs, inc(tensor.nop(), "pe")
            bs, ma = E("tensor", hmm)

            bcol = AR[:, BSB2 + hh:BSB2 + hh + 1]

            def hmsk(vector, bs=bs, ma=ma, vt_h=vt_h, bcol=bcol):
                vector.wait_ge(semh["pe"], ma)
                vector.tensor_tensor(out=xa_t[:, :], in0=psb[bs[0]][:, 0:128], in1=mxu[:, :], op=ALU.mult)
                vector.tensor_scalar(out=xa_t[:, :], in0=xa_t[:, :], scalar1=bcol, scalar2=None,
                                     op0=ALU.mult, op1=ALU.bypass)
                vector.tensor_tensor(out=m2t_t[:, :], in0=psb[bs[2]][:, 0:128], in1=mm2[:, :], op=ALU.mult)
                vector.tensor_scalar(out=m2t_t[:, :], in0=m2t_t[:, :], scalar1=bcol, scalar2=None,
                                     op0=ALU.mult, op1=ALU.bypass)
                vector.tensor_tensor(out=wa_t[:, :], in0=vt_h, in1=psb[bs[3]][:, 0:64], op=ALU.subtract)
                v = inc(vector.nop(), "dve")
                for b2 in bs:
                    consumed(b2, "dve", v)
                return v
            mh1 = E("vector", hmsk)

            def xtr0(tensor, mh1=mh1):
                tensor.wait_ge(semh["dve"], mh1)
                b = take_bank(tensor, 1)[0]
                tensor.transpose(psb[b][:, 0:128], xa_t[:, :], ident[:, :])
                return b, inc(tensor.nop(), "pe")
            bx, mh2 = E("tensor", xtr0)

            def xtr0e(vector, bx=bx, mh2=mh2):
                vector.wait_ge(semh["pe"], mh2)
                vector.tensor_copy(out=xta_t[:, :], in_=psb[bx][:, 0:128])
                v = inc(vector.nop(), "dve")
                consumed(bx, "dve", v)
                return v
            mdep = E("vector", xtr0e)

            xcur, xtcur, xnxt, xtnxt = xa_t, xta_t, xb_t, xtb_t
            wcur, wnxt = wa_t, wb_t
            for it in range(5):
                def sl(tensor, xcur=xcur, xtcur=xtcur, wcur=wcur, it=it, mdep=mdep):
                    tensor.wait_ge(semh["dve"], mdep)
                    bs = take_bank(tensor, 2 if it < 4 else 1)
                    tensor.matmul(psb[bs[0]][:, 0:64], xcur[:, :], wcur[:, :], start=True, stop=True)
                    if it < 4:
                        tensor.matmul(psb[bs[1]][:, 0:128], xtcur[:, :], xcur[:, :], start=True, stop=True)
                    return bs, inc(tensor.nop(), "pe")
                bs, mb = E("tensor", sl)

                def sle(vector, bs=bs, mb=mb, wcur=wcur, wnxt=wnxt, xnxt=xnxt, it=it):
                    vector.wait_ge(semh["pe"], mb)
                    vector.tensor_tensor(out=wnxt[:, :], in0=psb[bs[0]][:, 0:64], in1=wcur[:, :], op=ALU.add)
                    if it < 4:
                        vector.tensor_copy(out=xnxt[:, :], in_=psb[bs[1]][:, 0:128])
                    v = inc(vector.nop(), "dve")
                    for b2 in bs:
                        consumed(b2, "dve", v)
                    return v
                mdep = E("vector", sle)

                if it < 4:
                    def sltr(tensor, xnxt=xnxt, mdep=mdep):
                        tensor.wait_ge(semh["dve"], mdep)
                        b = take_bank(tensor, 1)[0]
                        tensor.transpose(psb[b][:, 0:128], xnxt[:, :], ident[:, :])
                        return b, inc(tensor.nop(), "pe")
                    bx2, mtx = E("tensor", sltr)

                    def sltre(vector, bx2=bx2, mtx=mtx, xtnxt=xtnxt):
                        vector.wait_ge(semh["pe"], mtx)
                        vector.tensor_copy(out=xtnxt[:, :], in_=psb[bx2][:, 0:128])
                        v = inc(vector.nop(), "dve")
                        consumed(bx2, "dve", v)
                        return v
                    mdep = E("vector", sltre)
                xcur, xnxt = xnxt, xcur
                xtcur, xtnxt = xtnxt, xtcur
                wcur, wnxt = wnxt, wcur
            wfin = wcur

            if DEBUG_SCAN and c == 0 and hh == 0:
                def dbg2(sync, mdep=mdep, wfin=wfin):
                    sync.wait_ge(semh["dve"], mdep)
                    sync.dma_start(out=sp_dbg[:, 8320:8320 + 128], in_=xa_t[:, :]).then_inc(semh["st"], 16)
                    sync.dma_start(out=sp_dbg[:, 8448:8448 + 64], in_=wa_t[:, :]).then_inc(semh["st"], 16)
                    sync.dma_start(out=sp_dbg[:, 8512:8512 + 64], in_=wfin[:, :]).then_inc(semh["st"], 16)
                    sync.dma_start(out=sp_dbg[:, 8576:8576 + 128], in_=m2t_t[:, :]).then_inc(semh["st"], 16)
                    dinc("st", 64)
                    return cnt["st"]
                mdb2 = E("sync", dbg2)

                def dbg2w(vector, mdb2=mdb2):
                    vector.wait_ge(semh["st"], mdb2)
                    return inc(vector.nop(), "dve")
                mdep = E("vector", dbg2w)

            wtl = wnxt

            def wtld(vector, wfin=wfin, wtl=wtl, bcol=bcol, mdep=mdep):
                vector.wait_ge(semh["dve"], mdep)
                vector.tensor_scalar(out=wtl[:, :], in0=wfin[:, :], scalar1=bcol, scalar2=None,
                                     op0=ALU.mult, op1=ALU.bypass)
                return inc(vector.nop(), "dve")
            mdep = E("vector", wtld)

            def omm(tensor, wfin=wfin, wtl=wtl, scur_h=scur_h, qg_h=qg_h, ktm_h=ktm_h, hh=hh, mdep=mdep):
                tensor.wait_ge(semh["dve"], mdep)
                bs = take_bank(tensor, 2)
                tensor.matmul(psb[bs[0]][0:64, 0:128], wfin[:, :], m2t_t[:, :], start=True, stop=False)
                tensor.matmul(psb[bs[0]][0:64, 0:128], scur_h, qg_h, start=False, stop=True)
                lo2 = (hh % 2) * 64
                tensor.matmul(psb[bs[1]][lo2:lo2 + 64, 0:64], ktm_h, wtl[:, :], start=True, stop=True)
                return bs, inc(tensor.nop(), "pe")
            bs, mo = E("tensor", omm)

            def oev(vector, bs=bs, mo=mo, hh=hh, scur_h=scur_h, snew_h=snew_h):
                vector.wait_ge(semh["pe"], mo)
                vector.tensor_scalar(out=ohp_t[:, hh * 128:(hh + 1) * 128], in0=psb[bs[0]][0:64, 0:128],
                                     scalar1=hnw_cols[:, hh:hh + 1], scalar2=None, op0=ALU.mult, op1=ALU.bypass)
                vector.tensor_copy(out=o2s_t[:, hh * 128:(hh + 1) * 128], in_=psb[bs[0]][0:64, 0:128])
                v = inc(vector.nop(), "dve")
                consumed(bs[0], "dve", v)
                lo2 = (hh % 2) * 64
                jj = hh // 2
                vector.tensor_scalar(out=snew_h, in0=scur_h, scalar1=GCFULL[lo2:lo2 + 64, jj:jj + 1],
                                     scalar2=None, op0=ALU.mult, op1=ALU.bypass)
                vector.scalar_tensor_tensor(out=snew_h, in0=psb[bs[1]][lo2:lo2 + 64, 0:64],
                                            scalar=GCR[lo2:lo2 + 64, jj:jj + 1], in1=snew_h,
                                            op0=ALU.mult, op1=ALU.add)
                v = inc(vector.nop(), "dve")
                consumed(bs[1], "dve", v)
                return v
            m_prev = E("vector", oev)

        if DEBUG_SCAN and c == 0:
            def dbg3(sync, m_prev=m_prev):
                sync.wait_ge(semh["dve"], m_prev)
                sync.dma_start(out=sp_dbg[0:64, 8704:8704 + 2048], in_=o2s_t[:, :]).then_inc(semh["st"], 16)
                sync.dma_start(out=sp_dbg[:, 10752:10752 + 512], in_=Snew[:, :]).then_inc(semh["st"], 16)
                dinc("st", 32)
                return cnt["st"]
            mdb3 = E("sync", dbg3)

            def dbg3w(vector, mdb3=mdb3):
                vector.wait_ge(semh["st"], mdb3)
                return inc(vector.nop(), "dve")
            m_prev = E("vector", dbg3w)

        def onorm_a(scalar, m_prev=m_prev):
            scalar.wait_ge(semh["dve"], m_prev)
            scalar.activation(out=o2s_t[:, :], in_=o2s_t[:, :], func=AF.Square)
            return inc(scalar.nop(), "act")
        ma = E("scalar", onorm_a)

        def onorm_t(tensor, ma=ma):
            tensor.wait_ge(semh["act"], ma)
            bs = take_bank(tensor, 4)
            for q in range(4):
                tensor.matmul(psb[bs[q]][0:64, 0:512], ones128[0:64, 0:64],
                              o2s_t[:, q * 512:(q + 1) * 512], start=True, stop=True)
            return bs, inc(tensor.nop(), "pe")
        bs5, mb5 = E("tensor", onorm_t)

        def onorm_a2(scalar, bs5=bs5, mb5=mb5):
            scalar.wait_ge(semh["pe"], mb5)
            for q in range(4):
                scalar.activation(out=AR[0:64, RNR_ + q * 512:RNR_ + (q + 1) * 512],
                                  in_=psb[bs5[q]][0:64, 0:512], func=AF.Ln,
                                  bias=cpool[0:64, 0:1], scale=1.0 / 64.0)
            scalar.activation(out=AR[0:64, RNR_:RNR_ + 2048], in_=AR[0:64, RNR_:RNR_ + 2048],
                              func=AF.Exp, scale=-0.5)
            v = inc(scalar.nop(), "act")
            for b2 in bs5:
                consumed(b2, "act", v)
            return v
        mc5 = E("scalar", onorm_a2)

        def onorm_v(vector, mc5=mc5):
            vector.wait_ge(semh["act"], mc5)
            for q in range(4):
                vector.tensor_tensor(out=ohp_t[:, q * 512:(q + 1) * 512],
                                     in0=ohp_t[:, q * 512:(q + 1) * 512],
                                     in1=AR[0:64, RNR_ + q * 512:RNR_ + (q + 1) * 512], op=ALU.mult)
            return inc(vector.nop(), "dve")
        me5 = E("vector", onorm_v)

        def osp(sync, c=c, me5=me5):
            sync.wait_ge(semh["dve"], me5)
            sync.dma_start(out=sp_on[:, c * 2048:(c + 1) * 2048], in_=ohp_t[:, :]).then_inc(semh["st"], 16)
            return dinc("st")
        msp = E("sync", osp)

        def fin(vector, msp=msp):
            vector.wait_ge(semh["st"], msp)
            return inc(vector.nop(), "dve")
        sa2_done[c] = E("vector", fin)

    SA2_DONE = sa2_done[NCH - 1]

    if KSTOP < 5:
        return _finish(nc, blockctx, stack, E, semh, cnt)

    M_FFNWB = emit_bcld(bc0, 7, dep_s="dve", dep_v=SA2_DONE)

    OW1, U2W1, ON1, SG1 = 0, 16384, 17408, 25600
    H1T1, Y1T1, GSC1, Y2T1, SQ1, RS1 = 26112, 30208, 34304, 34816, 38912, 43008

    def sb1_wld(sync):
        sync.wait_ge(semh["dve"], SA2_DONE)
        for hh in range(HN):
            sync.dma_start(out=AR[0:64, OW1 + hh * 1024:OW1 + (hh + 1) * 1024],
                           in_=outw_h[hh * 64:(hh + 1) * 64, :]).then_inc(semh["ld"], 16)
            dinc("ld")
        sync.dma_start(out=AR[0:64, U2W1:U2W1 + 1024], in_=u2w_h[:, :]).then_inc(semh["ld"], 16)
        return dinc("ld")
    M_SB1W = E("sync", sb1_wld)

    sb1_done = {}
    for t5 in range(NT5):
        def ld(sync, t5=t5):
            if t5 > 0:
                sync.wait_ge(semh["dve"], sb1_done[t5 - 1])
            else:
                sync.wait_ge(semh["ld"], M_SB1W)
                sync.wait_ge(semh["dve"], M_FFNWB)
            sync.dma_start(out=AR[0:64, ON1:ON1 + 8192],
                           in_=sp_on[:, t5 * 8192:(t5 + 1) * 8192]).then_inc(semh["ld"], 16)
            sync.dma_start(out=AR[0:64, SG1:SG1 + 512],
                           in_=sp_sg[:, t5 * 512:(t5 + 1) * 512]).then_inc(semh["ld"], 16)
            sync.dma_start(out=AR[:, H1T1:H1T1 + 4096],
                           in_=sp_h1[:, t5 * 4096:(t5 + 1) * 4096]).then_inc(semh["ld"], 16)
            dinc("ld", 48)
            return cnt["ld"]
        mld = E("sync", ld)

        m_y1 = None
        for mc in range(8):
            def om(tensor, mc=mc, mld=mld):
                tensor.wait_ge(semh["ld"], mld)
                bs = take_bank(tensor, 2)
                on3 = AR[0:64, ON1:ON1 + 8192].rearrange("p (cc q) -> p cc q", q=2048)
                for hh in range(HN):
                    tensor.matmul(psb[bs[0]][:, 0:512],
                                  AR[0:64, OW1 + hh * 1024 + mc * 128:OW1 + hh * 1024 + (mc + 1) * 128],
                                  on3[:, :, hh * 128:(hh + 1) * 128],
                                  start=(hh == 0), stop=(hh == HN - 1))
                tensor.matmul(psb[bs[1]][:, 0:512], AR[0:64, U2W1 + mc * 128:U2W1 + (mc + 1) * 128],
                              AR[0:64, SG1:SG1 + 512], start=True, stop=True)
                return bs, inc(tensor.nop(), "pe")
            bs, ma = E("tensor", om)

            def ge(scalar, bs=bs, ma=ma, mc=mc):
                scalar.wait_ge(semh["pe"], ma)
                scalar.activation(out=AR[:, GSC1:GSC1 + 512], in_=psb[bs[1]][:, 0:512],
                                  func=AF.Exp, scale=-1.0, bias=u2b_cols[:, mc:mc + 1])
                v = inc(scalar.nop(), "act")
                consumed(bs[1], "act", v)
                return v
            mb = E("scalar", ge)

            def gv(vector, bs=bs, mb=mb, mc=mc):
                vector.wait_ge(semh["act"], mb)
                vector.tensor_scalar(out=AR[:, GSC1:GSC1 + 512], in0=AR[:, GSC1:GSC1 + 512],
                                     scalar1=1.0, scalar2=None, op0=ALU.add, op1=ALU.bypass)
                vector.reciprocal(out=AR[:, GSC1:GSC1 + 512], in_=AR[:, GSC1:GSC1 + 512])
                vector.tensor_tensor(out=AR[:, GSC1:GSC1 + 512], in0=psb[bs[0]][:, 0:512],
                                     in1=AR[:, GSC1:GSC1 + 512], op=ALU.mult)
                v = inc(vector.nop(), "dve")
                consumed(bs[0], "dve", v)
                y1 = AR[:, Y1T1:Y1T1 + 4096].rearrange("p (cc q) -> p cc q", q=1024)[:, :, mc * 128:(mc + 1) * 128]
                h1 = AR[:, H1T1:H1T1 + 4096].rearrange("p (cc q) -> p cc q", q=1024)[:, :, mc * 128:(mc + 1) * 128]
                g = AR[:, GSC1:GSC1 + 512].rearrange("p (cc q) -> p cc q", q=128)
                vector.tensor_tensor(out=y1, in0=h1, in1=g, op=ALU.add)
                return inc(vector.nop(), "dve")
            m_y1 = E("vector", gv)

        def ysq(scalar, m_y1=m_y1):
            scalar.wait_ge(semh["dve"], m_y1)
            scalar.activation(out=AR[:, SQ1:SQ1 + 4096], in_=AR[:, Y1T1:Y1T1 + 4096], func=AF.Square)
            return inc(scalar.nop(), "act")
        ma = E("scalar", ysq)

        def yssq(tensor, ma=ma):
            tensor.wait_ge(semh["act"], ma)
            b = take_bank(tensor, 1)[0]
            for cc in range(4):
                for j in range(DC):
                    tensor.matmul(psb[b][:, cc:cc + 1],
                                  AR[:, SQ1 + cc * 1024 + j * 128:SQ1 + cc * 1024 + (j + 1) * 128],
                                  cpool[:, 2:3], start=(j == 0), stop=(j == DC - 1))
            return b, inc(tensor.nop(), "pe")
        b6, mb6 = E("tensor", yssq)

        def yrs(scalar, b6=b6, mb6=mb6):
            scalar.wait_ge(semh["pe"], mb6)
            scalar.activation(out=AR[:, RS1:RS1 + 4], in_=psb[b6][:, 0:4], func=AF.Ln,
                              bias=cpool[:, 0:1], scale=1.0 / 1024.0)
            scalar.activation(out=AR[:, RS1:RS1 + 4], in_=AR[:, RS1:RS1 + 4], func=AF.Exp, scale=-0.5)
            v = inc(scalar.nop(), "act")
            consumed(b6, "act", v)
            return v
        mc6 = E("scalar", yrs)

        def yrst(tensor, mc6=mc6):
            tensor.wait_ge(semh["act"], mc6)
            b = take_bank(tensor, 1)[0]
            tensor.transpose(psb[b][0:4, 0:128], AR[:, RS1:RS1 + 4], ident[:, :])
            return b, inc(tensor.nop(), "pe")
        b7, md6 = E("tensor", yrst)

        def yrse(vector, b7=b7, md6=md6):
            vector.wait_ge(semh["pe"], md6)
            vector.tensor_copy(out=AR[0:4, RS1 + 4:RS1 + 132], in_=psb[b7][0:4, 0:128])
            v = inc(vector.nop(), "dve")
            consumed(b7, "dve", v)
            return v
        md6b = E("vector", yrse)

        def yrsd(sync, md6b=md6b):
            sync.wait_ge(semh["dve"], md6b)
            sync.dma_start(out=sp_row[0:1, 0:512].rearrange("o (r q) -> (o r) q", q=128),
                           in_=AR[0:4, RS1 + 4:RS1 + 132]).then_inc(semh["st"], 16)
            dinc("st")
            sync.wait_ge(semh["st"], cnt["st"])
            sync.dma_start(out=AR[0:1, RS1 + 4:RS1 + 516],
                           in_=sp_row[0:1, 0:512]).then_inc(semh["ld"], 16)
            return dinc("ld")
        md6c = E("sync", yrsd)

        def yrsr(vector, md6c=md6c):
            vector.wait_ge(semh["ld"], md6c)
            return inc(vector.nop(), "dve")
        me6 = E("vector", yrsr)

        def yrsc(vector, me6=me6):
            vector.wait_ge(semh["ld"], md6c)
            if zb_last[0]:
                vector.wait_ge(semh["pe"], zb_last[0])
            vector.tensor_copy(out=zb[0:1, 0:512], in_=AR[0:1, RS1 + 4:RS1 + 516])
            return inc(vector.nop(), "dve")
        me6b = E("vector", yrsc)

        def yrsb(tensor, me6b=me6b):
            tensor.wait_ge(semh["dve"], me6b)
            b = take_bank(tensor, 1)[0]
            for cc in range(4):
                tensor.matmul(psb[b][:, cc * 128:(cc + 1) * 128], ones128[:, :],
                              zb[:, cc * 128:(cc + 1) * 128], start=True, stop=True)
            zb_last[0] = cnt["pe"] + 1
            return b, inc(tensor.nop(), "pe")
        b8, mf6 = E("tensor", yrsb)

        def y2v(vector, b8=b8, mf6=mf6):
            vector.wait_ge(semh["pe"], mf6)
            for cc in range(4):
                for j in range(DC):
                    vector.tensor_tensor(
                        out=AR[:, Y2T1 + cc * 1024 + j * 128:Y2T1 + cc * 1024 + (j + 1) * 128],
                        in0=AR[:, Y1T1 + cc * 1024 + j * 128:Y1T1 + cc * 1024 + (j + 1) * 128],
                        in1=psb[b8][:, cc * 128:(cc + 1) * 128], op=ALU.mult)
            v = inc(vector.nop(), "dve")
            consumed(b8, "dve", v)
            for cc in range(4):
                vector.tensor_tensor(out=AR[:, Y2T1 + cc * 1024:Y2T1 + (cc + 1) * 1024],
                                     in0=AR[:, Y2T1 + cc * 1024:Y2T1 + (cc + 1) * 1024],
                                     in1=bc0[:, :], op=ALU.mult)
            return inc(vector.nop(), "dve")
        m_y2 = E("vector", y2v)

        def ysp(sync, t5=t5, m_y2=m_y2):
            sync.wait_ge(semh["dve"], m_y2)
            sync.dma_start(out=sp_y1[:, t5 * 4096:(t5 + 1) * 4096],
                           in_=AR[:, Y1T1:Y1T1 + 4096]).then_inc(semh["st"], 16)
            sync.dma_start(out=sp_y2[:, t5 * 4096:(t5 + 1) * 4096],
                           in_=AR[:, Y2T1:Y2T1 + 4096]).then_inc(semh["st"], 16)
            dinc("st", 32)
            return cnt["st"]
        msp = E("sync", ysp)

        def fin(vector, msp=msp):
            vector.wait_ge(semh["st"], msp)
            return inc(vector.nop(), "dve")
        sb1_done[t5] = E("vector", fin)

    SB1_DONE = sb1_done[NT5 - 1]

    if KSTOP < 6:
        return _finish(nc, blockctx, stack, E, semh, cnt)

    FW1, FW3, FW2 = 0, 8192, 16384
    Y2T2, SL2, SF2 = 24576, 28672, 32768
    FO2 = Y2T2

    sb2_prev = SB1_DONE
    for g in range(4):
        def wld(sync, g=g, prev=sb2_prev):
            sync.wait_ge(semh["dve"], prev)
            for j in range(DC):
                sync.dma_start(out=AR[:, FW1 + j * 1024:FW1 + (j + 1) * 1024],
                               in_=fw1_h[j * 128:(j + 1) * 128, g * 1024:(g + 1) * 1024]).then_inc(semh["ld"], 16)
                sync.dma_start(out=AR[:, FW3 + j * 1024:FW3 + (j + 1) * 1024],
                               in_=fw3_h[j * 128:(j + 1) * 128, g * 1024:(g + 1) * 1024]).then_inc(semh["ld"], 16)
                sync.dma_start(out=AR[:, FW2 + j * 1024:FW2 + (j + 1) * 1024],
                               in_=fw2_h[g * 1024 + j * 128:g * 1024 + (j + 1) * 128, :]).then_inc(semh["ld"], 16)
                dinc("ld", 48)
            return cnt["ld"]
        mwld = E("sync", wld)

        tile_done = {}
        for t5 in range(NT5):
            def ld(sync, t5=t5, mwld=mwld):
                if t5 > 0:
                    sync.wait_ge(semh["dve"], tile_done[t5 - 1])
                else:
                    sync.wait_ge(semh["ld"], mwld)
                sync.dma_start(out=AR[:, Y2T2:Y2T2 + 4096],
                               in_=sp_y2[:, t5 * 4096:(t5 + 1) * 4096]).then_inc(semh["ld"], 16)
                return dinc("ld")
            mld = E("sync", ld)

            msl = None
            m_slv_prev = [None]
            for mc in range(8):
                def g13(tensor, mc=mc, mld=mld):
                    if mc == 0:
                        tensor.wait_ge(semh["ld"], mld)
                    bs = take_bank(tensor, 2)
                    y23 = AR[:, Y2T2:Y2T2 + 4096].rearrange("p (cc q) -> p cc q", q=1024)
                    for which, wb in ((0, FW1), (1, FW3)):
                        for j in range(DC):
                            tensor.matmul(psb[bs[which]][:, 0:512],
                                          AR[:, wb + j * 1024 + mc * 128:wb + j * 1024 + (mc + 1) * 128],
                                          y23[:, :, j * 128:(j + 1) * 128],
                                          start=(j == 0), stop=(j == DC - 1))
                    return bs, inc(tensor.nop(), "pe")
                bs, ma = E("tensor", g13)

                def sev(scalar, bs=bs, ma=ma):
                    scalar.wait_ge(semh["pe"], ma)
                    if m_slv_prev[0] is not None:
                        scalar.wait_ge(semh["dve"], m_slv_prev[0])
                    scalar.activation(out=AR[:, SF2:SF2 + 512], in_=psb[bs[0]][:, 0:512], func=AF.Silu)
                    v = inc(scalar.nop(), "act")
                    consumed(bs[0], "act", v)
                    return v
                mb = E("scalar", sev)

                def slv(vector, bs=bs, mb=mb, mc=mc):
                    vector.wait_ge(semh["act"], mb)
                    vector.tensor_tensor(out=AR[:, SL2 + mc * 512:SL2 + (mc + 1) * 512],
                                         in0=AR[:, SF2:SF2 + 512], in1=psb[bs[1]][:, 0:512], op=ALU.mult)
                    v = inc(vector.nop(), "dve")
                    consumed(bs[1], "dve", v)
                    return v
                msl = E("vector", slv)
                m_slv_prev[0] = msl

            for mo in range(8):
                def w2m(tensor, mo=mo, msl=msl):
                    if mo == 0:
                        tensor.wait_ge(semh["dve"], msl)
                    b = take_bank(tensor, 1)[0]
                    for kc in range(DC):
                        tensor.matmul(psb[b][:, 0:512],
                                      AR[:, FW2 + kc * 1024 + mo * 128:FW2 + kc * 1024 + (mo + 1) * 128],
                                      AR[:, SL2 + kc * 512:SL2 + (kc + 1) * 512],
                                      start=(kc == 0), stop=(kc == DC - 1))
                    return b, inc(tensor.nop(), "pe")
                b9, mw = E("tensor", w2m)

                def w2e(vector, b9=b9, mw=mw, mo=mo):
                    vector.wait_ge(semh["pe"], mw)
                    fo3 = AR[:, FO2:FO2 + 4096].rearrange("p (cc q) -> p cc q", q=1024)
                    ps3 = psb[b9][:, 0:512].rearrange("p (cc q) -> p cc q", q=128)
                    vector.tensor_copy(out=fo3[:, :, mo * 128:(mo + 1) * 128], in_=ps3)
                    v = inc(vector.nop(), "dve")
                    consumed(b9, "dve", v)
                    return v
                mev = E("vector", w2e)

            def spo(sync, g=g, t5=t5, mev=mev):
                sync.wait_ge(semh["dve"], mev)
                sync.dma_start(out=sp_fp[g][:, t5 * 4096:(t5 + 1) * 4096],
                               in_=AR[:, FO2:FO2 + 4096]).then_inc(semh["st"], 16)
                return dinc("st")
            msp = E("sync", spo)

            def fin(vector, msp=msp):
                vector.wait_ge(semh["st"], msp)
                return inc(vector.nop(), "dve")
            tile_done[t5] = E("vector", fin)
        sb2_prev = tile_done[NT5 - 1]

    SB2_DONE = sb2_prev

    if KSTOP < 7:
        return _finish(nc, blockctx, stack, E, semh, cnt)

    M_B30 = emit_bcld(bc0, 3, dep_s="dve", dep_v=SB2_DONE)
    M_B31 = emit_bcld(bc1, 2, dep_s="dve", dep_v=M_B30)
    M_B32 = emit_bcld(bc2, 4, dep_s="dve", dep_v=M_B31)

    Y1T3, P03, P13, P23, P33 = 0, 1024, 2048, 3072, 4096
    XT3, XM3, H23, CL3, SQ3 = 5120, 6144, 7168, 8192, 8704

    def hm_init(vector):
        vector.memset(hmax[:, :], 0.0)
        return inc(vector.nop(), "dve")
    M_HM0 = E("vector", hm_init)

    sb3_done = {}
    for ti in range(NCH):
        def ld(sync, ti=ti):
            if ti > 0:
                sync.wait_ge(semh["dve"], sb3_done[ti - 1])
            else:
                sync.wait_ge(semh["dve"], M_B32)
                sync.wait_ge(semh["dve"], M_HM0)
            sync.dma_start(out=AR[:, Y1T3:Y1T3 + 1024],
                           in_=sp_y1[:, ti * 1024:(ti + 1) * 1024]).then_inc(semh["ld"], 16)
            for g in range(4):
                sync.dma_start(out=AR[:, P03 + g * 1024:P03 + (g + 1) * 1024],
                               in_=sp_fp[g][:, ti * 1024:(ti + 1) * 1024]).then_inc(semh["ld"], 16)
            sync.dma_start(out=AR[:, XT3:XT3 + 1024],
                           in_=x_h[ti * 128:(ti + 1) * 128, :]).then_inc(semh["ld"], 16)
            dinc("ld", 96)
            return cnt["ld"]
        mld = E("sync", ld)

        def addp(vector, mld=mld):
            vector.wait_ge(semh["ld"], mld)
            vector.tensor_tensor(out=AR[:, Y1T3:Y1T3 + 1024], in0=AR[:, Y1T3:Y1T3 + 1024],
                                 in1=AR[:, P03:P03 + 1024], op=ALU.add)
            vector.tensor_tensor(out=AR[:, P13:P13 + 1024], in0=AR[:, P13:P13 + 1024],
                                 in1=AR[:, P23:P23 + 1024], op=ALU.add)
            vector.tensor_tensor(out=AR[:, Y1T3:Y1T3 + 1024], in0=AR[:, Y1T3:Y1T3 + 1024],
                                 in1=AR[:, P33:P33 + 1024], op=ALU.add)
            vector.tensor_tensor(out=AR[:, Y1T3:Y1T3 + 1024], in0=AR[:, Y1T3:Y1T3 + 1024],
                                 in1=AR[:, P13:P13 + 1024], op=ALU.add)
            return inc(vector.nop(), "dve")
        ma = E("vector", addp)

        def aoutt(tensor, ma=ma):
            tensor.wait_ge(semh["dve"], ma)
            bs = take_bank(tensor, 2)
            for j in range(DC):
                tensor.transpose(psb[bs[j // 4]][:, (j % 4) * 128:(j % 4) * 128 + 128],
                                 AR[:, Y1T3 + j * 128:Y1T3 + (j + 1) * 128], ident[:, :])
            return bs, inc(tensor.nop(), "pe")
        bs, mb = E("tensor", aoutt)

        def xmid(vector, bs=bs, mb=mb, ti=ti):
            vector.wait_ge(semh["pe"], mb)
            for h in range(2):
                vector.tensor_tensor(out=AR[:, XM3 + h * 512:XM3 + (h + 1) * 512],
                                     in0=psb[bs[h]][:, 0:512],
                                     in1=bc2[:, h * 512:(h + 1) * 512], op=ALU.mult)
            v = inc(vector.nop(), "dve")
            for b2 in bs:
                consumed(b2, "dve", v)
            vector.tensor_tensor(out=AR[:, XM3:XM3 + 1024], in0=AR[:, XM3:XM3 + 1024],
                                 in1=AR[:, XT3:XT3 + 1024], op=ALU.add)
            xm = AR[:, XM3:XM3 + 1024]
            vector.tensor_reduce(out=AR[:, CL3:CL3 + 1], in_=xm, axis=AX.X, op=ALU.add)
            vector.tensor_tensor(out=AR[:, SQ3:SQ3 + 1024], in0=xm, in1=xm, op=ALU.mult)
            vector.tensor_reduce(out=AR[:, CL3 + 1:CL3 + 2], in_=AR[:, SQ3:SQ3 + 1024],
                                 axis=AX.X, op=ALU.add)
            vector.tensor_scalar(out=AR[:, CL3:CL3 + 1], in0=AR[:, CL3:CL3 + 1],
                                 scalar1=1.0 / 1024.0, scalar2=None, op0=ALU.mult, op1=ALU.bypass)
            vector.tensor_scalar(out=AR[:, CL3 + 1:CL3 + 2], in0=AR[:, CL3 + 1:CL3 + 2],
                                 scalar1=1.0 / 1024.0, scalar2=None, op0=ALU.mult, op1=ALU.bypass)
            vector.tensor_tensor(out=AR[:, CL3 + 2:CL3 + 3], in0=AR[:, CL3:CL3 + 1],
                                 in1=AR[:, CL3:CL3 + 1], op=ALU.mult)
            vector.tensor_tensor(out=AR[:, CL3 + 1:CL3 + 2], in0=AR[:, CL3 + 1:CL3 + 2],
                                 in1=AR[:, CL3 + 2:CL3 + 3], op=ALU.subtract)
            return inc(vector.nop(), "dve")
        mc2 = E("vector", xmid)

        def rst(scalar, mc2=mc2):
            scalar.wait_ge(semh["dve"], mc2)
            scalar.activation(out=AR[:, CL3 + 3:CL3 + 4], in_=AR[:, CL3 + 1:CL3 + 2],
                              func=AF.Ln, bias=cpool[:, 1:2], scale=1.0)
            scalar.activation(out=AR[:, CL3 + 3:CL3 + 4], in_=AR[:, CL3 + 3:CL3 + 4],
                              func=AF.Exp, scale=-0.5)
            return inc(scalar.nop(), "act")
        md = E("scalar", rst)

        def h2c(vector, md=md, ti=ti):
            vector.wait_ge(semh["act"], md)
            vector.tensor_copy(out=mean2[:, ti:ti + 1], in_=AR[:, CL3:CL3 + 1])
            vector.tensor_copy(out=rstd2[:, ti:ti + 1], in_=AR[:, CL3 + 3:CL3 + 4])
            vector.tensor_scalar(out=AR[:, H23:H23 + 1024], in0=AR[:, XM3:XM3 + 1024],
                                 scalar1=AR[:, CL3:CL3 + 1], scalar2=AR[:, CL3 + 3:CL3 + 4],
                                 op0=ALU.subtract, op1=ALU.mult)
            vector.tensor_tensor(out=AR[:, H23:H23 + 1024], in0=AR[:, H23:H23 + 1024],
                                 in1=bc0[:, :], op=ALU.mult)
            vector.tensor_tensor(out=AR[:, H23:H23 + 1024], in0=AR[:, H23:H23 + 1024],
                                 in1=bc1[:, :], op=ALU.add)
            vector.tensor_reduce(out=AR[:, CL3 + 4:CL3 + 5], in_=AR[:, H23:H23 + 1024],
                                 axis=AX.X, op=ALU.max, apply_absolute_value=True)
            vector.tensor_tensor(out=hmax[:, :], in0=hmax[:, :], in1=AR[:, CL3 + 4:CL3 + 5], op=ALU.max)
            return inc(vector.nop(), "dve")
        me = E("vector", h2c)

        def spx(sync, ti=ti, me=me):
            sync.wait_ge(semh["dve"], me)
            sync.dma_start(out=sp_xm[ti * 128:(ti + 1) * 128, :],
                           in_=AR[:, XM3:XM3 + 1024]).then_inc(semh["st"], 16)
            return dinc("st")
        msp = E("sync", spx)

        def fin(vector, msp=msp):
            vector.wait_ge(semh["st"], msp)
            return inc(vector.nop(), "dve")
        sb3_done[ti] = E("vector", fin)

    SB3_DONE = sb3_done[NCH - 1]

    def hmx_fold(sync):
        sync.wait_ge(semh["dve"], SB3_DONE)
        sync.dma_start(out=sp_row[0:1, 0:128].rearrange("o (r q) -> (o r) q", q=1),
                       in_=hmax[:, :]).then_inc(semh["st"], 16)
        dinc("st")
        sync.wait_ge(semh["st"], cnt["st"])
        sync.dma_start(out=AR[0:1, 0:128], in_=sp_row[0:1, 0:128]).then_inc(semh["ld"], 16)
        return dinc("ld")
    m = E("sync", hmx_fold)

    def hmx_red(vector, m=m):
        vector.wait_ge(semh["ld"], m)
        vector.tensor_reduce(out=tiny[0:1, 0:1], in_=AR[0:1, 0:128], axis=AX.X, op=ALU.max)
        return inc(vector.nop(), "dve")
    m = E("vector", hmx_red)

    def cc1(gpsimd, m=m):
        gpsimd.wait_ge(semh["dve"], m)
        gpsimd.dma_start(out=ccb[0][:, :], in_=tiny[0:1, 0:1]).then_inc(semh["gdma"], 16)
        gpsimd.wait_ge(semh["gdma"], dinc("gdma"))
        gpsimd.collective_compute("AllReduce", ALU.max, replica_groups=[list(range(8))],
                                  ins=[ccb[0][:, :]], outs=[ccb[1][:, :]]).then_inc(semh["cc"], 1)
        gpsimd.wait_ge(semh["cc"], dinc("cc", 1))
        gpsimd.dma_start(out=tiny[0:1, 1:2], in_=ccb[1][:, :]).then_inc(semh["gdma"], 16)
        return dinc("gdma")
    m_cc1 = E("gpsimd", cc1)

    def sx_v(vector, m_cc1=m_cc1):
        vector.wait_ge(semh["gdma"], m_cc1)
        vector.tensor_tensor(out=tiny[0:1, 2:3], in0=tiny[0:1, 1:2], in1=sig_clip[0:1, 0:1], op=ALU.mult)
        vector.tensor_scalar(out=tiny[0:1, 2:3], in0=tiny[0:1, 2:3], scalar1=1.0 / 127.0,
                             scalar2=None, op0=ALU.mult, op1=ALU.bypass)
        vector.reciprocal(out=tiny[0:1, 3:4], in_=tiny[0:1, 2:3])
        return inc(vector.nop(), "dve")
    m = E("vector", sx_v)

    def sx_c(vector, m=m):
        vector.wait_ge(semh["dve"], m)
        if zb_last[0]:
            vector.wait_ge(semh["pe"], zb_last[0])
        vector.tensor_copy(out=zb[0:1, 0:1], in_=tiny[0:1, 3:4])
        vector.tensor_copy(out=zb[0:1, 1:2], in_=tiny[0:1, 2:3])
        return inc(vector.nop(), "dve")
    m = E("vector", sx_c)

    def sx_b(tensor, m=m):
        tensor.wait_ge(semh["dve"], m)
        b = take_bank(tensor, 1)[0]
        tensor.matmul(psb[b][:, 0:2], ones128[:, :], zb[:, 0:2], start=True, stop=True)
        zb_last[0] = cnt["pe"] + 1
        return b, inc(tensor.nop(), "pe")
    b, m = E("tensor", sx_b)

    def sx_e(vector, b=b, m=m):
        vector.wait_ge(semh["pe"], m)
        vector.tensor_copy(out=rsx_col[:, :], in_=psb[b][:, 0:1])
        vector.tensor_scalar(out=gsc_g[:, :], in0=swg_col[:, :], scalar1=psb[b][:, 1:2],
                             scalar2=None, op0=ALU.mult, op1=ALU.bypass)
        vector.tensor_scalar(out=gsc_u[:, :], in0=swu_col[:, :], scalar1=psb[b][:, 1:2],
                             scalar2=None, op0=ALU.mult, op1=ALU.bypass)
        v = inc(vector.nop(), "dve")
        consumed(b, "dve", v)
        return v
    M_SX = E("vector", sx_e)

    if KSTOP < 8:
        return _finish(nc, blockctx, stack, E, semh, cnt)

    WGQ1 = 0
    WUQ1 = 11264
    XM1, H21, V1, IC1, F1, R1 = 22528, 23552, 24576, 25600, 26624, 27648
    HQB1 = 28672
    GS1 = 31744
    GU1 = 32256

    def sc1_wld(sync):
        sync.wait_ge(semh["dve"], M_SX)
        sync.dma_start(out=ARB[:, 2 * WGQ1:2 * WGQ1 + DC * II], in_=sp_wgq[:, :]).then_inc(semh["ld"], 16)
        sync.dma_start(out=ARB[:, 2 * WUQ1:2 * WUQ1 + DC * II], in_=sp_wuq[:, :]).then_inc(semh["ld"], 16)
        dinc("ld", 32)
        return cnt["ld"]
    M_SC1W = E("sync", sc1_wld)

    def gum_init(vector):
        vector.memset(gumax[:, :], 0.0)
        return inc(vector.nop(), "dve")
    M_GU0 = E("vector", gum_init)

    sc1_done = {}
    for t5 in range(NT5):
        mq = None
        for q4 in range(4):
            ti = t5 * 4 + q4

            def ld(sync, ti=ti, t5=t5, q4=q4):
                if t5 == 0 and q4 == 0:
                    sync.wait_ge(semh["ld"], M_SC1W)
                    sync.wait_ge(semh["dve"], M_GU0)
                elif q4 == 0:
                    sync.wait_ge(semh["dve"], sc1_done[t5 - 1])
                else:
                    sync.wait_ge(semh["dve"], cnt["dve"])
                sync.dma_start(out=AR[:, XM1:XM1 + 1024],
                               in_=sp_xm[ti * 128:(ti + 1) * 128, :]).then_inc(semh["ld"], 16)
                return dinc("ld")
            mld = E("sync", ld)

            def qv(vector, mld=mld, ti=ti):
                vector.wait_ge(semh["ld"], mld)
                vector.tensor_scalar(out=AR[:, H21:H21 + 1024], in0=AR[:, XM1:XM1 + 1024],
                                     scalar1=mean2[:, ti:ti + 1], scalar2=rstd2[:, ti:ti + 1],
                                     op0=ALU.subtract, op1=ALU.mult)
                vector.tensor_tensor(out=AR[:, H21:H21 + 1024], in0=AR[:, H21:H21 + 1024],
                                     in1=bc0[:, :], op=ALU.mult)
                vector.tensor_tensor(out=AR[:, H21:H21 + 1024], in0=AR[:, H21:H21 + 1024],
                                     in1=bc1[:, :], op=ALU.add)
                vector.tensor_scalar(out=AR[:, V1:V1 + 1024], in0=AR[:, H21:H21 + 1024],
                                     scalar1=rsx_col[:, 0:1], scalar2=None, op0=ALU.mult, op1=ALU.bypass)
                icst = AR[:, IC1:IC1 + 1024].bitcast(I32)
                vector.tensor_copy(out=icst, in_=AR[:, V1:V1 + 1024])
                vector.tensor_copy(out=AR[:, R1:R1 + 1024], in_=icst)
                vector.tensor_tensor(out=AR[:, F1:F1 + 1024], in0=AR[:, V1:V1 + 1024],
                                     in1=AR[:, R1:R1 + 1024], op=ALU.subtract)
                vector.scalar_tensor_tensor(out=AR[:, V1:V1 + 1024], in0=AR[:, F1:F1 + 1024],
                                            scalar=0.5, in1=AR[:, R1:R1 + 1024], op0=ALU.is_gt, op1=ALU.add)
                vector.scalar_tensor_tensor(out=AR[:, R1:R1 + 1024], in0=AR[:, F1:F1 + 1024],
                                            scalar=-0.5, in1=AR[:, V1:V1 + 1024], op0=ALU.is_ge, op1=ALU.add)
                vector.tensor_scalar(out=AR[:, V1:V1 + 1024], in0=AR[:, R1:R1 + 1024],
                                     scalar1=-1.0, scalar2=127.0, op0=ALU.add, op1=ALU.min)
                vector.tensor_scalar(out=AR[:, V1:V1 + 1024], in0=AR[:, V1:V1 + 1024],
                                     scalar1=-128.0, scalar2=None, op0=ALU.max, op1=ALU.bypass)
                return inc(vector.nop(), "dve")
            ma = E("vector", qv)

            def qt(tensor, ma=ma):
                tensor.wait_ge(semh["dve"], ma)
                bs = take_bank(tensor, 2)
                for j in range(DC):
                    tensor.transpose(psb[bs[j // 4]][:, (j % 4) * 128:(j % 4) * 128 + 128],
                                     AR[:, V1 + j * 128:V1 + (j + 1) * 128], ident[:, :])
                return bs, inc(tensor.nop(), "pe")
            bs, mb = E("tensor", qt)

            def qe(vector, bs=bs, mb=mb, q4=q4):
                vector.wait_ge(semh["pe"], mb)
                for h in range(2):
                    vector.tensor_copy(out=ARB[:, 2 * HQB1 + q4 * 1024 + h * 512:
                                                2 * HQB1 + q4 * 1024 + (h + 1) * 512],
                                       in_=psb[bs[h]][:, 0:512])
                v = inc(vector.nop(), "dve")
                for b2 in bs:
                    consumed(b2, "dve", v)
                return v
            mq = E("vector", qe)

        mgu = None
        m_guv_prev = [None]
        for icx in range(IC):
            def gm(tensor, icx=icx, mq=mq):
                if icx == 0:
                    tensor.wait_ge(semh["dve"], mq)
                bs = take_bank(tensor, 2)
                for which, wb in ((0, WGQ1), (1, WUQ1)):
                    for kc in range(DC):
                        lhs = ARB[:, 2 * wb + kc * II + icx * 128:2 * wb + kc * II + (icx + 1) * 128]
                        rhs = ARB[:, 2 * HQB1:2 * HQB1 + 4096].rearrange("p (cc q) -> p cc q", q=1024)[:, :, kc * 128:(kc + 1) * 128]
                        tensor.matmul(psb[bs[which]][:, 0:512], lhs, rhs,
                                      start=(kc == 0), stop=(kc == DC - 1))
                return bs, inc(tensor.nop(), "pe")
            bs, ma = E("tensor", gm)

            def gev(scalar, bs=bs, ma=ma, icx=icx):
                scalar.wait_ge(semh["pe"], ma)
                if m_guv_prev[0] is not None:
                    scalar.wait_ge(semh["dve"], m_guv_prev[0])
                scalar.activation(out=AR[:, GS1:GS1 + 512], in_=psb[bs[0]][:, 0:512], func=AF.Silu,
                                  scale=gsc_g[:, icx:icx + 1])
                v = inc(scalar.nop(), "act")
                consumed(bs[0], "act", v)
                return v
            mb = E("scalar", gev)

            def guv(vector, bs=bs, mb=mb, icx=icx, t5=t5):
                vector.wait_ge(semh["act"], mb)
                vector.tensor_scalar(out=AR[:, GU1:GU1 + 512], in0=psb[bs[1]][:, 0:512],
                                     scalar1=gsc_u[:, icx:icx + 1], scalar2=None, op0=ALU.mult, op1=ALU.bypass)
                v = inc(vector.nop(), "dve")
                consumed(bs[1], "dve", v)
                vector.tensor_tensor(out=AR[:, GU1:GU1 + 512], in0=AR[:, GU1:GU1 + 512],
                                     in1=AR[:, GS1:GS1 + 512], op=ALU.mult)
                vector.tensor_reduce(out=AR[:, F1:F1 + 1], in_=AR[:, GU1:GU1 + 512],
                                     axis=AX.X, op=ALU.max, apply_absolute_value=True)
                vector.tensor_tensor(out=gumax[:, :], in0=gumax[:, :], in1=AR[:, F1:F1 + 1], op=ALU.max)
                return inc(vector.nop(), "dve")
            mc3 = E("vector", guv)
            m_guv_prev[0] = mc3

            def gsp(sync, icx=icx, t5=t5, mc3=mc3):
                sync.wait_ge(semh["dve"], mc3)
                sync.dma_start(out=sp_gu[:, icx * T + t5 * 512:icx * T + (t5 + 1) * 512],
                               in_=AR[:, GU1:GU1 + 512]).then_inc(semh["st"], 16)
                return dinc("st")
            mgu = E("sync", gsp)

        def fin(vector, mgu=mgu):
            vector.wait_ge(semh["st"], mgu)
            return inc(vector.nop(), "dve")
        sc1_done[t5] = E("vector", fin)

    SC1_DONE = sc1_done[NT5 - 1]

    def gmx_fold(sync):
        sync.wait_ge(semh["dve"], SC1_DONE)
        sync.dma_start(out=sp_row[0:1, 128:256].rearrange("o (r q) -> (o r) q", q=1),
                       in_=gumax[:, :]).then_inc(semh["st"], 16)
        dinc("st")
        sync.wait_ge(semh["st"], cnt["st"])
        sync.dma_start(out=AR[0:1, 0:128], in_=sp_row[0:1, 128:256]).then_inc(semh["ld"], 16)
        return dinc("ld")
    m = E("sync", gmx_fold)

    def gmx_red(vector, m=m):
        vector.wait_ge(semh["ld"], m)
        vector.tensor_reduce(out=tiny[0:1, 4:5], in_=AR[0:1, 0:128], axis=AX.X, op=ALU.max)
        return inc(vector.nop(), "dve")
    m = E("vector", gmx_red)

    def cc2(gpsimd, m=m):
        gpsimd.wait_ge(semh["dve"], m)
        gpsimd.dma_start(out=ccb[2][:, :], in_=tiny[0:1, 4:5]).then_inc(semh["gdma"], 16)
        gpsimd.wait_ge(semh["gdma"], dinc("gdma"))
        gpsimd.collective_compute("AllReduce", ALU.max, replica_groups=[list(range(8))],
                                  ins=[ccb[2][:, :]], outs=[ccb[3][:, :]]).then_inc(semh["cc"], 1)
        gpsimd.wait_ge(semh["cc"], dinc("cc", 1))
        gpsimd.dma_start(out=tiny[0:1, 5:6], in_=ccb[3][:, :]).then_inc(semh["gdma"], 16)
        return dinc("gdma")
    m_cc2 = E("gpsimd", cc2)

    def sd_v(vector, m_cc2=m_cc2):
        vector.wait_ge(semh["gdma"], m_cc2)
        vector.tensor_tensor(out=tiny[0:1, 6:7], in0=tiny[0:1, 5:6], in1=sig_clip[0:1, 1:2], op=ALU.mult)
        vector.tensor_scalar(out=tiny[0:1, 6:7], in0=tiny[0:1, 6:7], scalar1=1.0 / 127.0,
                             scalar2=None, op0=ALU.mult, op1=ALU.bypass)
        vector.reciprocal(out=tiny[0:1, 7:8], in_=tiny[0:1, 6:7])
        return inc(vector.nop(), "dve")
    m = E("vector", sd_v)

    def sd_c(vector, m=m):
        vector.wait_ge(semh["dve"], m)
        if zb_last[0]:
            vector.wait_ge(semh["pe"], zb_last[0])
        vector.tensor_copy(out=zb[0:1, 0:1], in_=tiny[0:1, 7:8])
        vector.tensor_copy(out=zb[0:1, 1:2], in_=tiny[0:1, 6:7])
        return inc(vector.nop(), "dve")
    m = E("vector", sd_c)

    def sd_b(tensor, m=m):
        tensor.wait_ge(semh["dve"], m)
        b = take_bank(tensor, 1)[0]
        tensor.matmul(psb[b][:, 0:2], ones128[:, :], zb[:, 0:2], start=True, stop=True)
        return b, inc(tensor.nop(), "pe")
    b, m = E("tensor", sd_b)

    def sd_e(vector, b=b, m=m):
        vector.wait_ge(semh["pe"], m)
        vector.tensor_copy(out=rsd_col[:, :], in_=psb[b][:, 0:1])
        vector.tensor_scalar(out=dsc[:, :], in0=swd_col[:, :], scalar1=psb[b][:, 1:2],
                             scalar2=None, op0=ALU.mult, op1=ALU.bypass)
        v = inc(vector.nop(), "dve")
        consumed(b, "dve", v)
        return v
    M_SD = E("vector", sd_e)

    if KSTOP < 9:
        return _finish(nc, blockctx, stack, E, semh, cnt)

    M_GBB = emit_bcld(bc2, 5, dep_s="dve", dep_v=M_SD)

    WDQ2 = 0
    GUT2 = 11264
    GUQ2 = 22528
    V2, IC2, F2, R2 = 28160, 29184, 30208, 31232
    XMT2 = GUT2
    OTM2 = GUT2 + 4096
    OT2 = GUT2 + 8192

    def sc2_wld(sync):
        sync.wait_ge(semh["dve"], M_GBB)
        sync.dma_start(out=ARB[:, 2 * WDQ2:2 * WDQ2 + IC * D], in_=sp_wdq[:, :]).then_inc(semh["ld"], 16)
        return dinc("ld")
    M_SC2W = E("sync", sc2_wld)

    sc2_done = {}
    m_out_final = None
    for t5 in range(NT5):
        def ld(sync, t5=t5):
            if t5 > 0:
                sync.wait_ge(semh["dve"], sc2_done[t5 - 1])
            else:
                sync.wait_ge(semh["ld"], M_SC2W)
            sync.dma_start(out=AR[:, GUT2:GUT2 + IC * 512].rearrange("p (k t) -> p k t", t=512),
                           in_=sp_gu.rearrange("p (k t) -> p k t", t=T)[:, :, t5 * 512:(t5 + 1) * 512]
                           ).then_inc(semh["ld"], 16)
            return dinc("ld")
        mld = E("sync", ld)

        mq = None
        for half in range(11):
            def qv2(vector, half=half, mld=mld):
                if half == 0:
                    vector.wait_ge(semh["ld"], mld)
                base = GUT2 + half * 1024
                vector.tensor_scalar(out=AR[:, V2:V2 + 1024], in0=AR[:, base:base + 1024],
                                     scalar1=rsd_col[:, 0:1], scalar2=None, op0=ALU.mult, op1=ALU.bypass)
                icst = AR[:, IC2:IC2 + 1024].bitcast(I32)
                vector.tensor_copy(out=icst, in_=AR[:, V2:V2 + 1024])
                vector.tensor_copy(out=AR[:, R2:R2 + 1024], in_=icst)
                vector.tensor_tensor(out=AR[:, F2:F2 + 1024], in0=AR[:, V2:V2 + 1024],
                                     in1=AR[:, R2:R2 + 1024], op=ALU.subtract)
                vector.scalar_tensor_tensor(out=AR[:, V2:V2 + 1024], in0=AR[:, F2:F2 + 1024],
                                            scalar=0.5, in1=AR[:, R2:R2 + 1024], op0=ALU.is_gt, op1=ALU.add)
                vector.scalar_tensor_tensor(out=AR[:, R2:R2 + 1024], in0=AR[:, F2:F2 + 1024],
                                            scalar=-0.5, in1=AR[:, V2:V2 + 1024], op0=ALU.is_ge, op1=ALU.add)
                vector.tensor_scalar(out=AR[:, V2:V2 + 1024], in0=AR[:, R2:R2 + 1024],
                                     scalar1=-1.0, scalar2=127.0, op0=ALU.add, op1=ALU.min)
                vector.tensor_scalar(out=ARB[:, 2 * GUQ2 + half * 1024:2 * GUQ2 + (half + 1) * 1024],
                                     in0=AR[:, V2:V2 + 1024],
                                     scalar1=-128.0, scalar2=None, op0=ALU.max, op1=ALU.bypass)
                return inc(vector.nop(), "dve")
            mq = E("vector", qv2)

        def xld(sync, t5=t5, mq=mq):
            sync.wait_ge(semh["dve"], mq)
            sync.dma_start(out=AR[:, XMT2:XMT2 + 4096].rearrange("p (cc d) -> p cc d", d=1024),
                           in_=sp_xm[t5 * 512:(t5 + 1) * 512, :].rearrange("(cc r) d -> r cc d", r=128)
                           ).then_inc(semh["ld"], 16)
            return dinc("ld")
        mxl = E("sync", xld)

        for mo in range(8):
            def dm(tensor, mo=mo, mq=mq):
                if mo == 0:
                    tensor.wait_ge(semh["dve"], mq)
                b = take_bank(tensor, 1)[0]
                for kc in range(IC):
                    lhs = ARB[:, 2 * WDQ2 + kc * D + mo * 128:2 * WDQ2 + kc * D + (mo + 1) * 128]
                    rhs = ARB[:, 2 * GUQ2 + kc * 512:2 * GUQ2 + (kc + 1) * 512]
                    tensor.matmul(psb[b][:, 0:512], lhs, rhs, start=(kc == 0), stop=(kc == IC - 1))
                return b, inc(tensor.nop(), "pe")
            bd, ma = E("tensor", dm)

            def de(vector, bd=bd, ma=ma, mo=mo):
                vector.wait_ge(semh["pe"], ma)
                vector.tensor_scalar(out=AR[:, OT2:OT2 + 512], in0=psb[bd][:, 0:512],
                                     scalar1=dsc[:, mo:mo + 1], scalar2=None, op0=ALU.mult, op1=ALU.bypass)
                v = inc(vector.nop(), "dve")
                consumed(bd, "dve", v)
                return v
            mb = E("vector", de)

            def dt(tensor, mb=mb):
                tensor.wait_ge(semh["dve"], mb)
                bs = take_bank(tensor, 2)
                for cc in range(4):
                    tensor.transpose(psb[bs[cc // 2]][:, (cc % 2) * 128:(cc % 2) * 128 + 128],
                                     AR[:, OT2 + cc * 128:OT2 + (cc + 1) * 128], ident[:, :])
                return bs, inc(tensor.nop(), "pe")
            bs, mc4 = E("tensor", dt)

            def dv(vector, bs=bs, mc4=mc4, mo=mo, mxl=mxl):
                vector.wait_ge(semh["pe"], mc4)
                if mo == 0:
                    vector.wait_ge(semh["ld"], mxl)
                for cc in range(4):
                    ps = psb[bs[cc // 2]][:, (cc % 2) * 128:(cc % 2) * 128 + 128]
                    dst = AR[:, OTM2 + cc * 1024 + mo * 128:OTM2 + cc * 1024 + (mo + 1) * 128]
                    xm = AR[:, XMT2 + cc * 1024 + mo * 128:XMT2 + cc * 1024 + (mo + 1) * 128]
                    vector.tensor_tensor(out=dst, in0=ps, in1=bc2[:, mo * 128:(mo + 1) * 128], op=ALU.mult)
                    vector.tensor_tensor(out=dst, in0=dst, in1=xm, op=ALU.add)
                v = inc(vector.nop(), "dve")
                for b2 in bs:
                    consumed(b2, "dve", v)
                return v
            mfin = E("vector", dv)

        def outdma(sync, t5=t5, mfin=mfin):
            sync.wait_ge(semh["dve"], mfin)
            sync.dma_start(out=out_h[t5 * 512:(t5 + 1) * 512, :].rearrange("(cc r) d -> r cc d", r=128),
                           in_=AR[:, OTM2:OTM2 + 4096].rearrange("p (cc d) -> p cc d", d=1024)
                           ).then_inc(semh["st"], 16)
            return dinc("st")
        mo_dma = E("sync", outdma)
        m_out_final = mo_dma

        def fin(vector, mo_dma=mo_dma):
            vector.wait_ge(semh["st"], mo_dma)
            return inc(vector.nop(), "dve")
        sc2_done[t5] = E("vector", fin)

    def finale(sync):
        sync.wait_ge(semh["st"], m_out_final)
    E("sync", finale)

    blockctx.__exit__(None, None, None)
    stack.close()
    return nc


_NC_CACHE = [None]


def kernel(**inputs):
    mid = 64
    s = np.arange(128)
    csrel = -(((s[:, None] <= np.arange(128)[None, :]).astype(np.float32))
              - ((s[:, None] <= mid).astype(np.float32)))
    cscols = np.stack([-(s <= mid).astype(np.float32),
                       -np.ones(128, np.float32),
                       -(s > mid).astype(np.float32)], axis=1).astype(np.float32)
    e2b = np.float32(np.exp(2 * EXPB))
    r = s[:, None]
    cidx = np.arange(128)[None, :]
    msk_xu = ((cidx > r).astype(np.float32) * e2b).astype(np.float32)
    msk_xl = ((cidx < r).astype(np.float32) * e2b).astype(np.float32)
    msk_m2 = ((cidx >= r).astype(np.float32) * e2b).astype(np.float32)
    hsel = np.zeros((128, 128), np.float32)
    hselT = np.zeros((16, 8 * 128), np.float32)
    for j in range(8):
        hsel[0:64, j * 16 + 2 * j] = 1.0
        hsel[64:128, j * 16 + 2 * j + 1] = 1.0
        hselT[2 * j, j * 128:j * 128 + 64] = 1.0
        hselT[2 * j + 1, j * 128 + 64:j * 128 + 128] = 1.0
    ident = np.eye(128, dtype=np.float32)
    ones1 = np.ones((1, 128), np.float32)
    ones64c = np.ones((64, 1), np.float32)
    cpool = np.zeros((128, 8), np.float32)
    cpool[:, 0] = 1e-6
    cpool[:, 1] = 1e-5
    cpool[:, 2] = 1.0
    cpool[:, 3] = -EXPB
    cpool[:, 4] = EXPB

    if _NC_CACHE[0] is None:
        _NC_CACHE[0] = build()
    nc = _NC_CACHE[0]

    x = np.ascontiguousarray(inputs["L0_attn_x"], dtype=np.float32)
    c_all = np.ascontiguousarray(inputs["L0_c"], dtype=np.float32)
    clip2 = np.stack([inputs["L0_ffn_clip_x"][0], inputs["L0_ffn_clip_down"][0]]).reshape(1, 2).astype(np.float32)

    common = dict(
        clip2=clip2,
        wmod=np.ascontiguousarray(inputs["L0_wmod"], np.float32),
        norm_in_w=np.ascontiguousarray(np.tile(np.asarray(inputs["L0_attn_norm_in_w"], np.float32).reshape(1, D), (128, 1))),
        conv_w=np.ascontiguousarray(np.transpose(np.asarray(inputs["L0_attn_conv_w"], np.float32)[:, 0, :].reshape(8, 128, 4), (1, 0, 2)).reshape(128, 32)),
        conv_b=np.ascontiguousarray(np.asarray(inputs["L0_attn_conv_b"], np.float32).reshape(8, 128).T),
        q_w=np.ascontiguousarray(inputs["L0_attn_q_w"], np.float32),
        k_w=np.ascontiguousarray(inputs["L0_attn_k_w"], np.float32),
        v_w=np.ascontiguousarray(inputs["L0_attn_v_w"], np.float32),
        alpha_w=np.ascontiguousarray(inputs["L0_attn_alpha_w"], np.float32),
        alpha_b=np.ascontiguousarray(np.tile(np.asarray(inputs["L0_attn_alpha_b"], np.float32).reshape(1, D), (128, 1))),
        beta_w=np.ascontiguousarray(inputs["L0_attn_beta_w"], np.float32),
        beta_b=np.ascontiguousarray(np.tile(np.asarray(inputs["L0_attn_beta_b"], np.float32).reshape(1, 16), (128, 1))),
        head_norm_w=np.ascontiguousarray(np.asarray(inputs["L0_attn_head_norm_w"], np.float32).T),
        u1_w=np.ascontiguousarray(inputs["L0_attn_u1_w"], np.float32),
        u1_b=np.asarray(inputs["L0_attn_u1_b"], np.float32).reshape(64, 1),
        u2_w=np.ascontiguousarray(inputs["L0_attn_u2_w"], np.float32),
        u2_b=np.ascontiguousarray(np.asarray(inputs["L0_attn_u2_b"], np.float32).reshape(8, 128).T),
        ff_norm_w=np.ascontiguousarray(np.tile(np.asarray(inputs["L0_attn_ff_norm_w"], np.float32).reshape(1, D), (128, 1))),
        ff_w1=np.ascontiguousarray(inputs["L0_attn_ff_w1"], np.float32),
        ff_w3=np.ascontiguousarray(inputs["L0_attn_ff_w3"], np.float32),
        ff_w2=np.ascontiguousarray(inputs["L0_attn_ff_w2"], np.float32),
        out_w=np.ascontiguousarray(inputs["L0_attn_out_w"], np.float32),
        wgT=np.ascontiguousarray(inputs["L0_ffn_w_gate"].T, np.float32),
        wuT=np.ascontiguousarray(inputs["L0_ffn_w_up"].T, np.float32),
        wdT=np.ascontiguousarray(inputs["L0_ffn_w_down"].T, np.float32),
        ident=ident, csrel=csrel.astype(np.float32), cscols=cscols,
        msk_xu=msk_xu, msk_xl=msk_xl, msk_m2=msk_m2,
        hsel=hsel, hselT=hselT, ones1=ones1, ones64c=ones64c, cpool=cpool,
        ones128=np.ones((128, 128), np.float32),
    )
    in_maps = []
    for b in range(8):
        m = dict(common)
        m["x"] = np.ascontiguousarray(x[b])
        m["cvec"] = np.ascontiguousarray(c_all[b].reshape(8, 128).T)
        in_maps.append(m)

    res = run_bass_kernel_spmd(nc, in_maps, core_ids=list(range(8)))
    globals()["_LAST_RES"] = res
    out = np.stack([res.results[b]["out"] for b in range(8)], axis=0)
    return out
